# version 37
# baseline (speedup 1.0000x reference)
import numpy as np
import ml_dtypes
from contextlib import ExitStack

import concourse.bass as bass
import concourse.mybir as mybir
from concourse.bass_utils import run_bass_kernel_spmd
from concourse.bass_core_rust import IndirectOffsetOnAxis

F32 = mybir.dt.float32
BF16 = mybir.dt.bfloat16
AF = mybir.ActivationFunctionType
ALU = mybir.AluOpType
AX = mybir.AxisListType

T = 2048
C = 1024
NE = 8
DI = 512
CC = C // 128
TCH = T // 128
NQ = T // 512
SCALE = 0.125
EPS = 1e-5
CAP = 512


def build(debug=False):
    nc = bass.Bass("TRN2", target_bir_lowering=False)
    dbg = {}

    x_d = nc.dram_tensor("x", [T, C], F32, kind="ExternalInput").ap()
    wq_d = nc.dram_tensor("wq", [C, C], F32, kind="ExternalInput").ap()
    wk_d = nc.dram_tensor("wk", [C, C], F32, kind="ExternalInput").ap()
    wv_d = nc.dram_tensor("wv", [C, C], F32, kind="ExternalInput").ap()
    wo_d = nc.dram_tensor("wo", [C, C], F32, kind="ExternalInput").ap()
    bo_d = nc.dram_tensor("bo", [C], F32, kind="ExternalInput").ap()
    gw_d = nc.dram_tensor("gw", [NE, C], F32, kind="ExternalInput").ap()
    wg_d = nc.dram_tensor("wgate", [NE, DI, C], F32, kind="ExternalInput").ap()
    wu_d = nc.dram_tensor("wup", [NE, DI, C], F32, kind="ExternalInput").ap()
    wd_d = nc.dram_tensor("wdown", [NE, C, DI], F32, kind="ExternalInput").ap()
    idnb_d = nc.dram_tensor("idnb", [128, 128], BF16, kind="ExternalInput").ap()
    trius_d = nc.dram_tensor("trius", [128, 128], BF16, kind="ExternalInput").ap()
    tri168_d = nc.dram_tensor("tri168", [128, 128], BF16, kind="ExternalInput").ap()
    ecap_d = nc.dram_tensor("ecapc", [128, 1], F32, kind="ExternalInput").ap()
    idnf_d = nc.dram_tensor("idnf", [128, 128], F32, kind="ExternalInput").ap()
    o_d = nc.dram_tensor("o", [T, C], F32, kind="ExternalOutput").ap()

    scr_g = nc.dram_tensor("scr_g", [NE, DI, C], BF16, kind="Internal").ap()
    scr_u = nc.dram_tensor("scr_u", [NE, DI, C], BF16, kind="Internal").ap()
    scr_d = nc.dram_tensor("scr_d", [NE, C, DI], BF16, kind="Internal").ap()
    scr_gT = nc.dram_tensor("scr_gT", [NE, C, DI], BF16, kind="Internal").ap()
    scr_uT = nc.dram_tensor("scr_uT", [NE, C, DI], BF16, kind="Internal").ap()
    scr_dT = nc.dram_tensor("scr_dT", [NE, DI, C], BF16, kind="Internal").ap()
    scr_X = nc.dram_tensor("scr_X", [NE * CAP, C], BF16, kind="Internal").ap()
    scr_O = nc.dram_tensor("scr_O", [NE * CAP, C], BF16, kind="Internal").ap()

    ctx = ExitStack()
    with ctx:
        def sbuf(name, shape, dtype=F32):
            return ctx.enter_context(nc.sbuf_tensor(name, shape, dtype))

        def psum(name, shape, dtype=F32):
            return ctx.enter_context(nc.psum_tensor(name, shape, dtype))

        resid = [sbuf(f"res{c}", [128, T], F32) for c in range(CC)]
        xn = [sbuf(f"xn{c}", [128, T], BF16) for c in range(CC)]
        stage = [sbuf(f"stg{i}", [128, C], F32) for i in range(2)]
        idn_b = sbuf("idn_b", [128, 128], BF16)
        idn_f = sbuf("idn_f", [128, 128], F32)
        ones_c = sbuf("ones_c", [128, 1], BF16)
        ones_r = sbuf("ones_r", [1, 128], F32)
        e2g = sbuf("e2g", [128, 2], BF16)
        bout = sbuf("bout", [128, CC], F32)
        gwT = sbuf("gwT", [128, CC, NE], BF16)
        Lsb = sbuf("Lsb", [NE, 512], F32)
        Ltm = sbuf("Ltm", [128, TCH, NE], F32)
        Mtm = sbuf("Mtm", [128, TCH, NE], BF16)
        trius = sbuf("trius_sb", [128, 128], BF16)
        tri168 = sbuf("tri168_sb", [128, 128], BF16)
        ecap = sbuf("ecap_sb", [128, 1], F32)
        totr = sbuf("totr", [1, 128], BF16)
        totc = sbuf("totc", [128, 1], BF16)
        offc = sbuf("offc", [128, 1], F32)
        offrow = sbuf("offrow", [1, 128], F32)
        t8b = sbuf("t8b", [128, 128], F32)
        pos_f = sbuf("pos_f", [128, TCH], F32)
        pos_i = sbuf("pos_i", [128, TCH], mybir.dt.int32)
        topv = sbuf("topv", [128, TCH], F32)
        rwv = sbuf("rwv", [128, TCH], F32)
        rows = [sbuf(f"row{i}", [1, 512], F32) for i in range(3)]
        rows2 = [sbuf(f"rowb{i}", [1, 512], F32) for i in range(3)]
        tmpb = sbuf("tmpb", [128, 512], BF16)

        q4 = sbuf("q4", [128, 4], F32)
        t4m = sbuf("t4m", [4, 1], F32)
        z16 = sbuf("z16", [16, 16], F32)
        zrow = sbuf("zrow", [1, 16], F32)
        b2 = sbuf("b2", [1, 2], F32)
        epsr = sbuf("epsr", [1, 1], F32)
        biasg = sbuf("biasg", [128, 2], F32)

        S0 = psum("S0", [128, 1024], F32)
        S1 = psum("S1", [128, 1024], F32)
        Y0 = psum("Y0", [128, 512], F32)
        Y1 = psum("Y1", [128, 512], F32)
        P6 = psum("P6", [128, 512], F32)
        P7 = psum("P7", [128, 512], F32)
        S0h = [S0[:, 0:512], S0[:, 512:1024]]
        S1h = [S1[:, 0:512], S1[:, 512:1024]]
        GEN = [S0h[0], S0h[1], S1h[0], S1h[1], Y0[:], Y1[:], P6[:], P7[:]]

        class Ch:
            def __init__(self, name):
                self.sem = ctx.enter_context(nc.semaphore(name))
                self.n = 0

            def inc(self, instr, k=1):
                instr.then_inc(self.sem, k)
                self.n += k
                return self.n

        te = Ch("c_te")
        ve = Ch("c_ve")
        se = Ch("c_se")
        po = Ch("c_po")
        qs = Ch("c_qs")
        qg = Ch("c_qg")
        qa = Ch("c_qa")
        chans = {"te": te, "ve": ve, "se": se, "po": po, "qs": qs,
                 "qg": qg, "qa": qa}

        gwctx = ExitStack()
        gwst = gwctx.enter_context(nc.sbuf_tensor("gwst", [NE, C], BF16))
        block = ctx.enter_context(nc.Block())

        bar = {k: 0 for k in chans}
        waited = set()

        def new_phase():
            nonlocal bar, waited
            bar = {k: c.n for k, c in chans.items()}
            waited = set()

        def wbar(eng, me):
            if me in waited:
                return
            waited.add(me)
            for k, c in chans.items():
                if k != me and bar[k] > 0:
                    eng.wait_ge(c.sem, bar[k])

        def on_te(f):
            block.tensor(lambda e: (wbar(e, "te"), f(e)))

        def on_ve(f):
            block.vector(lambda e: (wbar(e, "ve"), f(e)))

        def on_se(f):
            block.scalar(lambda e: (wbar(e, "se"), f(e)))

        def on_po(f):
            block.gpsimd(lambda e: (wbar(e, "po"), f(e)))

        def on_qs(f):
            block.sync(lambda e: (wbar(e, "qs"), f(e)))

        def tap(name, tiles, free, dtype):
            if not debug:
                return
            part = tiles[0].shape[0]
            d = nc.dram_tensor(f"dbg_{name}", [len(tiles), part, free], dtype,
                               kind="ExternalOutput").ap()
            dbg[name] = d

            def t_(e, name=name, tiles=tiles, d=d):
                e.wait_ge(ve.sem, ve.n)
                e.wait_ge(te.sem, te.n)
                e.wait_ge(se.sem, se.n)
                e.wait_ge(po.sem, po.n)
                for c, t in enumerate(tiles):
                    e.dma_start(out=d[c], in_=t[:] if hasattr(t, "shape") is False else t).then_inc(qs.sem, 16)
            on_qs(t_)
            qs.n += 16 * len(tiles)

        def setup(e):
            e.dma_start(out=idn_b[:], in_=idnb_d).then_inc(qs.sem, 16)
            e.dma_start(out=idn_f[:], in_=idnf_d).then_inc(qs.sem, 16)
            e.dma_start(out=trius[:], in_=trius_d).then_inc(qs.sem, 16)
            e.dma_start(out=tri168[:], in_=tri168_d).then_inc(qs.sem, 16)
            e.dma_start(out=ecap[:], in_=ecap_d).then_inc(qs.sem, 16)
            e.dma_start(out=stage[1][0:NE, 0:128],
                        in_=bo_d.rearrange("(c p) -> c p", p=128)
                        ).then_inc(qs.sem, 16)
        on_qs(setup)
        qs.n += 16 * 6
        qs_setup = qs.n

        def setup_g(e):
            e.dma_start(out=gwst[:], in_=gw_d).then_inc(qg.sem, 16)
        on_po(setup_g)
        qg.n += 16
        qg_gw = qg.n

        def setup_v(e):
            e.memset(ones_c[:], 1.0)
            e.memset(ones_r[:], 1.0)
            ve.inc(e.memset(e2g[:], 0.0))
            e.wait_ge(ve.sem, ve.n)
            e.memset(e2g[0:64, 0:1], 1.0)
            e.memset(e2g[64:128, 1:2], 1.0)
            e.memset(z16[:], 0.0)
            ve.inc(e.memset(epsr[:], EPS))
        on_ve(setup_v)

        def setup_t(e):
            e.wait_ge(qs.sem, qs_setup)
            e.wait_ge(qg.sem, qg_gw)
            te.inc(e.transpose(P6[0:128, 0:NE], stage[1][0:NE, 0:128],
                               idn_f[0:NE, 0:NE]))
        on_te(setup_t)
        te_bout = te.n

        def setup_ev(e):
            e.wait_ge(te.sem, te_bout)
            ve.inc(e.tensor_copy(bout[:], P6[0:128, 0:NE]))
        on_ve(setup_ev)

        for fc in range(CC):
            def t_one(e, fc=fc):
                e.wait_ge(ve.sem, ve.n)
                te.inc(e.transpose(
                    P7[:, 0:128].bitcast(BF16)[:, 0:NE],
                    gwst[0:NE, fc * 128:(fc + 1) * 128],
                    idn_b[0:NE, 0:NE]))
            on_te(t_one)

            def v_one(e, fc=fc):
                e.wait_ge(te.sem, te.n)
                ve.inc(e.tensor_copy(gwT[:, fc, :],
                                     P7[:, 0:128].bitcast(BF16)[:, 0:NE]))
            on_ve(v_one)
        ve_gwT = ve.n
        gwctx.close()

        ve_x_evac = {}
        for tc in range(TCH):
            def ld(e, tc=tc):
                if tc == 1:
                    e.wait_ge(te.sem, te_bout)
                if tc >= 2:
                    e.wait_ge(ve.sem, ve_x_evac[tc - 2])
                e.dma_start(out=stage[tc % 2][:],
                            in_=x_d[tc * 128:(tc + 1) * 128, :]
                            ).then_inc(qs.sem, 16)
            on_qs(ld)
            qs.n += 16
            qn = qs.n

            def tr(e, tc=tc, qn=qn):
                e.wait_ge(qs.sem, qn)
                if tc == 0:
                    e.wait_ge(ve.sem, ve_gwT)
                else:
                    e.wait_ge(ve.sem, ve_x_evac[tc - 1])
                for c in range(CC):
                    i = e.transpose(GEN[c][:, 0:128],
                                    stage[tc % 2][:, c * 128:(c + 1) * 128],
                                    idn_f[:])
                te.inc(i)
            on_te(tr)
            tn = te.n

            def ev(e, tc=tc, tn=tn):
                e.wait_ge(te.sem, tn)
                for c in range(CC):
                    i = e.tensor_copy(resid[c][:, tc * 128:(tc + 1) * 128],
                                      GEN[c][:, 0:128])
                ve.inc(i)
            on_ve(ev)
            ve_x_evac[tc] = ve.n
        ve_x_done = ve.n
        tap("resid0", [r[:] for r in resid], T, F32)

        def emit_ln_mk(res_ready=None, p67_floor=None, bc_banks=None):
            te_xn_free = te.n
            ve_res = ve.n
            if res_ready is None:
                res_ready = lambda qt: ve_res
            row_ev = {}
            rowset = [(rows[0], rows[1], rows[2], rows[0]),
                      (rows2[0], rows2[1], rows2[2], rows2[0])]

            def emit_qt(qt):
                ql = qt * 512
                par = qt % 2
                Pst = P6 if par == 0 else P7
                if bc_banks is None:
                    Ba = S0h[0] if par == 0 else S1h[0]
                    Bb = S0h[1] if par == 0 else S1h[1]
                else:
                    Ba, Bb = bc_banks[par]
                r0, r1, r2, r3 = rowset[par]
                tmx = tmpb

                def sq(e, ql=ql, qt=qt):
                    e.wait_ge(ve.sem, res_ready(qt))
                    e.wait_ge(te.sem, te_xn_free)
                    for c in range(CC):
                        i = e.tensor_tensor(xn[c][:, ql:ql + 512],
                                            resid[c][:, ql:ql + 512],
                                            resid[c][:, ql:ql + 512],
                                            ALU.mult)
                    ve.inc(i)
                on_ve(sq)
                sq_n = ve.n

                def mm_ss(e, ql=ql, sq_n=sq_n, qt=qt, Pst=Pst):
                    e.wait_ge(ve.sem, sq_n)
                    if qt >= 2:
                        e.wait_ge(ve.sem, row_ev[qt - 2])
                    elif p67_floor == "res":
                        e.wait_ge(ve.sem, res_ready(qt))
                    else:
                        e.wait_ge(ve.sem,
                                  ve_res if p67_floor is None else p67_floor)
                    for c in range(CC):
                        i = e.matmul(Pst[0:1, 0:512], ones_c[:],
                                     xn[c][:, ql:ql + 512],
                                     start=(c == 0), stop=(c == CC - 1))
                    te.inc(i)
                on_te(mm_ss)
                t1n = te.n

                def ev_ss(e, t1n=t1n, Pst=Pst, r0=r0):
                    e.wait_ge(te.sem, t1n)
                    ve.inc(e.tensor_copy(r0[:], Pst[0:1, 0:512]))
                on_ve(ev_ss)

                def cast(e, ql=ql, t1n=t1n):
                    e.wait_ge(te.sem, t1n)
                    for c in range(CC):
                        i = e.tensor_copy(xn[c][:, ql:ql + 512],
                                          resid[c][:, ql:ql + 512])
                    ve.inc(i)
                on_ve(cast)
                cn = ve.n

                def mm_s(e, ql=ql, cn=cn, Pst=Pst):
                    e.wait_ge(ve.sem, cn)
                    for c in range(CC):
                        i = e.matmul(Pst[0:1, 0:512], ones_c[:],
                                     xn[c][:, ql:ql + 512],
                                     start=(c == 0), stop=(c == CC - 1))
                    te.inc(i)
                on_te(mm_s)
                t2n = te.n

                def rowmath(e, t2n=t2n, qt=qt, Pst=Pst, r0=r0, r1=r1,
                            r2=r2):
                    e.wait_ge(te.sem, t2n)
                    ve.inc(e.tensor_copy(r1[:], Pst[0:1, 0:512]))
                    row_ev[qt] = ve.n
                    e.wait_ge(ve.sem, ve.n)
                    ve.inc(e.tensor_tensor(r2[:], r1[:], r1[:], ALU.mult))
                    e.wait_ge(ve.sem, ve.n)
                    i = e.scalar_tensor_tensor(r0[:], r0[:], 1024.0,
                                               r2[:], ALU.mult,
                                               ALU.subtract)
                    ve.inc(i)
                on_ve(rowmath)
                rmn = ve.n

                def rowact(e, rmn=rmn, r0=r0, r2=r2):
                    e.wait_ge(ve.sem, rmn)
                    se.inc(e.activation(r2[:], r0[:], AF.Ln,
                                        bias=epsr[:],
                                        scale=1.0 / (1024.0 * 1024.0)))
                    e.wait_ge(se.sem, se.n)
                    i = e.activation(r2[:], r2[:], AF.Exp, scale=-0.5)
                    se.inc(i)
                on_se(rowact)
                ran = se.n

                def rowt3(e, ran=ran, r1=r1, r2=r2, r3=r3):
                    e.wait_ge(se.sem, ran)
                    ve.inc(e.tensor_tensor(r3[:], r1[:], r2[:], ALU.mult))
                on_ve(rowt3)
                t3n = ve.n

                def bcast(e, t3n=t3n, Ba=Ba, Bb=Bb, r2=r2, r3=r3):
                    e.wait_ge(ve.sem, t3n)
                    e.matmul(Ba, ones_r[:], r2[:], start=True, stop=True)
                    i = e.matmul(Bb, ones_r[:], r3[:], start=True, stop=True)
                    te.inc(i)
                on_te(bcast)
                bn = te.n

                def apply(e, ql=ql, bn=bn, Ba=Ba, Bb=Bb, tmx=tmx):
                    e.wait_ge(te.sem, bn)
                    for c in range(CC):
                        ve.inc(e.tensor_tensor(tmx[:],
                                               resid[c][:, ql:ql + 512],
                                               Ba, ALU.mult))
                        e.wait_ge(ve.sem, ve.n)
                        i = e.scalar_tensor_tensor(
                            xn[c][:, ql:ql + 512], Bb, -1.0 / 1024.0,
                            tmx[:], ALU.mult, ALU.add)
                        ve.inc(i)
                on_ve(apply)
                return ve.n
            return emit_qt

        def emit_ln(res_ready=None):
            eq = emit_ln_mk(res_ready)
            for qt in range(NQ):
                r = eq(qt)
            return r

        def make_trload(stgs):
            state = {"stg_free": {0: 0, 1: 0}, "trb": 0, "call": 0,
                     "bank_ev": {0: None, 1: None}}

            def trload(src_ap, nch, dst_slices, ld_wait_qg=None,
                       dst_wait_qs=None, on_g=False, floor_ve=None):
                sp = state["call"] % 2
                state["call"] += 1
                stg = stgs[sp]
                lq = qg if on_g else qs

                def ld(eng, src_ap=src_ap, nch=nch, sp=sp, stg=stg, lq=lq):
                    if ld_wait_qg is not None:
                        eng.wait_ge(qg.sem, ld_wait_qg)
                    eng.wait_ge(te.sem, state["stg_free"][sp])
                    eng.dma_start(
                        out=stg[:].rearrange("p c m -> p (c m)")
                        .rearrange("p (c m) -> p c m", c=nch),
                        in_=src_ap.rearrange("(c p) m -> p c m", p=128),
                    ).then_inc(lq.sem, 16)
                (on_po if on_g else on_qs)(ld)
                lq.n += 16
                ldn = lq.n
                inner = 4096 // nch
                flat = stg[:].rearrange("p c m -> p (c m)")
                floor = ve.n if floor_ve is None else floor_ve
                for dst_ap, blocks in dst_slices:
                    par = state["trb"] % 2
                    state["trb"] += 1
                    Pb = (P6 if par == 0 else P7)[:, 0:512].bitcast(BF16)

                    def trf(eng, blocks=blocks, Pb=Pb, ldn=ldn, inner=inner,
                            par=par, floor=floor, lq=lq):
                        eng.wait_ge(lq.sem, ldn)
                        bev = state["bank_ev"][par]
                        if bev:
                            eng.wait_ge(ve.sem, bev[1])
                        eng.wait_ge(ve.sem, floor)
                        for bi, (br, bc) in enumerate(blocks):
                            off = br * inner + bc * 128
                            i = eng.transpose(
                                Pb[:, 128 * bi:128 * (bi + 1)],
                                flat[:, off:off + 128], idn_b[:])
                        te.inc(i)
                    on_te(trf)
                    tn = te.n

                    def evf(eng, dst_ap=dst_ap, Pb=Pb, tn=tn,
                            nb=len(blocks), dw=dst_wait_qs, par=par):
                        eng.wait_ge(te.sem, tn)
                        if dw is not None:
                            eng.wait_ge(dw[0].sem, dw[1])
                        ve.inc(eng.tensor_copy(dst_ap[:, 0:128 * nb],
                                               Pb[:, 0:128 * nb]))
                        state["bank_ev"][par] = ("ve", ve.n)
                    on_ve(evf)
                state["stg_free"][sp] = te.n
            return trload

        def emit_routing(xn_ready):
            for qt in range(NQ):
                ql = qt * 512

                def mm_lg(e, ql=ql, xn_ready=xn_ready):
                    e.wait_ge(ve.sem, max(xn_ready, ve.n))
                    for c in range(CC):
                        i = e.matmul(S0h[0][0:NE, :], gwT[:, c, :],
                                     xn[c][:, ql:ql + 512],
                                     start=(c == 0), stop=(c == CC - 1))
                    te.inc(i)
                on_te(mm_lg)
                lgn = te.n

                def ev_lg(e, lgn=lgn):
                    e.wait_ge(te.sem, lgn)
                    ve.inc(e.tensor_copy(Lsb[:], S0h[0][0:NE, :]))
                on_ve(ev_lg)
                evn = ve.n

                def tr_l(e, qt=qt, evn=evn):
                    e.wait_ge(ve.sem, max(evn, ve.n))
                    for tci in range(4):
                        i = e.transpose(
                            S1h[0][:, 8 * tci:8 * tci + 8],
                            Lsb[0:NE, tci * 128:(tci + 1) * 128],
                            idn_f[0:NE, 0:NE])
                    te.inc(i)
                on_te(tr_l)
                trn = te.n

                def ev_l(e, qt=qt, trn=trn):
                    e.wait_ge(te.sem, trn)
                    ve.inc(e.tensor_copy(Ltm[:, 4 * qt:4 * qt + 4, :],
                                         S1h[0][:, 0:32]))
                on_ve(ev_l)
            lt_done = ve.n

            def stats(e, lt_done=lt_done):
                e.wait_ge(ve.sem, lt_done)
                ve.inc(e.reduce_max(topv[:], Ltm[:], axis=AX.X))
            on_ve(stats)
            tv_n = ve.n

            def sig(e, tv_n=tv_n):
                e.wait_ge(ve.sem, tv_n)
                se.inc(e.activation(rwv[:], topv[:], AF.Sigmoid))
            on_se(sig)
            sg_n = se.n

            def mask(e, sg_n=sg_n):
                e.wait_ge(se.sem, sg_n)
                for ei in range(NE):
                    i = e.tensor_tensor(Mtm[:, :, ei], Ltm[:, :, ei],
                                        topv[:], ALU.is_equal)
                ve.inc(i)
            on_ve(mask)
            mk_n = ve.n

            def mm_tot(e, mk_n=mk_n):
                e.wait_ge(ve.sem, max(mk_n, ve.n))
                for tc in range(TCH):
                    i = e.matmul(P7[0:1, 8 * tc:8 * tc + 8], ones_c[:],
                                 Mtm[:, tc, :], start=True, stop=True)
                te.inc(i)
            on_te(mm_tot)
            mtn = te.n

            def ev_tot(e, mtn=mtn):
                e.wait_ge(te.sem, mtn)
                ve.inc(e.tensor_copy(totr[:], P7[0:1, 0:128]))
            on_ve(ev_tot)
            etn = ve.n

            def tr_tot(e, etn=etn):
                e.wait_ge(ve.sem, etn)
                te.inc(e.transpose(P7[:, 128:256].bitcast(BF16)[:, 0:1],
                                   totr[:], idn_b[0:1, 0:1]))
            on_te(tr_tot)
            ttn = te.n

            def ev_totc(e, ttn=ttn):
                e.wait_ge(te.sem, ttn)
                ve.inc(e.tensor_copy(totc[:],
                                     P7[:, 128:256].bitcast(BF16)[:, 0:1]))
            on_ve(ev_totc)
            etc_n = ve.n

            def mm_off(e, etc_n=etc_n):
                e.wait_ge(ve.sem, etc_n)
                te.inc(e.matmul(P7[:, 256:257], tri168[:], totc[:],
                                start=True, stop=True))
            on_te(mm_off)
            mon = te.n

            def ev_off(e, mon=mon):
                e.wait_ge(te.sem, mon)
                ve.inc(e.tensor_tensor(offc[:], P7[:, 256:257], ecap[:],
                                       ALU.add))
            on_ve(ev_off)
            eon = ve.n

            def tr_off(e, eon=eon):
                e.wait_ge(ve.sem, eon)
                te.inc(e.transpose(P7[0:1, 384:512], offc[:], idn_f[:]))
            on_te(tr_off)
            ton = te.n

            def ev_offr(e, ton=ton):
                e.wait_ge(te.sem, ton)
                ve.inc(e.tensor_copy(offrow[:], P7[0:1, 384:512]))
            on_ve(ev_offr)
            orn = ve.n

            def mm_rank(e, orn=orn):
                e.wait_ge(ve.sem, max(orn, ve.n))
                for tc in range(TCH):
                    e.matmul(P6[:, 8 * tc:8 * tc + 8], trius[:],
                             Mtm[:, tc, :], start=True, stop=False)
                    i = e.matmul(P6[:, 8 * tc:8 * tc + 8], ones_r[:],
                                 offrow[:, 8 * tc:8 * tc + 8],
                                 start=False, stop=True)
                te.inc(i)
            on_te(mm_rank)
            rkn = te.n

            def ev_rank(e, rkn=rkn):
                e.wait_ge(te.sem, rkn)
                ve.inc(e.tensor_tensor(t8b[:], P6[:, 0:128],
                                       Mtm[:].rearrange("p a b -> p (a b)"),
                                       ALU.mult))
                e.wait_ge(ve.sem, ve.n)
                ve.inc(e.reduce_sum(
                    pos_f[:], t8b[:].rearrange("p (a b) -> p a b", b=8),
                    axis=AX.X))
                e.wait_ge(ve.sem, ve.n)
                ve.inc(e.tensor_copy(pos_i[:], pos_f[:]))
            on_ve(ev_rank)
            return ve.n

        def emit_ffn_sparse(wbuf, xgb, Gs, Us, Og, xst, otm, pos_ready,
                            w_ready, trload, after_expert=None,
                            after_tc=None, st_hook={}, xg_on_g=False):
            te_exp = {}
            ns = len(xst)
            ng = len(otm)
            SBH = [S0[:, 0:512], S0[:, 512:1024],
                   S1[:, 0:512], S1[:, 512:1024]]
            sc_done = {}
            sx_ev = {-1: ve.n}
            for tc in range(TCH):
                s = tc % ns
                xs = xst[s]

                Sb = SBH[(tc % ns) if ns == 4 else 2 * (tc % 2)]

                def tr_x(e, tc=tc, Sb=Sb):
                    e.wait_ge(ve.sem, sx_ev.get(tc - ns, sx_ev.get(-1, 0)))
                    if tc >= ns:
                        e.wait_ge(qg.sem, sc_done[tc - ns])
                    for c in range(CC):
                        i = e.transpose(
                            Sb.bitcast(BF16)[:, 128 * c:128 * (c + 1)],
                            xn[c][:, tc * 128:(tc + 1) * 128], idn_b[:])
                    te.inc(i)
                on_te(tr_x)
                txn = te.n

                def ev_x(e, tc=tc, txn=txn, xs=xs, Sb=Sb):
                    e.wait_ge(te.sem, txn)
                    i = e.tensor_scalar(
                        xs[:], Sb.bitcast(BF16)[:, 0:1024],
                        rwv[:, tc:tc + 1], None, ALU.mult)
                    ve.inc(i)
                on_ve(ev_x)
                exn = ve.n
                sx_ev[tc] = exn

                def sc_x(e, tc=tc, exn=exn, xs=xs, pos_ready=pos_ready):
                    e.wait_ge(ve.sem, max(exn, pos_ready))
                    e.indirect_dma_start(
                        out=scr_X,
                        out_offset=IndirectOffsetOnAxis(pos_i[:, tc:tc + 1], 0),
                        in_=xs[:], in_offset=None,
                    ).then_inc(qg.sem, 16)
                on_po(sc_x)
                qg.n += 16
                sc_done[tc] = qg.n
            scat_all = qg.n

            og_store = {}
            gu_free = 0
            for e_i in range(NE):
                s = e_i % 2
                wgT, wuT, wdT = wbuf[s]
                xg = xgb[s]
                og = Og[s]
                xgf = xg[:].rearrange("p c m -> p (c m)")

                trload(scr_X[e_i * CAP:(e_i + 1) * CAP, :], 4,
                       [(xgf[:, fc * 512:(fc + 2) * 512],
                         [(sc, f) for f in (fc, fc + 1) for sc in range(4)])
                        for fc in range(0, CC, 2)],
                       ld_wait_qg=scat_all,
                       dst_wait_qs=((qa, og_store[e_i - 2])
                                    if e_i >= 2 else None),
                       on_g=xg_on_g, floor_ve=pos_ready)
                xg_n = ve.n
                xg_se = se.n
                if after_expert is not None and e_i >= 1:
                    after_expert(e_i - 1, te_exp)

                wr = w_ready[e_i]
                if len(wr) == 4:
                    ch_w, n_w = wr[0], wr[1]
                    ch_w2, n_w2 = wr[2], wr[3]
                else:
                    ch_w, n_w = wr
                    ch_w2, n_w2 = None, 0
                SB4 = [S0h[0], S0h[1], S1h[0], S1h[1]]

                def mm_g(eng, n_w=n_w, wgT=wgT, ch_w=ch_w, xg=xg,
                         xg_n=xg_n, gf=gu_free, xg_se=xg_se,
                         ch_w2=ch_w2, n_w2=n_w2):
                    eng.wait_ge(ch_w.sem, n_w)
                    if ch_w2 is not None:
                        eng.wait_ge(ch_w2.sem, n_w2)
                    eng.wait_ge(se.sem, xg_se)
                    eng.wait_ge(ve.sem, max(xg_n, gf))
                    for ic in range(4):
                        for c in range(CC):
                            i = eng.matmul(
                                SB4[ic], wgT[:, c, ic * 128:(ic + 1) * 128],
                                xg[:, c, :],
                                start=(c == 0), stop=(c == CC - 1))
                    te.inc(i)
                on_te(mm_g)
                gn = te.n

                def ev_g(eng, gn=gn, Gs=Gs):
                    eng.wait_ge(te.sem, gn)
                    eng.tensor_copy(Gs.rearrange("p a b -> p (a b)")
                                    [:, 0:1024], S0[:])
                    i = eng.tensor_copy(Gs.rearrange("p a b -> p (a b)")
                                        [:, 1024:2048], S1[:])
                    ve.inc(i, 2)
                on_ve(ev_g)
                evg_n = ve.n

                def silu(eng, evg_n=evg_n, Gs=Gs):
                    eng.wait_ge(ve.sem, evg_n)
                    se.inc(eng.activation(
                        Gs.rearrange("p a b -> p (a b)"),
                        Gs.rearrange("p a b -> p (a b)"), AF.Silu))
                on_se(silu)
                sl_n = se.n

                def mm_u(eng, n_w=n_w, wuT=wuT, ch_w=ch_w, xg=xg,
                         evg_n=evg_n):
                    eng.wait_ge(ve.sem, evg_n)
                    for ic in range(4):
                        for c in range(CC):
                            i = eng.matmul(
                                SB4[ic], wuT[:, c, ic * 128:(ic + 1) * 128],
                                xg[:, c, :],
                                start=(c == 0), stop=(c == CC - 1))
                    te.inc(i)
                on_te(mm_u)
                un = te.n

                def ev_u(eng, un=un, Us=Us):
                    eng.wait_ge(te.sem, un)
                    eng.tensor_copy(Us.rearrange("p a b -> p (a b)")
                                    [:, 0:1024], S0[:])
                    i = eng.tensor_copy(Us.rearrange("p a b -> p (a b)")
                                        [:, 1024:2048], S1[:])
                    ve.inc(i, 2)
                on_ve(ev_u)
                gu_done = ve.n

                def hmul(eng, sl_n=sl_n, gu_done=gu_done, Gs=Gs, Us=Us):
                    eng.wait_ge(se.sem, sl_n)
                    eng.wait_ge(ve.sem, gu_done)
                    ve.inc(eng.tensor_tensor(
                        Us.rearrange("p a b -> p (a b)"),
                        Us.rearrange("p a b -> p (a b)"),
                        Gs.rearrange("p a b -> p (a b)"), ALU.mult))
                on_ve(hmul)
                h_n = ve.n

                DB = [Y0[:], Y1[:], S0h[0], S0h[1], S1h[0], S1h[1]]
                karr = [(sc4, hh) for sc4 in range(4) for hh in range(2)]

                def mm_d1(eng, h_n=h_n, wdT=wdT, Us=Us):
                    eng.wait_ge(ve.sem, h_n)
                    for k in range(6):
                        sc4, hh = karr[k]
                        for ic in range(4):
                            i = eng.matmul(
                                DB[k], Us[:, ic, sc4 * 128:(sc4 + 1) * 128],
                                wdT[:, ic, hh * 512:(hh + 1) * 512],
                                start=(ic == 0), stop=(ic == 3))
                    te.inc(i)
                on_te(mm_d1)
                d1n = te.n

                def ev_d1(eng, d1n=d1n, og=og, e_i=e_i):
                    eng.wait_ge(te.sem, d1n)
                    if e_i >= 1:
                        eng.wait_ge(qa.sem, og_store[e_i - 1])
                    for k in range(6):
                        sc4, hh = karr[k]
                        i = eng.tensor_copy(
                            og[:, sc4, hh * 512:(hh + 1) * 512], DB[k])
                    ve.inc(i)
                on_ve(ev_d1)
                ed1 = ve.n

                def mm_d2(eng, ed1=ed1, wdT=wdT, Us=Us):
                    eng.wait_ge(ve.sem, ed1)
                    for k in range(6, 8):
                        sc4, hh = karr[k]
                        for ic in range(4):
                            i = eng.matmul(
                                DB[k - 6],
                                Us[:, ic, sc4 * 128:(sc4 + 1) * 128],
                                wdT[:, ic, hh * 512:(hh + 1) * 512],
                                start=(ic == 0), stop=(ic == 3))
                    te.inc(i)
                on_te(mm_d2)
                d2n = te.n

                def ev_d2(eng, d2n=d2n, og=og):
                    eng.wait_ge(te.sem, d2n)
                    for k in range(6, 8):
                        sc4, hh = karr[k]
                        i = eng.tensor_copy(
                            og[:, sc4, hh * 512:(hh + 1) * 512], DB[k - 6])
                    ve.inc(i)
                on_ve(ev_d2)
                og_done = ve.n
                gu_free = og_done

                def st_og(eng, e_i=e_i, og=og, og_done=og_done):
                    eng.wait_ge(ve.sem, og_done)
                    eng.dma_start(
                        out=scr_O[e_i * CAP:(e_i + 1) * CAP, :]
                        .rearrange("(c p) m -> p c m", p=128),
                        in_=og[:]).then_inc(qa.sem, 16)
                on_se(st_og)
                qa.n += 16
                og_store[e_i] = qa.n

                te_exp[e_i] = te.n
            if after_expert is not None:
                after_expert(NE - 1, te_exp)
            all_og = qa.n

            ga_done = {}
            ot_used = {}
            ao_ev = {-1: ve.n}
            for tc in range(TCH):
                s = tc % ng
                ot = otm[s]

                def ga(eng, tc=tc, ot=ot, s=s):
                    eng.wait_ge(qa.sem, all_og)
                    if tc >= ng:
                        eng.wait_ge(te.sem, ot_used[tc - ng])
                    if after_tc is not None and (tc - 2) in st_hook:
                        eng.wait_ge(qs.sem, st_hook[tc - 2])
                    eng.indirect_dma_start(
                        out=ot[:], out_offset=None,
                        in_=scr_O,
                        in_offset=IndirectOffsetOnAxis(pos_i[:, tc:tc + 1],
                                                       0),
                    ).then_inc(qg.sem, 16)
                on_po(ga)
                qg.n += 16
                gn_ = qg.n

                Sb = SBH[(tc % ng) if ng == 4 else 2 * (tc % 2)]

                def tr_o(eng, tc=tc, gn_=gn_, ot=ot, Sb=Sb):
                    eng.wait_ge(qg.sem, gn_)
                    eng.wait_ge(ve.sem, ao_ev.get(tc - ng, ao_ev.get(-1, 0)))
                    for c in range(CC):
                        i = eng.transpose(
                            Sb.bitcast(BF16)
                            [:, 128 * c:128 * (c + 1)],
                            ot[:, c * 128:(c + 1) * 128], idn_b[:])
                    te.inc(i)
                on_te(tr_o)
                ot_used[tc] = te.n
                trn = te.n

                def add_o(eng, tc=tc, trn=trn, Sb=Sb):
                    eng.wait_ge(te.sem, trn)
                    for c in range(CC):
                        i = eng.scalar_tensor_tensor(
                            resid[c][:, tc * 128:(tc + 1) * 128],
                            Sb.bitcast(BF16)
                            [:, 128 * c:128 * (c + 1)],
                            0.5, resid[c][:, tc * 128:(tc + 1) * 128],
                            ALU.mult, ALU.add)
                    ve.inc(i)
                on_ve(add_o)
                ao_ev[tc] = ve.n
                if after_tc is not None:
                    r = after_tc(tc, ve.n)
                    if r is not None:
                        st_hook[tc] = r
            return ve.n, te.n

        new_phase()
        with ExitStack() as pctx:
            def psb(name, shape, dtype=BF16):
                return pctx.enter_context(nc.sbuf_tensor(name, shape, dtype))

            wbuf = []
            for s in range(2):
                wbuf.append((psb(f"wgT{s}", [128, CC, DI]),
                             psb(f"wuT{s}", [128, CC, DI]),
                             psb(f"wdT{s}", [128, 4, C])))
            Gs = xn[2][:].rearrange("p (a b) -> p a b", a=4)
            Us = xn[3][:].rearrange("p (a b) -> p a b", a=4)
            xgb = [psb("xgb0", [128, CC, 512]), psb("xgb1", [128, CC, 512])]
            Og = [x[:].rearrange("p c m -> p (c m)")
                  .rearrange("p (c m) -> p c m", c=4) for x in xgb]
            stg = [psb("stg_p0", [128, 4, 1024]), psb("stg_p1", [128, 4, 1024])]
            trload = make_trload(stg)
            xse = [psb("xse0", [128, 1024]), psb("xse1", [128, 1024])]
            xst = ([stage[i][:].bitcast(BF16)[:, 0:1024] for i in range(2)]
                   + [x[:] for x in xse])
            otm = xst[0:2]

            w_ready1 = {}
            w4_done = {}
            w2_done = {}

            def emit_w12(e_i):
                def w12(eng, e_i=e_i):
                    eng.dma_start(out=scr_g[e_i], in_=wg_d[e_i]
                                  ).then_inc(qg.sem, 16)
                    eng.dma_start(out=scr_u[e_i], in_=wu_d[e_i]
                                  ).then_inc(qg.sem, 16)
                    eng.dma_start(out=scr_d[e_i], in_=wd_d[e_i]
                                  ).then_inc(qg.sem, 16)
                on_po(w12)
                qg.n += 48
                w2_done[e_i] = qg.n


            prep_floor = [None]

            def emit_prep(e_i, te_exp):
                s = e_i % 2
                wgT, wuT, wdT = wbuf[s]
                dw = (qa, w4_done[e_i - 2]) if e_i >= 2 else None
                for mat, dst in ((scr_g[e_i], wgT), (scr_u[e_i], wuT)):
                    dflat = dst[:].rearrange("p c m -> p (c m)")
                    trload(
                        mat, 4,
                        [(dflat[:, fc * 512:(fc + 2) * 512],
                          [(ic, f) for f in (fc, fc + 1) for ic in range(4)])
                         for fc in range(0, CC, 2)],
                        ld_wait_qg=w2_done[e_i], dst_wait_qs=dw,
                        floor_ve=prep_floor[0])
                trload(
                    scr_d[e_i], 8,
                    [(wdT[:, ic, :], [(hc, ic) for hc in range(CC)])
                     for ic in range(4)],
                    ld_wait_qg=w2_done[e_i], dst_wait_qs=dw,
                    floor_ve=prep_floor[0])
                ev_now = ve.n

                def st_T(eng, e_i=e_i, wgT=wgT, wuT=wuT, wdT=wdT,
                         ev_now=ev_now):
                    eng.wait_ge(ve.sem, ev_now)
                    eng.dma_start(
                        out=scr_gT[e_i].rearrange("(c p) m -> p c m", p=128),
                        in_=wgT[:]).then_inc(qa.sem, 16)
                    eng.dma_start(
                        out=scr_uT[e_i].rearrange("(c p) m -> p c m", p=128),
                        in_=wuT[:]).then_inc(qa.sem, 16)
                    eng.dma_start(
                        out=scr_dT[e_i].rearrange("(c p) m -> p c m", p=128),
                        in_=wdT[:]).then_inc(qa.sem, 16)
                on_se(st_T)
                qa.n += 48
                w4_done[e_i] = qa.n
                w_ready1[e_i] = (ve, ve.n, se, se.n)

            for e_i in range(NE):
                emit_w12(e_i)

            xn1 = emit_ln(res_ready=lambda qt: ve_x_evac[4 * qt + 3])
            tap("xn1", [t[:] for t in xn], T, BF16)
            pr1 = emit_routing(xn1)
            prep_floor[0] = pr1
            emit_prep(0, {})
            emit_prep(1, {})
            tap("mtm1", [Mtm[:]], TCH * NE, BF16)
            tap("rwv1", [rwv[:]], TCH, F32)
            tap("pos1", [pos_f[:]], TCH, F32)

            def prep_next(e_i, te_exp):
                if e_i + 2 < NE:
                    emit_prep(e_i + 2, te_exp)

            ln2rr = {}
            xn2h = {}
            eq2 = emit_ln_mk(res_ready=lambda qt: ln2rr[qt],
                             p67_floor="res",
                             bc_banks=[(S0h[1], Y0[:]),
                                       (S1h[1], Y1[:])])

            def ln2_hook(tc, resid_ve):
                if tc % 4 == 3:
                    qt = tc // 4
                    ln2rr[qt] = resid_ve
                    xn2h[qt] = eq2(qt)
                return None

            emit_ffn_sparse(wbuf, xgb, Gs, Us, Og, xst, otm, pr1,
                            w_ready1, trload, after_expert=prep_next,
                            after_tc=ln2_hook)
            xn2 = xn2h[3]
            tap("resid1", [r[:] for r in resid], T, F32)
            tap("wgt0", [wbuf[0][0][:].rearrange("p c m -> p (c m)")], CC * DI, BF16)

        w4_all = [qa.n]
        new_phase()
        with ExitStack() as actx:
            y = [actx.enter_context(nc.sbuf_tensor(f"y{c}", [128, T], BF16))
                 for c in range(CC)]
            rza = actx.enter_context(nc.sbuf_tensor("rza", [1, 512], F32))
            rzb_r = actx.enter_context(nc.sbuf_tensor("rzb_r", [1, 512], F32))
            ysb = actx.enter_context(nc.sbuf_tensor("ysb", [128, 512], BF16))
            t4 = actx.enter_context(nc.sbuf_tensor("t4", [4, 128], F32))

            with ExitStack() as a1ctx:
                def a1(name, shape, dtype=BF16):
                    return a1ctx.enter_context(
                        nc.sbuf_tensor(name, shape, dtype))

                vt = a1("vt", [128, TCH, 8, 65])
                tap("xn2", [t[:] for t in xn], T, BF16)
                wvsb = a1("wvsb", [128, CC, 512])
                wqk = [(a1(f"wqs{s}", [128, CC, 128]),
                        a1(f"wks{s}", [128, CC, 128])) for s in range(2)]
                Qg = a1("Qg", [128, T])
                Kg = a1("Kg", [128, T])
                pts = [a1(f"pts{i}", [128, 1024]) for i in range(2)]

                def vones(eng):
                    ve.inc(eng.memset(vt[:, :, :, 64:65], 1.0))
                on_ve(vones)

                def ld_qk0(eng):
                    eng.dma_start(
                        out=wqk[0][0][:],
                        in_=wq_d[:, 0:128].rearrange("(c p) m -> p c m",
                                                     p=128),
                    ).then_inc(qg.sem, 16)
                    eng.dma_start(
                        out=wqk[0][1][:],
                        in_=wk_d[:, 0:128].rearrange("(c p) m -> p c m",
                                                     p=128),
                    ).then_inc(qg.sem, 16)
                on_po(ld_qk0)
                qg.n += 32
                qk_ld = {0: qg.n}

                def emit_vhalf(half, xn_ready):
                    def ldv(eng, half=half):
                        eng.dma_start(
                            out=wvsb[:],
                            in_=wv_d[:, half * 512:(half + 1) * 512]
                            .rearrange("(c p) m -> p c m", p=128),
                        ).then_inc(qg.sem, 16)
                    on_po(ldv)
                    qg.n += 16
                    vl = qg.n
                    vfloor = ve.n
                    vbank_ev = {}
                    for tc in range(TCH):
                        def mmv(eng, tc=tc, vl=vl, xn_ready=xn_ready,
                                bev=vbank_ev.get(tc % 4, vfloor)):
                            eng.wait_ge(qg.sem, vl)
                            eng.wait_ge(ve.sem, max(xn_ready, bev))
                            eng.wait_ge(se.sem, se.n)
                            bank = GEN[tc % 4]
                            for c in range(CC):
                                i = eng.matmul(
                                    bank, xn[c][:, tc * 128:(tc + 1) * 128],
                                    wvsb[:, c, :],
                                    start=(c == 0), stop=(c == CC - 1))
                            te.inc(i)
                        on_te(mmv)
                        vn = te.n

                        def evv(eng, tc=tc, vn=vn):
                            eng.wait_ge(te.sem, vn)
                            bank = GEN[tc % 4]
                            ve.inc(eng.tensor_copy(
                                vt[:, tc, :, 0:64],
                                bank[:, 0:512].rearrange("p (h c) -> p h c",
                                                         h=8)))
                        on_ve(evv)
                        vbank_ev[tc % 4] = ve.n
                    return ve.n

                emit_vhalf(0, xn2)

                yfm_cnt = {0: (0, 0), 1: (0, 0)}
                s_exp_cnt = {0: 0, 1: 0}
                rzb_rd = {0: 0, 1: 0}

                def emit_group(g):
                    sset = g % 2
                    wqs, wks = wqk[sset]
                    if g + 1 < 8:
                        def ldn(eng, g=g):
                            eng.wait_ge(te.sem, te.n)
                            c0 = (g + 1) * 128
                            eng.dma_start(
                                out=wqk[(g + 1) % 2][0][:],
                                in_=wq_d[:, c0:c0 + 128]
                                .rearrange("(c p) m -> p c m", p=128),
                            ).then_inc(qg.sem, 16)
                            eng.dma_start(
                                out=wqk[(g + 1) % 2][1][:],
                                in_=wk_d[:, c0:c0 + 128]
                                .rearrange("(c p) m -> p c m", p=128),
                            ).then_inc(qg.sem, 16)
                        on_po(ldn)
                        qg.n += 32
                        qk_ld[g + 1] = qg.n

                    pbe = {0: ve.n, 1: ve.n}
                    for which, wsl, dst in ((0, wqs, Qg), (1, wks, Kg)):
                        for qt in range(NQ):
                            ql = qt * 512
                            bit = (which * NQ + qt) % 2
                            bankP = P6 if bit == 0 else P7

                            def mmp(eng, wsl=wsl, ql=ql, bankP=bankP, g=g,
                                    bev=pbe[bit]):
                                eng.wait_ge(qg.sem, qk_ld[g])
                                eng.wait_ge(ve.sem, bev)
                                for c in range(CC):
                                    i = eng.matmul(bankP[:], wsl[:, c, :],
                                                   xn[c][:, ql:ql + 512],
                                                   start=(c == 0),
                                                   stop=(c == CC - 1))
                                te.inc(i)
                            on_te(mmp)
                            pn = te.n

                            def evp(eng, dst=dst, ql=ql, bankP=bankP, pn=pn):
                                eng.wait_ge(te.sem, pn)
                                ve.inc(eng.tensor_copy(dst[:, ql:ql + 512],
                                                       bankP[:]))
                            on_ve(evp)
                            pbe[bit] = ve.n
                    qk_done = ve.n

                    def sq_q(eng, qk_done=qk_done):
                        eng.wait_ge(ve.sem, qk_done)
                        eng.wait_ge(se.sem, se.n)
                        eng.wait_ge(te.sem, te.n)
                        eng.tensor_tensor(pts[0][:], Qg[:, 0:1024],
                                          Qg[:, 0:1024], ALU.mult)
                        i = eng.tensor_tensor(pts[1][:], Qg[:, 1024:2048],
                                              Qg[:, 1024:2048], ALU.mult)
                        ve.inc(i)
                    on_ve(sq_q)
                    sqq_n = ve.n

                    def mm_qss(eng, sqq_n=sqq_n):
                        eng.wait_ge(ve.sem, sqq_n)
                        for tc in range(TCH):
                            src = pts[0] if tc < 8 else pts[1]
                            i = eng.matmul(
                                P7[:, 2 * tc:2 * tc + 2],
                                src[:, (tc % 8) * 128:(tc % 8 + 1) * 128],
                                e2g[:], start=True, stop=True)
                        te.inc(i)
                    on_te(mm_qss)
                    qssn = te.n

                    def sq_k(eng, qssn=qssn):
                        eng.wait_ge(te.sem, qssn)
                        eng.tensor_tensor(pts[0][:], Kg[:, 0:1024],
                                          Kg[:, 0:1024], ALU.mult)
                        i = eng.tensor_tensor(pts[1][:], Kg[:, 1024:2048],
                                              Kg[:, 1024:2048], ALU.mult)
                        ve.inc(i)
                    on_ve(sq_k)
                    sqk_n = ve.n

                    def mm_kss(eng, sqk_n=sqk_n):
                        eng.wait_ge(ve.sem, sqk_n)
                        for tc in range(TCH):
                            src = pts[0] if tc < 8 else pts[1]
                            i = eng.matmul(
                                P7[:, 32 + 2 * tc:32 + 2 * tc + 2],
                                src[:, (tc % 8) * 128:(tc % 8 + 1) * 128],
                                e2g[:], start=True, stop=True)
                        te.inc(i)
                    on_te(mm_kss)
                    kssn = te.n

                    def ev_qss(eng, kssn=kssn):
                        eng.wait_ge(te.sem, kssn)
                        eng.reduce_max(
                            q4[:, 0:2],
                            P7[:, 0:32].rearrange("p (t j) -> p j t", j=2),
                            axis=AX.X)
                        i = eng.reduce_max(
                            q4[:, 2:4],
                            P7[:, 32:64].rearrange("p (t j) -> p j t", j=2),
                            axis=AX.X)
                        ve.inc(i)
                    on_ve(ev_qss)
                    q4n = ve.n

                    def tr_q4(eng, q4n=q4n):
                        eng.wait_ge(ve.sem, q4n)
                        te.inc(eng.transpose(P7[0:4, 64:192], q4[:],
                                             idn_f[:]))
                    on_te(tr_q4)
                    t4n = te.n

                    def red_q4(eng, t4n=t4n):
                        eng.wait_ge(te.sem, t4n)
                        e = eng
                        ve.inc(e.tensor_copy(t4[:], P7[0:4, 64:192]))
                        e.wait_ge(ve.sem, ve.n)
                        ve.inc(e.reduce_max(t4m[:], t4[:], axis=AX.X))
                        e.wait_ge(ve.sem, ve.n)
                        ve.inc(e.tensor_copy(z16[0:4, 0:1], t4m[:]))
                    on_ve(red_q4)
                    z16n = ve.n

                    def tr_z(eng, z16n=z16n):
                        eng.wait_ge(ve.sem, z16n)
                        te.inc(eng.transpose(P7[0:16, 192:208], z16[:],
                                             idn_f[0:16, 0:16]))
                    on_te(tr_z)
                    tzn = te.n

                    def mk_b2(eng, tzn=tzn):
                        eng.wait_ge(te.sem, tzn)
                        ve.inc(eng.tensor_copy(zrow[:], P7[0:1, 192:208]))
                        eng.wait_ge(ve.sem, ve.n)
                        ve.inc(eng.tensor_tensor(b2[:], zrow[:, 0:2],
                                                 zrow[:, 2:4], ALU.add))
                    on_ve(mk_b2)
                    b2n = ve.n

                    def bc_b2(eng, b2n=b2n):
                        eng.wait_ge(ve.sem, b2n)
                        te.inc(eng.matmul(P7[:, 208:210], ones_r[:], b2[:],
                                          start=True, stop=True))
                    on_te(bc_b2)
                    bcn = te.n

                    def ev_b2(eng, bcn=bcn):
                        eng.wait_ge(te.sem, bcn)
                        eng.wait_ge(se.sem, max(s_exp_cnt[0],
                                                s_exp_cnt[1]))
                        ve.inc(eng.tensor_scalar(
                            biasg[:], P7[:, 208:210],
                            -0.5 * SCALE * 1.01, None, ALU.mult))
                    on_ve(ev_b2)
                    bias_n = ve.n

                    for qt in range(NQ - 1, -1, -1):
                        nk = 4 * qt + 4
                        nb = nk // 2
                        ql = qt * 512
                        T0 = te.n
                        se0 = se.n
                        po0 = po.n
                        pred_te = lambda b, j, T0=T0: T0 + 2 * b + j + 1
                        pred_se = lambda b, j, se0=se0: se0 + 2 * b + j + 1

                        def pred_po(b, j, qt=qt, po0=po0):
                            return po0 + (b - 2 * qt) * 2 + j + 1

                        for b in range(nb):
                            for j in range(2):
                                hb = 64 * j
                                blk = (2 * g + j) % 8
                                Stile = S0 if j == 0 else S1
                                Yv = Y0 if j == 0 else Y1
                                pt = pts[j]
                                kcs = (2 * b, 2 * b + 1)

                                def mm_sc(eng, b=b, j=j, kcs=kcs,
                                          Stile=Stile, hb=hb, ql=ql,
                                          bias_n=bias_n, blk=blk, Yv=Yv,
                                          pt=pt, qt=qt):
                                    if b == 0:
                                        eng.wait_ge(se.sem, s_exp_cnt[j])
                                        eng.wait_ge(ve.sem, bias_n)
                                        rv, rs = yfm_cnt[j]
                                        if rv:
                                            eng.wait_ge(ve.sem, rv)
                                        if rs:
                                            eng.wait_ge(se.sem, rs)
                                    else:
                                        eng.wait_ge(se.sem,
                                                    pred_se(b - 1, j))
                                        if 2 * b - 1 >= 4 * qt:
                                            eng.wait_ge(po.sem,
                                                        pred_po(b - 1, j))
                                        for i_k, kc in enumerate(
                                                (2 * b - 2, 2 * b - 1)):
                                            eng.matmul(
                                                Yv[0:65, :],
                                                vt[:, kc, blk, :],
                                                pt[:, i_k * 512:
                                                   (i_k + 1) * 512],
                                                start=(kc == 0), stop=False)
                                    for i_k, kc in enumerate(kcs):
                                        i = eng.matmul(
                                            Stile[:, i_k * 512:
                                                  (i_k + 1) * 512],
                                            Kg[hb:hb + 64,
                                               kc * 128:(kc + 1) * 128],
                                            Qg[hb:hb + 64, ql:ql + 512],
                                            start=True, stop=True)
                                    te.inc(i)
                                on_te(mm_sc)
                                assert te.n == pred_te(b, j)

                        def expall(eng, nb=nb):
                            eng.wait_ge(ve.sem, bias_n)
                            for b in range(nb):
                                for j in range(2):
                                    eng.wait_ge(te.sem, pred_te(b, j))
                                    i = eng.activation(
                                        pts[j][:],
                                        S0[:] if j == 0 else S1[:],
                                        AF.Exp, bias=biasg[:, j:j + 1],
                                        scale=SCALE)
                                    se.inc(i)
                        on_se(expall)
                        assert se.n == pred_se(nb - 1, 1)
                        for j in range(2):
                            s_exp_cnt[j] = pred_se(nb - 1, j)

                        for b in (2 * qt, 2 * qt + 1):
                            for j in range(2):
                                pt = pts[j]
                                kcs = (2 * b, 2 * b + 1)

                                def dfix(eng, b=b, j=j, kcs=kcs, qt=qt,
                                         pt=pt):
                                    eng.wait_ge(se.sem, pred_se(b, j))
                                    for i_k, kc in enumerate(kcs):
                                        if kc < 4 * qt:
                                            continue
                                        off = i_k * 512
                                        vs = 128 * (kc - 4 * qt)
                                        if vs > 0:
                                            eng.memset(
                                                pt[:, off:off + vs], 0.0)
                                        i = eng.affine_select(
                                            out=pt[:, off + vs:
                                                   off + vs + 128],
                                            in_=pt[:, off + vs:
                                                   off + vs + 128],
                                            pattern=[[1, 128]],
                                            compare_op=ALU.is_ge,
                                            fill=0.0, base=0,
                                            channel_multiplier=-1)
                                    po.inc(i)
                                on_po(dfix)
                                assert po.n == pred_po(b, j), (po.n, b, j)

                        avn = {}
                        for j in range(2):
                            blk = (2 * g + j) % 8
                            Yv = Y0 if j == 0 else Y1
                            pt = pts[j]

                            def mm_av_last(eng, j=j, nb=nb, nk=nk, blk=blk,
                                           Yv=Yv, pt=pt):
                                eng.wait_ge(se.sem, pred_se(nb - 1, j))
                                eng.wait_ge(po.sem, pred_po(nb - 1, j))
                                for i_k, kc in enumerate((nk - 2, nk - 1)):
                                    i = eng.matmul(
                                        Yv[0:65, :], vt[:, kc, blk, :],
                                        pt[:, i_k * 512:(i_k + 1) * 512],
                                        start=(kc == 0), stop=(kc == nk - 1))
                                te.inc(i)
                            on_te(mm_av_last)
                            avn[j] = te.n

                        def nrm_ln(eng, avn=avn):
                            eng.wait_ge(te.sem, max(avn[0], avn[1]))
                            eng.activation(rza[:], Y0[64:65, :], AF.Ln)
                            se.inc(eng.activation(rzb_r[:], Y1[64:65, :],
                                                  AF.Ln))
                            eng.wait_ge(se.sem, se.n)
                            eng.activation(rza[:], rza[:], AF.Exp,
                                           scale=-1.0)
                            i = eng.activation(rzb_r[:], rzb_r[:], AF.Exp,
                                               scale=-1.0)
                            se.inc(i)
                        on_se(nrm_ln)
                        rz_se = se.n

                        for j in range(2):
                            hb = 64 * j
                            Yv = Y0 if j == 0 else Y1
                            Pz = P6 if j == 0 else P7
                            rzj = rza if j == 0 else rzb_r

                            def nrm_bc(eng, rz_se=rz_se, Pz=Pz, rzj=rzj):
                                eng.wait_ge(se.sem, rz_se)
                                te.inc(eng.matmul(Pz[0:64, :],
                                                  ones_r[:, 0:64], rzj[:],
                                                  start=True, stop=True))
                            on_te(nrm_bc)
                            nbn = te.n

                            def nrm_ysb(eng, j=j, Yv=Yv, avn=avn, hb=hb):
                                eng.wait_ge(te.sem, avn[j])
                                ve.inc(eng.tensor_copy(ysb[hb:hb + 64, :],
                                                       Yv[0:64, :]))
                            on_ve(nrm_ysb)
                            ysn = ve.n

                            def nrm_yfm(eng, g=g, hb=hb, ql=ql, nbn=nbn,
                                        ysn=ysn, Pz=Pz):
                                eng.wait_ge(te.sem, nbn)
                                eng.wait_ge(ve.sem, ysn)
                                i = eng.tensor_tensor(
                                    y[g][hb:hb + 64, ql:ql + 512],
                                    ysb[hb:hb + 64, :], Pz[0:64, :],
                                    ALU.mult)
                                ve.inc(i)
                            on_ve(nrm_yfm)
                            yfm_cnt[j] = (ve.n, 0)

                for g in range(4):
                    emit_group(g)
                emit_vhalf(1, xn2)
                for g in range(4, 8):
                    emit_group(g)
                tap("qg7", [Qg[:], Kg[:]], T, BF16)
                tap("biasg7", [biasg[:]], 2, F32)
                tap("y", [t[:] for t in y], T, BF16)
                tap("vt15", [vt[:, 15, :, :].rearrange("p a b -> p (a b)")], 8 * 65, BF16)

            with ExitStack() as a2ctx:
                wout = a2ctx.enter_context(
                    nc.sbuf_tensor("wout", [128, CC, C], BF16))

                def ld_wo(eng):
                    eng.wait_ge(te.sem, te.n)
                    eng.dma_start(
                        out=wout[:],
                        in_=wo_d.rearrange("(c p) m -> p c m", p=128),
                    ).then_inc(qg.sem, 16)
                on_po(ld_wo)
                qg.n += 16
                wo_n = qg.n

                obe = {0: ve.n, 1: ve.n}
                ose = se.n
                oq_ev = {}
                eq3 = emit_ln_mk(res_ready=lambda qt: oq_ev[qt],
                                 p67_floor=ve.n)
                for qt in range(NQ):
                    for fo in range(CC):
                        ql = qt * 512
                        obit = (qt * CC + fo) % 2
                        bankP = Y0 if obit == 0 else Y1

                        def mm_o(eng, fo=fo, ql=ql, bankP=bankP, wo_n=wo_n,
                                 bev=obe[obit], ose=ose):
                            eng.wait_ge(qg.sem, wo_n)
                            eng.wait_ge(ve.sem, bev)
                            eng.wait_ge(se.sem, ose)
                            for c in range(CC):
                                i = eng.matmul(
                                    bankP[:],
                                    wout[:, c, fo * 128:(fo + 1) * 128],
                                    y[c][:, ql:ql + 512],
                                    start=(c == 0), stop=(c == CC - 1))
                            te.inc(i)
                        on_te(mm_o)
                        on_ = te.n

                        def ev_o(eng, fo=fo, ql=ql, bankP=bankP, on_=on_):
                            eng.wait_ge(te.sem, on_)
                            i = eng.scalar_tensor_tensor(
                                resid[fo][:, ql:ql + 512], bankP[:],
                                bout[:, fo:fo + 1],
                                resid[fo][:, ql:ql + 512], ALU.add, ALU.add)
                            ve.inc(i)
                        on_ve(ev_o)
                        obe[obit] = ve.n
                    oq_ev[qt] = ve.n
                    xn3 = eq3(qt)

        tap("resid2", [r[:] for r in resid], T, F32)
        new_phase()
        with ExitStack() as f2ctx:
            def fsb(name, shape, dtype=BF16):
                return f2ctx.enter_context(nc.sbuf_tensor(name, shape, dtype))

            wbuf2 = []
            for s in range(2):
                wbuf2.append((fsb(f"w2gT{s}", [128, CC, DI]),
                              fsb(f"w2uT{s}", [128, CC, DI]),
                              fsb(f"w2dT{s}", [128, 4, C])))
            Gs2 = xn[2][:].rearrange("p (a b) -> p a b", a=4)
            Us2 = xn[3][:].rearrange("p (a b) -> p a b", a=4)
            xgb2 = [fsb("xgb20", [128, CC, 512]),
                    fsb("xgb21", [128, CC, 512])]
            Og2 = [x[:].rearrange("p c m -> p (c m)")
                   .rearrange("p (c m) -> p c m", c=4) for x in xgb2]
            stg2 = [fsb("stg_f20", [128, 4, 1024]),
                    fsb("stg_f21", [128, 4, 1024])]
            trload2 = make_trload(stg2)
            xse2 = [fsb("xse20", [128, 1024]), fsb("xse21", [128, 1024])]
            xst2 = ([stage[i][:].bitcast(BF16)[:, 0:1024] for i in range(2)]
                    + [x[:] for x in xse2])
            otm2 = xst2[0:2]

            pr2 = emit_routing(xn3)

            w_ready2 = {}

            def emit_load2(e_i, te_exp):
                s = e_i % 2

                def ld(eng, e_i=e_i, s=s, te_exp=te_exp):
                    eng.wait_ge(qa.sem, w4_all[0])
                    if e_i >= 2:
                        eng.wait_ge(te.sem, te_exp[e_i - 2])
                    eng.dma_start(
                        out=wbuf2[s][0][:],
                        in_=scr_gT[e_i].rearrange("(c p) m -> p c m", p=128),
                    ).then_inc(qs.sem, 16)
                    eng.dma_start(
                        out=wbuf2[s][1][:],
                        in_=scr_uT[e_i].rearrange("(c p) m -> p c m", p=128),
                    ).then_inc(qs.sem, 16)
                    eng.dma_start(
                        out=wbuf2[s][2][:],
                        in_=scr_dT[e_i].rearrange("(c p) m -> p c m", p=128),
                    ).then_inc(qs.sem, 16)
                on_qs(ld)
                qs.n += 48
                w_ready2[e_i] = (qs, qs.n)

            emit_load2(0, {})
            emit_load2(1, {})

            def load_next(e_i, te_exp):
                if e_i + 2 < NE:
                    emit_load2(e_i + 2, te_exp)

            st_n = {}

            NB4 = [Y0[:], Y1[:], P6[:], P7[:]]

            def out_tc(tc, resid_ve):
                for w in range(2):
                    def tro(e, tc=tc, resid_ve=resid_ve, w=w):
                        e.wait_ge(ve.sem, resid_ve if w == 0 else ve.n)
                        if w == 0 and tc >= 2:
                            e.wait_ge(qs.sem, st_n[tc - 2])
                        for ci in range(4):
                            c = 4 * w + ci
                            i = e.transpose(
                                NB4[ci][:, 0:128],
                                resid[c][:, tc * 128:(tc + 1) * 128],
                                idn_f[:])
                        te.inc(i)
                    on_te(tro)
                    tn = te.n

                    def evo(e, tn=tn, tc=tc, w=w):
                        e.wait_ge(te.sem, tn)
                        for ci in range(4):
                            c = 4 * w + ci
                            i = e.tensor_copy(
                                stage[tc % 2][:, c * 128:(c + 1) * 128],
                                NB4[ci][:, 0:128])
                        ve.inc(i)
                    on_ve(evo)
                en = ve.n

                def sto(e, tc=tc, en=en):
                    e.wait_ge(ve.sem, en)
                    e.dma_start(out=o_d[tc * 128:(tc + 1) * 128, :],
                                in_=stage[tc % 2][:]).then_inc(qs.sem, 16)
                on_qs(sto)
                qs.n += 16
                st_n[tc] = qs.n
                return qs.n

            emit_ffn_sparse(wbuf2, xgb2, Gs2, Us2, Og2, xst2, otm2, pr2,
                            w_ready2, trload2, after_expert=load_next,
                            after_tc=out_tc)

        def fin(e):
            e.wait_ge(qs.sem, qs.n)
        on_qs(fin)

    return nc


_CACHE = {}


def _tri168():
    t = np.zeros((128, 128), np.float32)
    for p in range(128):
        for q in range(128):
            if (p // 8 < q // 8) and (p % 8 == q % 8):
                t[p, q] = 1.0
    return t.astype(ml_dtypes.bfloat16)


def kernel(L0_attn_x, L0_attn_W_query, L0_attn_W_key, L0_attn_W_value,
           L0_attn_W_out, L0_attn_b_out, L0_ffn_gate_w, L0_ffn_w_gate,
           L0_ffn_w_up, L0_ffn_w_down):
    if "nc" not in _CACHE:
        _CACHE["nc"] = build()
    nc = _CACHE["nc"]

    x = np.ascontiguousarray(np.asarray(L0_attn_x, dtype=np.float32))
    com = dict(
        wq=np.ascontiguousarray(np.asarray(L0_attn_W_query, np.float32)),
        wk=np.ascontiguousarray(np.asarray(L0_attn_W_key, np.float32)),
        wv=np.ascontiguousarray(np.asarray(L0_attn_W_value, np.float32)),
        wo=np.ascontiguousarray(np.asarray(L0_attn_W_out, np.float32)),
        bo=np.ascontiguousarray(np.asarray(L0_attn_b_out, np.float32)),
        gw=np.ascontiguousarray(np.asarray(L0_ffn_gate_w, np.float32)),
        wgate=np.ascontiguousarray(np.asarray(L0_ffn_w_gate, np.float32)),
        wup=np.ascontiguousarray(np.asarray(L0_ffn_w_up, np.float32)),
        wdown=np.ascontiguousarray(np.asarray(L0_ffn_w_down, np.float32)),
        idnb=np.eye(128, dtype=ml_dtypes.bfloat16),
        idnf=np.eye(128, dtype=np.float32),
        trius=np.triu(np.ones((128, 128)), 1).astype(ml_dtypes.bfloat16),
        tri168=_tri168(),
        ecapc=(np.arange(128) % 8 * CAP).astype(np.float32).reshape(128, 1),
    )
    in_maps = [dict(com, x=np.ascontiguousarray(x[b])) for b in range(8)]
    res = run_bass_kernel_spmd(nc, in_maps, core_ids=list(range(8)))
    return np.stack([np.asarray(r["o"], dtype=np.float32)
                     for r in res.results], axis=0)


# version 38
# speedup vs baseline: 1.0072x; 1.0072x over previous
import numpy as np
import ml_dtypes
from contextlib import ExitStack

import concourse.bass as bass
import concourse.mybir as mybir
from concourse.bass_utils import run_bass_kernel_spmd
from concourse.bass_core_rust import IndirectOffsetOnAxis

F32 = mybir.dt.float32
BF16 = mybir.dt.bfloat16
AF = mybir.ActivationFunctionType
ALU = mybir.AluOpType
AX = mybir.AxisListType

T = 2048
C = 1024
NE = 8
DI = 512
CC = C // 128
TCH = T // 128
NQ = T // 512
SCALE = 0.125
EPS = 1e-5
CAP = 512


def build(debug=False):
    nc = bass.Bass("TRN2", target_bir_lowering=False)
    dbg = {}

    x_d = nc.dram_tensor("x", [T, C], F32, kind="ExternalInput").ap()
    wq_d = nc.dram_tensor("wq", [C, C], F32, kind="ExternalInput").ap()
    wk_d = nc.dram_tensor("wk", [C, C], F32, kind="ExternalInput").ap()
    wv_d = nc.dram_tensor("wv", [C, C], F32, kind="ExternalInput").ap()
    wo_d = nc.dram_tensor("wo", [C, C], F32, kind="ExternalInput").ap()
    bo_d = nc.dram_tensor("bo", [C], F32, kind="ExternalInput").ap()
    gw_d = nc.dram_tensor("gw", [NE, C], F32, kind="ExternalInput").ap()
    wg_d = nc.dram_tensor("wgate", [NE, DI, C], F32, kind="ExternalInput").ap()
    wu_d = nc.dram_tensor("wup", [NE, DI, C], F32, kind="ExternalInput").ap()
    wd_d = nc.dram_tensor("wdown", [NE, C, DI], F32, kind="ExternalInput").ap()
    idnb_d = nc.dram_tensor("idnb", [128, 128], BF16, kind="ExternalInput").ap()
    trius_d = nc.dram_tensor("trius", [128, 128], BF16, kind="ExternalInput").ap()
    tri168_d = nc.dram_tensor("tri168", [128, 128], BF16, kind="ExternalInput").ap()
    ecap_d = nc.dram_tensor("ecapc", [128, 1], F32, kind="ExternalInput").ap()
    idnf_d = nc.dram_tensor("idnf", [128, 128], F32, kind="ExternalInput").ap()
    o_d = nc.dram_tensor("o", [T, C], F32, kind="ExternalOutput").ap()

    scr_g = nc.dram_tensor("scr_g", [NE, DI, C], BF16, kind="Internal").ap()
    scr_u = nc.dram_tensor("scr_u", [NE, DI, C], BF16, kind="Internal").ap()
    scr_d = nc.dram_tensor("scr_d", [NE, C, DI], BF16, kind="Internal").ap()
    scr_gT = nc.dram_tensor("scr_gT", [NE, C, DI], BF16, kind="Internal").ap()
    scr_uT = nc.dram_tensor("scr_uT", [NE, C, DI], BF16, kind="Internal").ap()
    scr_dT = nc.dram_tensor("scr_dT", [NE, DI, C], BF16, kind="Internal").ap()
    scr_X = nc.dram_tensor("scr_X", [NE * CAP, C], BF16, kind="Internal").ap()
    scr_O = nc.dram_tensor("scr_O", [NE * CAP, C], BF16, kind="Internal").ap()

    ctx = ExitStack()
    with ctx:
        def sbuf(name, shape, dtype=F32):
            return ctx.enter_context(nc.sbuf_tensor(name, shape, dtype))

        def psum(name, shape, dtype=F32):
            return ctx.enter_context(nc.psum_tensor(name, shape, dtype))

        resid = [sbuf(f"res{c}", [128, T], F32) for c in range(CC)]
        xn = [sbuf(f"xn{c}", [128, T], BF16) for c in range(CC)]
        stage = [sbuf(f"stg{i}", [128, C], F32) for i in range(2)]
        idn_b = sbuf("idn_b", [128, 128], BF16)
        idn_f = sbuf("idn_f", [128, 128], F32)
        ones_c = sbuf("ones_c", [128, 1], BF16)
        ones_r = sbuf("ones_r", [1, 128], F32)
        e2g = sbuf("e2g", [128, 2], BF16)
        bout = sbuf("bout", [128, CC], F32)
        gwT = sbuf("gwT", [128, CC, NE], BF16)
        Lsb = sbuf("Lsb", [NE, 512], F32)
        Ltm = sbuf("Ltm", [128, TCH, NE], F32)
        Mtm = sbuf("Mtm", [128, TCH, NE], BF16)
        trius = sbuf("trius_sb", [128, 128], BF16)
        tri168 = sbuf("tri168_sb", [128, 128], BF16)
        ecap = sbuf("ecap_sb", [128, 1], F32)
        totr = sbuf("totr", [1, 128], BF16)
        totc = sbuf("totc", [128, 1], BF16)
        offc = sbuf("offc", [128, 1], F32)
        offrow = sbuf("offrow", [1, 128], F32)
        t8b = sbuf("t8b", [128, 128], F32)
        pos_f = sbuf("pos_f", [128, TCH], F32)
        pos_i = sbuf("pos_i", [128, TCH], mybir.dt.int32)
        topv = sbuf("topv", [128, TCH], F32)
        rwv = sbuf("rwv", [128, TCH], F32)
        rows = [sbuf(f"row{i}", [1, 512], F32) for i in range(3)]
        rows2 = [sbuf(f"rowb{i}", [1, 512], F32) for i in range(3)]
        tmpb = sbuf("tmpb", [128, 512], BF16)

        q4 = sbuf("q4", [128, 4], F32)
        t4m = sbuf("t4m", [4, 1], F32)
        z16 = sbuf("z16", [16, 16], F32)
        zrow = sbuf("zrow", [1, 16], F32)
        b2 = sbuf("b2", [1, 2], F32)
        epsr = sbuf("epsr", [1, 1], F32)
        biasg = sbuf("biasg", [128, 2], F32)

        S0 = psum("S0", [128, 1024], F32)
        S1 = psum("S1", [128, 1024], F32)
        Y0 = psum("Y0", [128, 512], F32)
        Y1 = psum("Y1", [128, 512], F32)
        P6 = psum("P6", [128, 512], F32)
        P7 = psum("P7", [128, 512], F32)
        S0h = [S0[:, 0:512], S0[:, 512:1024]]
        S1h = [S1[:, 0:512], S1[:, 512:1024]]
        GEN = [S0h[0], S0h[1], S1h[0], S1h[1], Y0[:], Y1[:], P6[:], P7[:]]

        class Ch:
            def __init__(self, name):
                self.sem = ctx.enter_context(nc.semaphore(name))
                self.n = 0

            def inc(self, instr, k=1):
                instr.then_inc(self.sem, k)
                self.n += k
                return self.n

        te = Ch("c_te")
        ve = Ch("c_ve")
        se = Ch("c_se")
        po = Ch("c_po")
        qs = Ch("c_qs")
        qg = Ch("c_qg")
        qa = Ch("c_qa")
        chans = {"te": te, "ve": ve, "se": se, "po": po, "qs": qs,
                 "qg": qg, "qa": qa}

        gwctx = ExitStack()
        gwst = gwctx.enter_context(nc.sbuf_tensor("gwst", [NE, C], BF16))
        block = ctx.enter_context(nc.Block())

        bar = {k: 0 for k in chans}
        waited = set()

        def new_phase():
            nonlocal bar, waited
            bar = {k: c.n for k, c in chans.items()}
            waited = set()

        def wbar(eng, me):
            if me in waited:
                return
            waited.add(me)
            for k, c in chans.items():
                if k != me and bar[k] > 0:
                    eng.wait_ge(c.sem, bar[k])

        def on_te(f):
            block.tensor(lambda e: (wbar(e, "te"), f(e)))

        def on_ve(f):
            block.vector(lambda e: (wbar(e, "ve"), f(e)))

        def on_se(f):
            block.scalar(lambda e: (wbar(e, "se"), f(e)))

        def on_po(f):
            block.gpsimd(lambda e: (wbar(e, "po"), f(e)))

        def on_qs(f):
            block.sync(lambda e: (wbar(e, "qs"), f(e)))

        def tap(name, tiles, free, dtype):
            if not debug:
                return
            part = tiles[0].shape[0]
            d = nc.dram_tensor(f"dbg_{name}", [len(tiles), part, free], dtype,
                               kind="ExternalOutput").ap()
            dbg[name] = d

            def t_(e, name=name, tiles=tiles, d=d):
                e.wait_ge(ve.sem, ve.n)
                e.wait_ge(te.sem, te.n)
                e.wait_ge(se.sem, se.n)
                e.wait_ge(po.sem, po.n)
                for c, t in enumerate(tiles):
                    e.dma_start(out=d[c], in_=t[:] if hasattr(t, "shape") is False else t).then_inc(qs.sem, 16)
            on_qs(t_)
            qs.n += 16 * len(tiles)

        def setup(e):
            e.dma_start(out=idn_b[:], in_=idnb_d).then_inc(qs.sem, 16)
            e.dma_start(out=idn_f[:], in_=idnf_d).then_inc(qs.sem, 16)
            e.dma_start(out=trius[:], in_=trius_d).then_inc(qs.sem, 16)
            e.dma_start(out=tri168[:], in_=tri168_d).then_inc(qs.sem, 16)
            e.dma_start(out=ecap[:], in_=ecap_d).then_inc(qs.sem, 16)
            e.dma_start(out=stage[1][0:NE, 0:128],
                        in_=bo_d.rearrange("(c p) -> c p", p=128)
                        ).then_inc(qs.sem, 16)
        on_qs(setup)
        qs.n += 16 * 6
        qs_setup = qs.n

        def setup_g(e):
            e.dma_start(out=gwst[:], in_=gw_d).then_inc(qg.sem, 16)
        on_po(setup_g)
        qg.n += 16
        qg_gw = qg.n

        def setup_v(e):
            e.memset(ones_c[:], 1.0)
            e.memset(ones_r[:], 1.0)
            ve.inc(e.memset(e2g[:], 0.0))
            e.wait_ge(ve.sem, ve.n)
            e.memset(e2g[0:64, 0:1], 1.0)
            e.memset(e2g[64:128, 1:2], 1.0)
            e.memset(z16[:], 0.0)
            ve.inc(e.memset(epsr[:], EPS))
        on_ve(setup_v)

        def setup_t(e):
            e.wait_ge(qs.sem, qs_setup)
            e.wait_ge(qg.sem, qg_gw)
            te.inc(e.transpose(P6[0:128, 0:NE], stage[1][0:NE, 0:128],
                               idn_f[0:NE, 0:NE]))
        on_te(setup_t)
        te_bout = te.n

        def setup_ev(e):
            e.wait_ge(te.sem, te_bout)
            ve.inc(e.tensor_copy(bout[:], P6[0:128, 0:NE]))
        on_ve(setup_ev)

        for fc in range(CC):
            def t_one(e, fc=fc):
                e.wait_ge(ve.sem, ve.n)
                te.inc(e.transpose(
                    P7[:, 0:128].bitcast(BF16)[:, 0:NE],
                    gwst[0:NE, fc * 128:(fc + 1) * 128],
                    idn_b[0:NE, 0:NE]))
            on_te(t_one)

            def v_one(e, fc=fc):
                e.wait_ge(te.sem, te.n)
                ve.inc(e.tensor_copy(gwT[:, fc, :],
                                     P7[:, 0:128].bitcast(BF16)[:, 0:NE]))
            on_ve(v_one)
        ve_gwT = ve.n
        gwctx.close()

        ve_x_evac = {}
        for tc in range(TCH):
            def ld(e, tc=tc):
                if tc == 1:
                    e.wait_ge(te.sem, te_bout)
                if tc >= 2:
                    e.wait_ge(ve.sem, ve_x_evac[tc - 2])
                e.dma_start(out=stage[tc % 2][:],
                            in_=x_d[tc * 128:(tc + 1) * 128, :]
                            ).then_inc(qs.sem, 16)
            on_qs(ld)
            qs.n += 16
            qn = qs.n

            def tr(e, tc=tc, qn=qn):
                e.wait_ge(qs.sem, qn)
                if tc == 0:
                    e.wait_ge(ve.sem, ve_gwT)
                else:
                    e.wait_ge(ve.sem, ve_x_evac[tc - 1])
                for c in range(CC):
                    i = e.transpose(GEN[c][:, 0:128],
                                    stage[tc % 2][:, c * 128:(c + 1) * 128],
                                    idn_f[:])
                te.inc(i)
            on_te(tr)
            tn = te.n

            def ev(e, tc=tc, tn=tn):
                e.wait_ge(te.sem, tn)
                for c in range(CC):
                    i = e.tensor_copy(resid[c][:, tc * 128:(tc + 1) * 128],
                                      GEN[c][:, 0:128])
                ve.inc(i)
            on_ve(ev)
            ve_x_evac[tc] = ve.n
        ve_x_done = ve.n
        tap("resid0", [r[:] for r in resid], T, F32)

        def emit_ln_mk(res_ready=None, p67_floor=None):
            te_xn_free = te.n
            ve_res = ve.n
            if res_ready is None:
                res_ready = lambda qt: ve_res
            row_ev = {}
            rowset = [(rows[0], rows[1], rows[2], rows[0]),
                      (rows2[0], rows2[1], rows2[2], rows2[0])]

            def emit_qt(qt):
                ql = qt * 512
                par = qt % 2
                Pst = P6 if par == 0 else P7
                Ba = S0h[0] if par == 0 else S1h[0]
                Bb = S0h[1] if par == 0 else S1h[1]
                r0, r1, r2, r3 = rowset[par]
                tmx = tmpb

                def sq(e, ql=ql, qt=qt):
                    e.wait_ge(ve.sem, res_ready(qt))
                    e.wait_ge(te.sem, te_xn_free)
                    for c in range(CC):
                        i = e.tensor_tensor(xn[c][:, ql:ql + 512],
                                            resid[c][:, ql:ql + 512],
                                            resid[c][:, ql:ql + 512],
                                            ALU.mult)
                    ve.inc(i)
                on_ve(sq)
                sq_n = ve.n

                def mm_ss(e, ql=ql, sq_n=sq_n, qt=qt, Pst=Pst):
                    e.wait_ge(ve.sem, sq_n)
                    if qt >= 2:
                        e.wait_ge(ve.sem, row_ev[qt - 2])
                    elif p67_floor == "res":
                        e.wait_ge(ve.sem, res_ready(qt))
                    else:
                        e.wait_ge(ve.sem,
                                  ve_res if p67_floor is None else p67_floor)
                    for c in range(CC):
                        i = e.matmul(Pst[0:1, 0:512], ones_c[:],
                                     xn[c][:, ql:ql + 512],
                                     start=(c == 0), stop=(c == CC - 1))
                    te.inc(i)
                on_te(mm_ss)
                t1n = te.n

                def ev_ss(e, t1n=t1n, Pst=Pst, r0=r0):
                    e.wait_ge(te.sem, t1n)
                    ve.inc(e.tensor_copy(r0[:], Pst[0:1, 0:512]))
                on_ve(ev_ss)

                def cast(e, ql=ql, t1n=t1n):
                    e.wait_ge(te.sem, t1n)
                    for c in range(CC):
                        i = e.tensor_copy(xn[c][:, ql:ql + 512],
                                          resid[c][:, ql:ql + 512])
                    ve.inc(i)
                on_ve(cast)
                cn = ve.n

                def mm_s(e, ql=ql, cn=cn, Pst=Pst):
                    e.wait_ge(ve.sem, cn)
                    for c in range(CC):
                        i = e.matmul(Pst[0:1, 0:512], ones_c[:],
                                     xn[c][:, ql:ql + 512],
                                     start=(c == 0), stop=(c == CC - 1))
                    te.inc(i)
                on_te(mm_s)
                t2n = te.n

                def rowmath(e, t2n=t2n, qt=qt, Pst=Pst, r0=r0, r1=r1,
                            r2=r2):
                    e.wait_ge(te.sem, t2n)
                    ve.inc(e.tensor_copy(r1[:], Pst[0:1, 0:512]))
                    row_ev[qt] = ve.n
                    e.wait_ge(ve.sem, ve.n)
                    ve.inc(e.tensor_tensor(r2[:], r1[:], r1[:], ALU.mult))
                    e.wait_ge(ve.sem, ve.n)
                    i = e.scalar_tensor_tensor(r0[:], r0[:], 1024.0,
                                               r2[:], ALU.mult,
                                               ALU.subtract)
                    ve.inc(i)
                on_ve(rowmath)
                rmn = ve.n

                def rowact(e, rmn=rmn, r0=r0, r2=r2):
                    e.wait_ge(ve.sem, rmn)
                    se.inc(e.activation(r2[:], r0[:], AF.Ln,
                                        bias=epsr[:],
                                        scale=1.0 / (1024.0 * 1024.0)))
                    e.wait_ge(se.sem, se.n)
                    i = e.activation(r2[:], r2[:], AF.Exp, scale=-0.5)
                    se.inc(i)
                on_se(rowact)
                ran = se.n

                def rowt3(e, ran=ran, r1=r1, r2=r2, r3=r3):
                    e.wait_ge(se.sem, ran)
                    ve.inc(e.tensor_tensor(r3[:], r1[:], r2[:], ALU.mult))
                on_ve(rowt3)
                t3n = ve.n

                def bcast(e, t3n=t3n, Ba=Ba, Bb=Bb, r2=r2, r3=r3):
                    e.wait_ge(ve.sem, t3n)
                    e.matmul(Ba, ones_r[:], r2[:], start=True, stop=True)
                    i = e.matmul(Bb, ones_r[:], r3[:], start=True, stop=True)
                    te.inc(i)
                on_te(bcast)
                bn = te.n

                def apply(e, ql=ql, bn=bn, Ba=Ba, Bb=Bb, tmx=tmx):
                    e.wait_ge(te.sem, bn)
                    for c in range(CC):
                        ve.inc(e.tensor_tensor(tmx[:],
                                               resid[c][:, ql:ql + 512],
                                               Ba, ALU.mult))
                        e.wait_ge(ve.sem, ve.n)
                        i = e.scalar_tensor_tensor(
                            xn[c][:, ql:ql + 512], Bb, -1.0 / 1024.0,
                            tmx[:], ALU.mult, ALU.add)
                        ve.inc(i)
                on_ve(apply)
                return ve.n
            return emit_qt

        def emit_ln(res_ready=None):
            eq = emit_ln_mk(res_ready)
            for qt in range(NQ):
                r = eq(qt)
            return r

        def make_trload(stgs):
            state = {"stg_free": {0: 0, 1: 0}, "trb": 0, "call": 0,
                     "bank_ev": {0: None, 1: None, 2: None, 3: None}}
            BANKS = [P6[:, 0:512], P7[:, 0:512], Y0[:], Y1[:]]

            def trload(src_ap, nch, dst_slices, ld_wait_qg=None,
                       dst_wait_qs=None, on_g=False, floor_ve=None,
                       banks4=False):
                sp = state["call"] % 2
                state["call"] += 1
                stg = stgs[sp]
                lq = qg if on_g else qs

                def ld(eng, src_ap=src_ap, nch=nch, sp=sp, stg=stg, lq=lq):
                    if ld_wait_qg is not None:
                        eng.wait_ge(qg.sem, ld_wait_qg)
                    eng.wait_ge(te.sem, state["stg_free"][sp])
                    eng.dma_start(
                        out=stg[:].rearrange("p c m -> p (c m)")
                        .rearrange("p (c m) -> p c m", c=nch),
                        in_=src_ap.rearrange("(c p) m -> p c m", p=128),
                    ).then_inc(lq.sem, 16)
                (on_po if on_g else on_qs)(ld)
                lq.n += 16
                ldn = lq.n
                inner = 4096 // nch
                flat = stg[:].rearrange("p c m -> p (c m)")
                floor = ve.n if floor_ve is None else floor_ve
                for dst_ap, blocks in dst_slices:
                    par = state["trb"] % (4 if banks4 else 2)
                    state["trb"] += 1
                    Pb = BANKS[par].bitcast(BF16)

                    def trf(eng, blocks=blocks, Pb=Pb, ldn=ldn, inner=inner,
                            par=par, floor=floor, lq=lq):
                        eng.wait_ge(lq.sem, ldn)
                        bev = state["bank_ev"][par]
                        if bev:
                            eng.wait_ge(ve.sem, bev[1])
                        eng.wait_ge(ve.sem, floor)
                        for bi, (br, bc) in enumerate(blocks):
                            off = br * inner + bc * 128
                            i = eng.transpose(
                                Pb[:, 128 * bi:128 * (bi + 1)],
                                flat[:, off:off + 128], idn_b[:])
                        te.inc(i)
                    on_te(trf)
                    tn = te.n

                    def evf(eng, dst_ap=dst_ap, Pb=Pb, tn=tn,
                            nb=len(blocks), dw=dst_wait_qs, par=par):
                        eng.wait_ge(te.sem, tn)
                        if dw is not None:
                            eng.wait_ge(dw[0].sem, dw[1])
                        ve.inc(eng.tensor_copy(dst_ap[:, 0:128 * nb],
                                               Pb[:, 0:128 * nb]))
                        state["bank_ev"][par] = ("ve", ve.n)
                    on_ve(evf)
                state["stg_free"][sp] = te.n
            trload.state = state
            return trload

        def emit_routing(xn_ready):
            for qt in range(NQ):
                ql = qt * 512

                def mm_lg(e, ql=ql, xn_ready=xn_ready):
                    e.wait_ge(ve.sem, max(xn_ready, ve.n))
                    for c in range(CC):
                        i = e.matmul(S0h[0][0:NE, :], gwT[:, c, :],
                                     xn[c][:, ql:ql + 512],
                                     start=(c == 0), stop=(c == CC - 1))
                    te.inc(i)
                on_te(mm_lg)
                lgn = te.n

                def ev_lg(e, lgn=lgn):
                    e.wait_ge(te.sem, lgn)
                    ve.inc(e.tensor_copy(Lsb[:], S0h[0][0:NE, :]))
                on_ve(ev_lg)
                evn = ve.n

                def tr_l(e, qt=qt, evn=evn):
                    e.wait_ge(ve.sem, max(evn, ve.n))
                    for tci in range(4):
                        i = e.transpose(
                            S1h[0][:, 8 * tci:8 * tci + 8],
                            Lsb[0:NE, tci * 128:(tci + 1) * 128],
                            idn_f[0:NE, 0:NE])
                    te.inc(i)
                on_te(tr_l)
                trn = te.n

                def ev_l(e, qt=qt, trn=trn):
                    e.wait_ge(te.sem, trn)
                    ve.inc(e.tensor_copy(Ltm[:, 4 * qt:4 * qt + 4, :],
                                         S1h[0][:, 0:32]))
                on_ve(ev_l)
            lt_done = ve.n

            def stats(e, lt_done=lt_done):
                e.wait_ge(ve.sem, lt_done)
                ve.inc(e.reduce_max(topv[:], Ltm[:], axis=AX.X))
            on_ve(stats)
            tv_n = ve.n

            def sig(e, tv_n=tv_n):
                e.wait_ge(ve.sem, tv_n)
                se.inc(e.activation(rwv[:], topv[:], AF.Sigmoid))
            on_se(sig)
            sg_n = se.n

            def mask(e, sg_n=sg_n):
                e.wait_ge(se.sem, sg_n)
                for ei in range(NE):
                    i = e.tensor_tensor(Mtm[:, :, ei], Ltm[:, :, ei],
                                        topv[:], ALU.is_equal)
                ve.inc(i)
            on_ve(mask)
            mk_n = ve.n

            def mm_tot(e, mk_n=mk_n):
                e.wait_ge(ve.sem, max(mk_n, ve.n))
                for tc in range(TCH):
                    i = e.matmul(P7[0:1, 8 * tc:8 * tc + 8], ones_c[:],
                                 Mtm[:, tc, :], start=True, stop=True)
                te.inc(i)
            on_te(mm_tot)
            mtn = te.n

            def ev_tot(e, mtn=mtn):
                e.wait_ge(te.sem, mtn)
                ve.inc(e.tensor_copy(totr[:], P7[0:1, 0:128]))
            on_ve(ev_tot)
            etn = ve.n

            def tr_tot(e, etn=etn):
                e.wait_ge(ve.sem, etn)
                te.inc(e.transpose(P7[:, 128:256].bitcast(BF16)[:, 0:1],
                                   totr[:], idn_b[0:1, 0:1]))
            on_te(tr_tot)
            ttn = te.n

            def ev_totc(e, ttn=ttn):
                e.wait_ge(te.sem, ttn)
                ve.inc(e.tensor_copy(totc[:],
                                     P7[:, 128:256].bitcast(BF16)[:, 0:1]))
            on_ve(ev_totc)
            etc_n = ve.n

            def mm_off(e, etc_n=etc_n):
                e.wait_ge(ve.sem, etc_n)
                te.inc(e.matmul(P7[:, 256:257], tri168[:], totc[:],
                                start=True, stop=True))
            on_te(mm_off)
            mon = te.n

            def ev_off(e, mon=mon):
                e.wait_ge(te.sem, mon)
                ve.inc(e.tensor_tensor(offc[:], P7[:, 256:257], ecap[:],
                                       ALU.add))
            on_ve(ev_off)
            eon = ve.n

            def tr_off(e, eon=eon):
                e.wait_ge(ve.sem, eon)
                te.inc(e.transpose(P7[0:1, 384:512], offc[:], idn_f[:]))
            on_te(tr_off)
            ton = te.n

            def ev_offr(e, ton=ton):
                e.wait_ge(te.sem, ton)
                ve.inc(e.tensor_copy(offrow[:], P7[0:1, 384:512]))
            on_ve(ev_offr)
            orn = ve.n

            def mm_rank(e, orn=orn):
                e.wait_ge(ve.sem, max(orn, ve.n))
                for tc in range(TCH):
                    e.matmul(P6[:, 8 * tc:8 * tc + 8], trius[:],
                             Mtm[:, tc, :], start=True, stop=False)
                    i = e.matmul(P6[:, 8 * tc:8 * tc + 8], ones_r[:],
                                 offrow[:, 8 * tc:8 * tc + 8],
                                 start=False, stop=True)
                te.inc(i)
            on_te(mm_rank)
            rkn = te.n

            def ev_rank(e, rkn=rkn):
                e.wait_ge(te.sem, rkn)
                ve.inc(e.tensor_tensor(t8b[:], P6[:, 0:128],
                                       Mtm[:].rearrange("p a b -> p (a b)"),
                                       ALU.mult))
                e.wait_ge(ve.sem, ve.n)
                ve.inc(e.reduce_sum(
                    pos_f[:], t8b[:].rearrange("p (a b) -> p a b", b=8),
                    axis=AX.X))
                e.wait_ge(ve.sem, ve.n)
                ve.inc(e.tensor_copy(pos_i[:], pos_f[:]))
            on_ve(ev_rank)
            return ve.n

        def emit_ffn_sparse(wbuf, xgb, Gs, Us, Og, xst, otm, pos_ready,
                            w_ready, trload, after_expert=None,
                            after_tc=None, st_hook={}, xg_on_g=False):
            te_exp = {}
            ns = len(xst)
            ng = len(otm)
            SBH = [S0[:, 0:512], S0[:, 512:1024],
                   S1[:, 0:512], S1[:, 512:1024]]
            sc_done = {}
            sx_ev = {-1: ve.n}
            for tc in range(TCH):
                s = tc % ns
                xs = xst[s]

                Sb = SBH[(tc % ns) if ns == 4 else 2 * (tc % 2)]

                def tr_x(e, tc=tc, Sb=Sb):
                    e.wait_ge(ve.sem, sx_ev.get(tc - ns, sx_ev.get(-1, 0)))
                    if tc >= ns:
                        e.wait_ge(qg.sem, sc_done[tc - ns])
                    for c in range(CC):
                        i = e.transpose(
                            Sb.bitcast(BF16)[:, 128 * c:128 * (c + 1)],
                            xn[c][:, tc * 128:(tc + 1) * 128], idn_b[:])
                    te.inc(i)
                on_te(tr_x)
                txn = te.n

                def ev_x(e, tc=tc, txn=txn, xs=xs, Sb=Sb):
                    e.wait_ge(te.sem, txn)
                    i = e.tensor_scalar(
                        xs[:], Sb.bitcast(BF16)[:, 0:1024],
                        rwv[:, tc:tc + 1], None, ALU.mult)
                    ve.inc(i)
                on_ve(ev_x)
                exn = ve.n
                sx_ev[tc] = exn

                def sc_x(e, tc=tc, exn=exn, xs=xs, pos_ready=pos_ready):
                    e.wait_ge(ve.sem, max(exn, pos_ready))
                    e.indirect_dma_start(
                        out=scr_X,
                        out_offset=IndirectOffsetOnAxis(pos_i[:, tc:tc + 1], 0),
                        in_=xs[:], in_offset=None,
                    ).then_inc(qg.sem, 16)
                on_po(sc_x)
                qg.n += 16
                sc_done[tc] = qg.n
            scat_all = qg.n

            og_store = {}
            gu_free = 0
            for e_i in range(NE):
                s = e_i % 2
                wgT, wuT, wdT = wbuf[s]
                xg = xgb[s]
                og = Og[s]
                xgf = xg[:].rearrange("p c m -> p (c m)")

                trload(scr_X[e_i * CAP:(e_i + 1) * CAP, :], 4,
                       [(xgf[:, fc * 512:(fc + 2) * 512],
                         [(sc, f) for f in (fc, fc + 1) for sc in range(4)])
                        for fc in range(0, CC, 2)],
                       ld_wait_qg=scat_all,
                       dst_wait_qs=((qa, og_store[e_i - 2])
                                    if e_i >= 2 else None),
                       on_g=xg_on_g, floor_ve=pos_ready)
                xg_n = ve.n
                xg_se = se.n
                if after_expert is not None and e_i >= 1:
                    after_expert(e_i - 1, te_exp)

                wr = w_ready[e_i]
                if len(wr) == 4:
                    ch_w, n_w = wr[0], wr[1]
                    ch_w2, n_w2 = wr[2], wr[3]
                else:
                    ch_w, n_w = wr
                    ch_w2, n_w2 = None, 0
                SB4 = [S0h[0], S0h[1], S1h[0], S1h[1]]

                def mm_g(eng, n_w=n_w, wgT=wgT, ch_w=ch_w, xg=xg,
                         xg_n=xg_n, gf=gu_free, xg_se=xg_se,
                         ch_w2=ch_w2, n_w2=n_w2):
                    eng.wait_ge(ch_w.sem, n_w)
                    if ch_w2 is not None:
                        eng.wait_ge(ch_w2.sem, n_w2)
                    eng.wait_ge(se.sem, xg_se)
                    eng.wait_ge(ve.sem, max(xg_n, gf))
                    for ic in range(4):
                        for c in range(CC):
                            i = eng.matmul(
                                SB4[ic], wgT[:, c, ic * 128:(ic + 1) * 128],
                                xg[:, c, :],
                                start=(c == 0), stop=(c == CC - 1))
                    te.inc(i)
                on_te(mm_g)
                gn = te.n

                def ev_g(eng, gn=gn, Gs=Gs):
                    eng.wait_ge(te.sem, gn)
                    eng.tensor_copy(Gs.rearrange("p a b -> p (a b)")
                                    [:, 0:1024], S0[:])
                    i = eng.tensor_copy(Gs.rearrange("p a b -> p (a b)")
                                        [:, 1024:2048], S1[:])
                    ve.inc(i, 2)
                on_ve(ev_g)
                evg_n = ve.n

                def silu(eng, evg_n=evg_n, Gs=Gs):
                    eng.wait_ge(ve.sem, evg_n)
                    se.inc(eng.activation(
                        Gs.rearrange("p a b -> p (a b)"),
                        Gs.rearrange("p a b -> p (a b)"), AF.Silu))
                on_se(silu)
                sl_n = se.n

                def mm_u(eng, n_w=n_w, wuT=wuT, ch_w=ch_w, xg=xg,
                         evg_n=evg_n):
                    eng.wait_ge(ve.sem, evg_n)
                    for ic in range(4):
                        for c in range(CC):
                            i = eng.matmul(
                                SB4[ic], wuT[:, c, ic * 128:(ic + 1) * 128],
                                xg[:, c, :],
                                start=(c == 0), stop=(c == CC - 1))
                    te.inc(i)
                on_te(mm_u)
                un = te.n

                def ev_u(eng, un=un, Us=Us):
                    eng.wait_ge(te.sem, un)
                    eng.tensor_copy(Us.rearrange("p a b -> p (a b)")
                                    [:, 0:1024], S0[:])
                    i = eng.tensor_copy(Us.rearrange("p a b -> p (a b)")
                                        [:, 1024:2048], S1[:])
                    ve.inc(i, 2)
                on_ve(ev_u)
                gu_done = ve.n

                def hmul(eng, sl_n=sl_n, gu_done=gu_done, Gs=Gs, Us=Us):
                    eng.wait_ge(se.sem, sl_n)
                    eng.wait_ge(ve.sem, gu_done)
                    ve.inc(eng.tensor_tensor(
                        Us.rearrange("p a b -> p (a b)"),
                        Us.rearrange("p a b -> p (a b)"),
                        Gs.rearrange("p a b -> p (a b)"), ALU.mult))
                on_ve(hmul)
                h_n = ve.n

                DB = [Y0[:], Y1[:], S0h[0], S0h[1], S1h[0], S1h[1]]
                karr = [(sc4, hh) for sc4 in range(4) for hh in range(2)]

                def mm_d1(eng, h_n=h_n, wdT=wdT, Us=Us):
                    eng.wait_ge(ve.sem, h_n)
                    for k in range(6):
                        sc4, hh = karr[k]
                        for ic in range(4):
                            i = eng.matmul(
                                DB[k], Us[:, ic, sc4 * 128:(sc4 + 1) * 128],
                                wdT[:, ic, hh * 512:(hh + 1) * 512],
                                start=(ic == 0), stop=(ic == 3))
                    te.inc(i)
                on_te(mm_d1)
                d1n = te.n

                def ev_d1(eng, d1n=d1n, og=og, e_i=e_i):
                    eng.wait_ge(te.sem, d1n)
                    if e_i >= 1:
                        eng.wait_ge(qa.sem, og_store[e_i - 1])
                    for k in range(6):
                        sc4, hh = karr[k]
                        i = eng.tensor_copy(
                            og[:, sc4, hh * 512:(hh + 1) * 512], DB[k])
                    ve.inc(i)
                on_ve(ev_d1)
                ed1 = ve.n

                def mm_d2(eng, ed1=ed1, wdT=wdT, Us=Us):
                    eng.wait_ge(ve.sem, ed1)
                    for k in range(6, 8):
                        sc4, hh = karr[k]
                        for ic in range(4):
                            i = eng.matmul(
                                DB[k - 6],
                                Us[:, ic, sc4 * 128:(sc4 + 1) * 128],
                                wdT[:, ic, hh * 512:(hh + 1) * 512],
                                start=(ic == 0), stop=(ic == 3))
                    te.inc(i)
                on_te(mm_d2)
                d2n = te.n

                def ev_d2(eng, d2n=d2n, og=og):
                    eng.wait_ge(te.sem, d2n)
                    for k in range(6, 8):
                        sc4, hh = karr[k]
                        i = eng.tensor_copy(
                            og[:, sc4, hh * 512:(hh + 1) * 512], DB[k - 6])
                    ve.inc(i)
                on_ve(ev_d2)
                og_done = ve.n
                gu_free = og_done

                def st_og(eng, e_i=e_i, og=og, og_done=og_done):
                    eng.wait_ge(ve.sem, og_done)
                    eng.dma_start(
                        out=scr_O[e_i * CAP:(e_i + 1) * CAP, :]
                        .rearrange("(c p) m -> p c m", p=128),
                        in_=og[:]).then_inc(qa.sem, 16)
                on_se(st_og)
                qa.n += 16
                og_store[e_i] = qa.n

                te_exp[e_i] = te.n
            if after_expert is not None:
                after_expert(NE - 1, te_exp)
            all_og = qa.n

            ga_done = {}
            ot_used = {}
            ao_ev = {-1: ve.n}
            for tc in range(TCH):
                s = tc % ng
                ot = otm[s]

                def ga(eng, tc=tc, ot=ot, s=s):
                    eng.wait_ge(qa.sem, all_og)
                    if tc >= ng:
                        eng.wait_ge(te.sem, ot_used[tc - ng])
                    if after_tc is not None and (tc - 2) in st_hook:
                        eng.wait_ge(qs.sem, st_hook[tc - 2])
                    eng.indirect_dma_start(
                        out=ot[:], out_offset=None,
                        in_=scr_O,
                        in_offset=IndirectOffsetOnAxis(pos_i[:, tc:tc + 1],
                                                       0),
                    ).then_inc(qg.sem, 16)
                on_po(ga)
                qg.n += 16
                gn_ = qg.n

                Sb = SBH[(tc % ng) if ng == 4 else 2 * (tc % 2)]

                def tr_o(eng, tc=tc, gn_=gn_, ot=ot, Sb=Sb):
                    eng.wait_ge(qg.sem, gn_)
                    eng.wait_ge(ve.sem, ao_ev.get(tc - ng, ao_ev.get(-1, 0)))
                    for c in range(CC):
                        i = eng.transpose(
                            Sb.bitcast(BF16)
                            [:, 128 * c:128 * (c + 1)],
                            ot[:, c * 128:(c + 1) * 128], idn_b[:])
                    te.inc(i)
                on_te(tr_o)
                ot_used[tc] = te.n
                trn = te.n

                def add_o(eng, tc=tc, trn=trn, Sb=Sb):
                    eng.wait_ge(te.sem, trn)
                    for c in range(CC):
                        i = eng.scalar_tensor_tensor(
                            resid[c][:, tc * 128:(tc + 1) * 128],
                            Sb.bitcast(BF16)
                            [:, 128 * c:128 * (c + 1)],
                            0.5, resid[c][:, tc * 128:(tc + 1) * 128],
                            ALU.mult, ALU.add)
                    ve.inc(i)
                on_ve(add_o)
                ao_ev[tc] = ve.n
                if after_tc is not None:
                    st_hook[tc] = after_tc(tc, ve.n)
            return ve.n, te.n

        new_phase()
        with ExitStack() as pctx:
            def psb(name, shape, dtype=BF16):
                return pctx.enter_context(nc.sbuf_tensor(name, shape, dtype))

            wbuf = []
            for s in range(2):
                wbuf.append((psb(f"wgT{s}", [128, CC, DI]),
                             psb(f"wuT{s}", [128, CC, DI]),
                             psb(f"wdT{s}", [128, 4, C])))
            Gs = xn[2][:].rearrange("p (a b) -> p a b", a=4)
            Us = xn[3][:].rearrange("p (a b) -> p a b", a=4)
            xgb = [psb("xgb0", [128, CC, 512]), psb("xgb1", [128, CC, 512])]
            Og = [x[:].rearrange("p c m -> p (c m)")
                  .rearrange("p (c m) -> p c m", c=4) for x in xgb]
            stg = [psb("stg_p0", [128, 4, 1024]), psb("stg_p1", [128, 4, 1024])]
            trload = make_trload(stg)
            xse = [psb("xse0", [128, 1024]), psb("xse1", [128, 1024])]
            xst = ([stage[i][:].bitcast(BF16)[:, 0:1024] for i in range(2)]
                   + [x[:] for x in xse])
            otm = xst[0:2]

            w_ready1 = {}
            w4_done = {}
            w2_done = {}

            def emit_w12(e_i):
                def w12(eng, e_i=e_i):
                    eng.dma_start(out=scr_g[e_i], in_=wg_d[e_i]
                                  ).then_inc(qg.sem, 16)
                    eng.dma_start(out=scr_u[e_i], in_=wu_d[e_i]
                                  ).then_inc(qg.sem, 16)
                    eng.dma_start(out=scr_d[e_i], in_=wd_d[e_i]
                                  ).then_inc(qg.sem, 16)
                on_po(w12)
                qg.n += 48
                w2_done[e_i] = qg.n


            prep_floor = [None]

            def emit_prep(e_i, te_exp):
                s = e_i % 2
                wgT, wuT, wdT = wbuf[s]
                trload.state["bank_ev"][2] = ("ve", ve.n)
                trload.state["bank_ev"][3] = ("ve", ve.n)
                dw = (qa, w4_done[e_i - 2]) if e_i >= 2 else None
                for mat, dst in ((scr_g[e_i], wgT), (scr_u[e_i], wuT)):
                    dflat = dst[:].rearrange("p c m -> p (c m)")
                    trload(
                        mat, 4,
                        [(dflat[:, fc * 512:(fc + 2) * 512],
                          [(ic, f) for f in (fc, fc + 1) for ic in range(4)])
                         for fc in range(0, CC, 2)],
                        ld_wait_qg=w2_done[e_i], dst_wait_qs=dw,
                        floor_ve=prep_floor[0], banks4=True)
                trload(
                    scr_d[e_i], 8,
                    [(wdT[:, ic, :], [(hc, ic) for hc in range(CC)])
                     for ic in range(4)],
                    ld_wait_qg=w2_done[e_i], dst_wait_qs=dw,
                    floor_ve=prep_floor[0], banks4=True)
                ev_now = ve.n

                def st_T(eng, e_i=e_i, wgT=wgT, wuT=wuT, wdT=wdT,
                         ev_now=ev_now):
                    eng.wait_ge(ve.sem, ev_now)
                    eng.dma_start(
                        out=scr_gT[e_i].rearrange("(c p) m -> p c m", p=128),
                        in_=wgT[:]).then_inc(qa.sem, 16)
                    eng.dma_start(
                        out=scr_uT[e_i].rearrange("(c p) m -> p c m", p=128),
                        in_=wuT[:]).then_inc(qa.sem, 16)
                    eng.dma_start(
                        out=scr_dT[e_i].rearrange("(c p) m -> p c m", p=128),
                        in_=wdT[:]).then_inc(qa.sem, 16)
                on_se(st_T)
                qa.n += 48
                w4_done[e_i] = qa.n
                w_ready1[e_i] = (ve, ve.n, se, se.n)

            for e_i in range(NE):
                emit_w12(e_i)

            xn1 = emit_ln(res_ready=lambda qt: ve_x_evac[4 * qt + 3])
            tap("xn1", [t[:] for t in xn], T, BF16)
            pr1 = emit_routing(xn1)
            prep_floor[0] = pr1
            emit_prep(0, {})
            emit_prep(1, {})
            tap("mtm1", [Mtm[:]], TCH * NE, BF16)
            tap("rwv1", [rwv[:]], TCH, F32)
            tap("pos1", [pos_f[:]], TCH, F32)

            def prep_next(e_i, te_exp):
                if e_i + 2 < NE:
                    emit_prep(e_i + 2, te_exp)

            emit_ffn_sparse(wbuf, xgb, Gs, Us, Og, xst, otm, pr1,
                            w_ready1, trload, after_expert=prep_next)
            tap("resid1", [r[:] for r in resid], T, F32)
            tap("wgt0", [wbuf[0][0][:].rearrange("p c m -> p (c m)")], CC * DI, BF16)

        w4_all = [qa.n]
        new_phase()
        with ExitStack() as actx:
            y = [actx.enter_context(nc.sbuf_tensor(f"y{c}", [128, T], BF16))
                 for c in range(CC)]
            rza = actx.enter_context(nc.sbuf_tensor("rza", [1, 512], F32))
            rzb_r = actx.enter_context(nc.sbuf_tensor("rzb_r", [1, 512], F32))
            ysb = actx.enter_context(nc.sbuf_tensor("ysb", [128, 512], BF16))
            t4 = actx.enter_context(nc.sbuf_tensor("t4", [4, 128], F32))

            xn2 = emit_ln()

            with ExitStack() as a1ctx:
                def a1(name, shape, dtype=BF16):
                    return a1ctx.enter_context(
                        nc.sbuf_tensor(name, shape, dtype))

                vt = a1("vt", [128, TCH, 8, 65])
                tap("xn2", [t[:] for t in xn], T, BF16)
                wvsb = a1("wvsb", [128, CC, 512])
                wqk = [(a1(f"wqs{s}", [128, CC, 128]),
                        a1(f"wks{s}", [128, CC, 128])) for s in range(2)]
                Qg = a1("Qg", [128, T])
                Kg = a1("Kg", [128, T])
                pts = [a1(f"pts{i}", [128, 1024]) for i in range(2)]

                def vones(eng):
                    ve.inc(eng.memset(vt[:, :, :, 64:65], 1.0))
                on_ve(vones)

                def ld_qk0(eng):
                    eng.dma_start(
                        out=wqk[0][0][:],
                        in_=wq_d[:, 0:128].rearrange("(c p) m -> p c m",
                                                     p=128),
                    ).then_inc(qg.sem, 16)
                    eng.dma_start(
                        out=wqk[0][1][:],
                        in_=wk_d[:, 0:128].rearrange("(c p) m -> p c m",
                                                     p=128),
                    ).then_inc(qg.sem, 16)
                on_po(ld_qk0)
                qg.n += 32
                qk_ld = {0: qg.n}

                def emit_vhalf(half, xn_ready):
                    def ldv(eng, half=half):
                        eng.dma_start(
                            out=wvsb[:],
                            in_=wv_d[:, half * 512:(half + 1) * 512]
                            .rearrange("(c p) m -> p c m", p=128),
                        ).then_inc(qg.sem, 16)
                    on_po(ldv)
                    qg.n += 16
                    vl = qg.n
                    vfloor = ve.n
                    vbank_ev = {}
                    for tc in range(TCH):
                        def mmv(eng, tc=tc, vl=vl, xn_ready=xn_ready,
                                bev=vbank_ev.get(tc % 4, vfloor)):
                            eng.wait_ge(qg.sem, vl)
                            eng.wait_ge(ve.sem, max(xn_ready, bev))
                            eng.wait_ge(se.sem, se.n)
                            bank = GEN[tc % 4]
                            for c in range(CC):
                                i = eng.matmul(
                                    bank, xn[c][:, tc * 128:(tc + 1) * 128],
                                    wvsb[:, c, :],
                                    start=(c == 0), stop=(c == CC - 1))
                            te.inc(i)
                        on_te(mmv)
                        vn = te.n

                        def evv(eng, tc=tc, vn=vn):
                            eng.wait_ge(te.sem, vn)
                            bank = GEN[tc % 4]
                            ve.inc(eng.tensor_copy(
                                vt[:, tc, :, 0:64],
                                bank[:, 0:512].rearrange("p (h c) -> p h c",
                                                         h=8)))
                        on_ve(evv)
                        vbank_ev[tc % 4] = ve.n
                    return ve.n

                emit_vhalf(0, xn2)

                yfm_cnt = {0: (0, 0), 1: (0, 0)}
                s_exp_cnt = {0: 0, 1: 0}
                rzb_rd = {0: 0, 1: 0}

                def emit_group(g):
                    sset = g % 2
                    wqs, wks = wqk[sset]
                    if g + 1 < 8:
                        def ldn(eng, g=g):
                            eng.wait_ge(te.sem, te.n)
                            c0 = (g + 1) * 128
                            eng.dma_start(
                                out=wqk[(g + 1) % 2][0][:],
                                in_=wq_d[:, c0:c0 + 128]
                                .rearrange("(c p) m -> p c m", p=128),
                            ).then_inc(qg.sem, 16)
                            eng.dma_start(
                                out=wqk[(g + 1) % 2][1][:],
                                in_=wk_d[:, c0:c0 + 128]
                                .rearrange("(c p) m -> p c m", p=128),
                            ).then_inc(qg.sem, 16)
                        on_po(ldn)
                        qg.n += 32
                        qk_ld[g + 1] = qg.n

                    pbe = {0: ve.n, 1: ve.n}
                    for which, wsl, dst in ((0, wqs, Qg), (1, wks, Kg)):
                        for qt in range(NQ):
                            ql = qt * 512
                            bit = (which * NQ + qt) % 2
                            bankP = P6 if bit == 0 else P7

                            def mmp(eng, wsl=wsl, ql=ql, bankP=bankP, g=g,
                                    bev=pbe[bit]):
                                eng.wait_ge(qg.sem, qk_ld[g])
                                eng.wait_ge(ve.sem, bev)
                                for c in range(CC):
                                    i = eng.matmul(bankP[:], wsl[:, c, :],
                                                   xn[c][:, ql:ql + 512],
                                                   start=(c == 0),
                                                   stop=(c == CC - 1))
                                te.inc(i)
                            on_te(mmp)
                            pn = te.n

                            def evp(eng, dst=dst, ql=ql, bankP=bankP, pn=pn):
                                eng.wait_ge(te.sem, pn)
                                ve.inc(eng.tensor_copy(dst[:, ql:ql + 512],
                                                       bankP[:]))
                            on_ve(evp)
                            pbe[bit] = ve.n
                    qk_done = ve.n

                    def sq_q(eng, qk_done=qk_done):
                        eng.wait_ge(ve.sem, qk_done)
                        eng.wait_ge(se.sem, se.n)
                        eng.wait_ge(te.sem, te.n)
                        eng.tensor_tensor(pts[0][:], Qg[:, 0:1024],
                                          Qg[:, 0:1024], ALU.mult)
                        i = eng.tensor_tensor(pts[1][:], Qg[:, 1024:2048],
                                              Qg[:, 1024:2048], ALU.mult)
                        ve.inc(i)
                    on_ve(sq_q)
                    sqq_n = ve.n

                    def mm_qss(eng, sqq_n=sqq_n):
                        eng.wait_ge(ve.sem, sqq_n)
                        for tc in range(TCH):
                            src = pts[0] if tc < 8 else pts[1]
                            i = eng.matmul(
                                P7[:, 2 * tc:2 * tc + 2],
                                src[:, (tc % 8) * 128:(tc % 8 + 1) * 128],
                                e2g[:], start=True, stop=True)
                        te.inc(i)
                    on_te(mm_qss)
                    qssn = te.n

                    def sq_k(eng, qssn=qssn):
                        eng.wait_ge(te.sem, qssn)
                        eng.tensor_tensor(pts[0][:], Kg[:, 0:1024],
                                          Kg[:, 0:1024], ALU.mult)
                        i = eng.tensor_tensor(pts[1][:], Kg[:, 1024:2048],
                                              Kg[:, 1024:2048], ALU.mult)
                        ve.inc(i)
                    on_ve(sq_k)
                    sqk_n = ve.n

                    def mm_kss(eng, sqk_n=sqk_n):
                        eng.wait_ge(ve.sem, sqk_n)
                        for tc in range(TCH):
                            src = pts[0] if tc < 8 else pts[1]
                            i = eng.matmul(
                                P7[:, 32 + 2 * tc:32 + 2 * tc + 2],
                                src[:, (tc % 8) * 128:(tc % 8 + 1) * 128],
                                e2g[:], start=True, stop=True)
                        te.inc(i)
                    on_te(mm_kss)
                    kssn = te.n

                    def ev_qss(eng, kssn=kssn):
                        eng.wait_ge(te.sem, kssn)
                        eng.reduce_max(
                            q4[:, 0:2],
                            P7[:, 0:32].rearrange("p (t j) -> p j t", j=2),
                            axis=AX.X)
                        i = eng.reduce_max(
                            q4[:, 2:4],
                            P7[:, 32:64].rearrange("p (t j) -> p j t", j=2),
                            axis=AX.X)
                        ve.inc(i)
                    on_ve(ev_qss)
                    q4n = ve.n

                    def tr_q4(eng, q4n=q4n):
                        eng.wait_ge(ve.sem, q4n)
                        te.inc(eng.transpose(P7[0:4, 64:192], q4[:],
                                             idn_f[:]))
                    on_te(tr_q4)
                    t4n = te.n

                    def red_q4(eng, t4n=t4n):
                        eng.wait_ge(te.sem, t4n)
                        e = eng
                        ve.inc(e.tensor_copy(t4[:], P7[0:4, 64:192]))
                        e.wait_ge(ve.sem, ve.n)
                        ve.inc(e.reduce_max(t4m[:], t4[:], axis=AX.X))
                        e.wait_ge(ve.sem, ve.n)
                        ve.inc(e.tensor_copy(z16[0:4, 0:1], t4m[:]))
                    on_ve(red_q4)
                    z16n = ve.n

                    def tr_z(eng, z16n=z16n):
                        eng.wait_ge(ve.sem, z16n)
                        te.inc(eng.transpose(P7[0:16, 192:208], z16[:],
                                             idn_f[0:16, 0:16]))
                    on_te(tr_z)
                    tzn = te.n

                    def mk_b2(eng, tzn=tzn):
                        eng.wait_ge(te.sem, tzn)
                        ve.inc(eng.tensor_copy(zrow[:], P7[0:1, 192:208]))
                        eng.wait_ge(ve.sem, ve.n)
                        ve.inc(eng.tensor_tensor(b2[:], zrow[:, 0:2],
                                                 zrow[:, 2:4], ALU.add))
                    on_ve(mk_b2)
                    b2n = ve.n

                    def bc_b2(eng, b2n=b2n):
                        eng.wait_ge(ve.sem, b2n)
                        te.inc(eng.matmul(P7[:, 208:210], ones_r[:], b2[:],
                                          start=True, stop=True))
                    on_te(bc_b2)
                    bcn = te.n

                    def ev_b2(eng, bcn=bcn):
                        eng.wait_ge(te.sem, bcn)
                        eng.wait_ge(se.sem, max(s_exp_cnt[0],
                                                s_exp_cnt[1]))
                        ve.inc(eng.tensor_scalar(
                            biasg[:], P7[:, 208:210],
                            -0.5 * SCALE * 1.01, None, ALU.mult))
                    on_ve(ev_b2)
                    bias_n = ve.n

                    for qt in range(NQ - 1, -1, -1):
                        nk = 4 * qt + 4
                        nb = nk // 2
                        ql = qt * 512
                        T0 = te.n
                        se0 = se.n
                        po0 = po.n
                        pred_te = lambda b, j, T0=T0: T0 + 2 * b + j + 1
                        pred_se = lambda b, j, se0=se0: se0 + 2 * b + j + 1

                        def pred_po(b, j, qt=qt, po0=po0):
                            return po0 + (b - 2 * qt) * 2 + j + 1

                        for b in range(nb):
                            for j in range(2):
                                hb = 64 * j
                                blk = (2 * g + j) % 8
                                Stile = S0 if j == 0 else S1
                                Yv = Y0 if j == 0 else Y1
                                pt = pts[j]
                                kcs = (2 * b, 2 * b + 1)

                                def mm_sc(eng, b=b, j=j, kcs=kcs,
                                          Stile=Stile, hb=hb, ql=ql,
                                          bias_n=bias_n, blk=blk, Yv=Yv,
                                          pt=pt, qt=qt):
                                    if b == 0:
                                        eng.wait_ge(se.sem, s_exp_cnt[j])
                                        eng.wait_ge(ve.sem, bias_n)
                                        rv, rs = yfm_cnt[j]
                                        if rv:
                                            eng.wait_ge(ve.sem, rv)
                                        if rs:
                                            eng.wait_ge(se.sem, rs)
                                    else:
                                        eng.wait_ge(se.sem,
                                                    pred_se(b - 1, j))
                                        if 2 * b - 1 >= 4 * qt:
                                            eng.wait_ge(po.sem,
                                                        pred_po(b - 1, j))
                                        for i_k, kc in enumerate(
                                                (2 * b - 2, 2 * b - 1)):
                                            eng.matmul(
                                                Yv[0:65, :],
                                                vt[:, kc, blk, :],
                                                pt[:, i_k * 512:
                                                   (i_k + 1) * 512],
                                                start=(kc == 0), stop=False)
                                    for i_k, kc in enumerate(kcs):
                                        i = eng.matmul(
                                            Stile[:, i_k * 512:
                                                  (i_k + 1) * 512],
                                            Kg[hb:hb + 64,
                                               kc * 128:(kc + 1) * 128],
                                            Qg[hb:hb + 64, ql:ql + 512],
                                            start=True, stop=True)
                                    te.inc(i)
                                on_te(mm_sc)
                                assert te.n == pred_te(b, j)

                        def expall(eng, nb=nb):
                            eng.wait_ge(ve.sem, bias_n)
                            for b in range(nb):
                                for j in range(2):
                                    eng.wait_ge(te.sem, pred_te(b, j))
                                    i = eng.activation(
                                        pts[j][:],
                                        S0[:] if j == 0 else S1[:],
                                        AF.Exp, bias=biasg[:, j:j + 1],
                                        scale=SCALE)
                                    se.inc(i)
                        on_se(expall)
                        assert se.n == pred_se(nb - 1, 1)
                        for j in range(2):
                            s_exp_cnt[j] = pred_se(nb - 1, j)

                        for b in (2 * qt, 2 * qt + 1):
                            for j in range(2):
                                pt = pts[j]
                                kcs = (2 * b, 2 * b + 1)

                                def dfix(eng, b=b, j=j, kcs=kcs, qt=qt,
                                         pt=pt):
                                    eng.wait_ge(se.sem, pred_se(b, j))
                                    for i_k, kc in enumerate(kcs):
                                        if kc < 4 * qt:
                                            continue
                                        off = i_k * 512
                                        vs = 128 * (kc - 4 * qt)
                                        if vs > 0:
                                            eng.memset(
                                                pt[:, off:off + vs], 0.0)
                                        i = eng.affine_select(
                                            out=pt[:, off + vs:
                                                   off + vs + 128],
                                            in_=pt[:, off + vs:
                                                   off + vs + 128],
                                            pattern=[[1, 128]],
                                            compare_op=ALU.is_ge,
                                            fill=0.0, base=0,
                                            channel_multiplier=-1)
                                    po.inc(i)
                                on_po(dfix)
                                assert po.n == pred_po(b, j), (po.n, b, j)

                        avn = {}
                        for j in range(2):
                            blk = (2 * g + j) % 8
                            Yv = Y0 if j == 0 else Y1
                            pt = pts[j]

                            def mm_av_last(eng, j=j, nb=nb, nk=nk, blk=blk,
                                           Yv=Yv, pt=pt):
                                eng.wait_ge(se.sem, pred_se(nb - 1, j))
                                eng.wait_ge(po.sem, pred_po(nb - 1, j))
                                for i_k, kc in enumerate((nk - 2, nk - 1)):
                                    i = eng.matmul(
                                        Yv[0:65, :], vt[:, kc, blk, :],
                                        pt[:, i_k * 512:(i_k + 1) * 512],
                                        start=(kc == 0), stop=(kc == nk - 1))
                                te.inc(i)
                            on_te(mm_av_last)
                            avn[j] = te.n

                        def nrm_ln(eng, avn=avn):
                            eng.wait_ge(te.sem, max(avn[0], avn[1]))
                            eng.activation(rza[:], Y0[64:65, :], AF.Ln)
                            se.inc(eng.activation(rzb_r[:], Y1[64:65, :],
                                                  AF.Ln))
                            eng.wait_ge(se.sem, se.n)
                            eng.activation(rza[:], rza[:], AF.Exp,
                                           scale=-1.0)
                            i = eng.activation(rzb_r[:], rzb_r[:], AF.Exp,
                                               scale=-1.0)
                            se.inc(i)
                        on_se(nrm_ln)
                        rz_se = se.n

                        for j in range(2):
                            hb = 64 * j
                            Yv = Y0 if j == 0 else Y1
                            Pz = P6 if j == 0 else P7
                            rzj = rza if j == 0 else rzb_r

                            def nrm_bc(eng, rz_se=rz_se, Pz=Pz, rzj=rzj):
                                eng.wait_ge(se.sem, rz_se)
                                te.inc(eng.matmul(Pz[0:64, :],
                                                  ones_r[:, 0:64], rzj[:],
                                                  start=True, stop=True))
                            on_te(nrm_bc)
                            nbn = te.n

                            def nrm_ysb(eng, j=j, Yv=Yv, avn=avn, hb=hb):
                                eng.wait_ge(te.sem, avn[j])
                                ve.inc(eng.tensor_copy(ysb[hb:hb + 64, :],
                                                       Yv[0:64, :]))
                            on_ve(nrm_ysb)
                            ysn = ve.n

                            def nrm_yfm(eng, g=g, hb=hb, ql=ql, nbn=nbn,
                                        ysn=ysn, Pz=Pz):
                                eng.wait_ge(te.sem, nbn)
                                eng.wait_ge(ve.sem, ysn)
                                i = eng.tensor_tensor(
                                    y[g][hb:hb + 64, ql:ql + 512],
                                    ysb[hb:hb + 64, :], Pz[0:64, :],
                                    ALU.mult)
                                ve.inc(i)
                            on_ve(nrm_yfm)
                            yfm_cnt[j] = (ve.n, 0)

                for g in range(4):
                    emit_group(g)
                emit_vhalf(1, xn2)
                for g in range(4, 8):
                    emit_group(g)
                tap("qg7", [Qg[:], Kg[:]], T, BF16)
                tap("biasg7", [biasg[:]], 2, F32)
                tap("y", [t[:] for t in y], T, BF16)
                tap("vt15", [vt[:, 15, :, :].rearrange("p a b -> p (a b)")], 8 * 65, BF16)

            with ExitStack() as a2ctx:
                wout = a2ctx.enter_context(
                    nc.sbuf_tensor("wout", [128, CC, C], BF16))

                def ld_wo(eng):
                    eng.wait_ge(te.sem, te.n)
                    eng.dma_start(
                        out=wout[:],
                        in_=wo_d.rearrange("(c p) m -> p c m", p=128),
                    ).then_inc(qg.sem, 16)
                on_po(ld_wo)
                qg.n += 16
                wo_n = qg.n

                obe = {0: ve.n, 1: ve.n}
                ose = se.n
                oq_ev = {}
                eq3 = emit_ln_mk(res_ready=lambda qt: oq_ev[qt],
                                 p67_floor=ve.n)
                for qt in range(NQ):
                    for fo in range(CC):
                        ql = qt * 512
                        obit = (qt * CC + fo) % 2
                        bankP = Y0 if obit == 0 else Y1

                        def mm_o(eng, fo=fo, ql=ql, bankP=bankP, wo_n=wo_n,
                                 bev=obe[obit], ose=ose):
                            eng.wait_ge(qg.sem, wo_n)
                            eng.wait_ge(ve.sem, bev)
                            eng.wait_ge(se.sem, ose)
                            for c in range(CC):
                                i = eng.matmul(
                                    bankP[:],
                                    wout[:, c, fo * 128:(fo + 1) * 128],
                                    y[c][:, ql:ql + 512],
                                    start=(c == 0), stop=(c == CC - 1))
                            te.inc(i)
                        on_te(mm_o)
                        on_ = te.n

                        def ev_o(eng, fo=fo, ql=ql, bankP=bankP, on_=on_):
                            eng.wait_ge(te.sem, on_)
                            i = eng.scalar_tensor_tensor(
                                resid[fo][:, ql:ql + 512], bankP[:],
                                bout[:, fo:fo + 1],
                                resid[fo][:, ql:ql + 512], ALU.add, ALU.add)
                            ve.inc(i)
                        on_ve(ev_o)
                        obe[obit] = ve.n
                    oq_ev[qt] = ve.n
                    xn3 = eq3(qt)

        tap("resid2", [r[:] for r in resid], T, F32)
        new_phase()
        with ExitStack() as f2ctx:
            def fsb(name, shape, dtype=BF16):
                return f2ctx.enter_context(nc.sbuf_tensor(name, shape, dtype))

            wbuf2 = []
            for s in range(2):
                wbuf2.append((fsb(f"w2gT{s}", [128, CC, DI]),
                              fsb(f"w2uT{s}", [128, CC, DI]),
                              fsb(f"w2dT{s}", [128, 4, C])))
            Gs2 = xn[2][:].rearrange("p (a b) -> p a b", a=4)
            Us2 = xn[3][:].rearrange("p (a b) -> p a b", a=4)
            xgb2 = [fsb("xgb20", [128, CC, 512]),
                    fsb("xgb21", [128, CC, 512])]
            Og2 = [x[:].rearrange("p c m -> p (c m)")
                   .rearrange("p (c m) -> p c m", c=4) for x in xgb2]
            stg2 = [fsb("stg_f20", [128, 4, 1024]),
                    fsb("stg_f21", [128, 4, 1024])]
            trload2 = make_trload(stg2)
            xse2 = [fsb("xse20", [128, 1024]), fsb("xse21", [128, 1024])]
            xst2 = ([stage[i][:].bitcast(BF16)[:, 0:1024] for i in range(2)]
                    + [x[:] for x in xse2])
            otm2 = xst2[0:2]

            pr2 = emit_routing(xn3)

            w_ready2 = {}

            def emit_load2(e_i, te_exp):
                s = e_i % 2

                def ld(eng, e_i=e_i, s=s, te_exp=te_exp):
                    eng.wait_ge(qa.sem, w4_all[0])
                    if e_i >= 2:
                        eng.wait_ge(te.sem, te_exp[e_i - 2])
                    eng.dma_start(
                        out=wbuf2[s][0][:],
                        in_=scr_gT[e_i].rearrange("(c p) m -> p c m", p=128),
                    ).then_inc(qs.sem, 16)
                    eng.dma_start(
                        out=wbuf2[s][1][:],
                        in_=scr_uT[e_i].rearrange("(c p) m -> p c m", p=128),
                    ).then_inc(qs.sem, 16)
                    eng.dma_start(
                        out=wbuf2[s][2][:],
                        in_=scr_dT[e_i].rearrange("(c p) m -> p c m", p=128),
                    ).then_inc(qs.sem, 16)
                on_qs(ld)
                qs.n += 48
                w_ready2[e_i] = (qs, qs.n)

            emit_load2(0, {})
            emit_load2(1, {})

            def load_next(e_i, te_exp):
                if e_i + 2 < NE:
                    emit_load2(e_i + 2, te_exp)

            st_n = {}

            NB4 = [Y0[:], Y1[:], P6[:], P7[:]]

            def out_tc(tc, resid_ve):
                for w in range(2):
                    def tro(e, tc=tc, resid_ve=resid_ve, w=w):
                        e.wait_ge(ve.sem, resid_ve if w == 0 else ve.n)
                        if w == 0 and tc >= 2:
                            e.wait_ge(qs.sem, st_n[tc - 2])
                        for ci in range(4):
                            c = 4 * w + ci
                            i = e.transpose(
                                NB4[ci][:, 0:128],
                                resid[c][:, tc * 128:(tc + 1) * 128],
                                idn_f[:])
                        te.inc(i)
                    on_te(tro)
                    tn = te.n

                    def evo(e, tn=tn, tc=tc, w=w):
                        e.wait_ge(te.sem, tn)
                        for ci in range(4):
                            c = 4 * w + ci
                            i = e.tensor_copy(
                                stage[tc % 2][:, c * 128:(c + 1) * 128],
                                NB4[ci][:, 0:128])
                        ve.inc(i)
                    on_ve(evo)
                en = ve.n

                def sto(e, tc=tc, en=en):
                    e.wait_ge(ve.sem, en)
                    e.dma_start(out=o_d[tc * 128:(tc + 1) * 128, :],
                                in_=stage[tc % 2][:]).then_inc(qs.sem, 16)
                on_qs(sto)
                qs.n += 16
                st_n[tc] = qs.n
                return qs.n

            emit_ffn_sparse(wbuf2, xgb2, Gs2, Us2, Og2, xst2, otm2, pr2,
                            w_ready2, trload2, after_expert=load_next,
                            after_tc=out_tc)

        def fin(e):
            e.wait_ge(qs.sem, qs.n)
        on_qs(fin)

    return nc


_CACHE = {}


def _tri168():
    t = np.zeros((128, 128), np.float32)
    for p in range(128):
        for q in range(128):
            if (p // 8 < q // 8) and (p % 8 == q % 8):
                t[p, q] = 1.0
    return t.astype(ml_dtypes.bfloat16)


def kernel(L0_attn_x, L0_attn_W_query, L0_attn_W_key, L0_attn_W_value,
           L0_attn_W_out, L0_attn_b_out, L0_ffn_gate_w, L0_ffn_w_gate,
           L0_ffn_w_up, L0_ffn_w_down):
    if "nc" not in _CACHE:
        _CACHE["nc"] = build()
    nc = _CACHE["nc"]

    x = np.ascontiguousarray(np.asarray(L0_attn_x, dtype=np.float32))
    com = dict(
        wq=np.ascontiguousarray(np.asarray(L0_attn_W_query, np.float32)),
        wk=np.ascontiguousarray(np.asarray(L0_attn_W_key, np.float32)),
        wv=np.ascontiguousarray(np.asarray(L0_attn_W_value, np.float32)),
        wo=np.ascontiguousarray(np.asarray(L0_attn_W_out, np.float32)),
        bo=np.ascontiguousarray(np.asarray(L0_attn_b_out, np.float32)),
        gw=np.ascontiguousarray(np.asarray(L0_ffn_gate_w, np.float32)),
        wgate=np.ascontiguousarray(np.asarray(L0_ffn_w_gate, np.float32)),
        wup=np.ascontiguousarray(np.asarray(L0_ffn_w_up, np.float32)),
        wdown=np.ascontiguousarray(np.asarray(L0_ffn_w_down, np.float32)),
        idnb=np.eye(128, dtype=ml_dtypes.bfloat16),
        idnf=np.eye(128, dtype=np.float32),
        trius=np.triu(np.ones((128, 128)), 1).astype(ml_dtypes.bfloat16),
        tri168=_tri168(),
        ecapc=(np.arange(128) % 8 * CAP).astype(np.float32).reshape(128, 1),
    )
    in_maps = [dict(com, x=np.ascontiguousarray(x[b])) for b in range(8)]
    res = run_bass_kernel_spmd(nc, in_maps, core_ids=list(range(8)))
    return np.stack([np.asarray(r["o"], dtype=np.float32)
                     for r in res.results], axis=0)


# version 39
# speedup vs baseline: 1.0238x; 1.0165x over previous
import numpy as np
import ml_dtypes
from contextlib import ExitStack

import concourse.bass as bass
import concourse.mybir as mybir
from concourse.bass_utils import run_bass_kernel_spmd
from concourse.bass_core_rust import IndirectOffsetOnAxis

F32 = mybir.dt.float32
BF16 = mybir.dt.bfloat16
AF = mybir.ActivationFunctionType
ALU = mybir.AluOpType
AX = mybir.AxisListType

T = 2048
C = 1024
NE = 8
DI = 512
CC = C // 128
TCH = T // 128
NQ = T // 512
SCALE = 0.125
EPS = 1e-5
CAP = 512


def build(debug=False):
    nc = bass.Bass("TRN2", target_bir_lowering=False)
    dbg = {}

    x_d = nc.dram_tensor("x", [T, C], F32, kind="ExternalInput").ap()
    wq_d = nc.dram_tensor("wq", [C, C], F32, kind="ExternalInput").ap()
    wk_d = nc.dram_tensor("wk", [C, C], F32, kind="ExternalInput").ap()
    wv_d = nc.dram_tensor("wv", [C, C], F32, kind="ExternalInput").ap()
    wo_d = nc.dram_tensor("wo", [C, C], F32, kind="ExternalInput").ap()
    bo_d = nc.dram_tensor("bo", [C], F32, kind="ExternalInput").ap()
    gw_d = nc.dram_tensor("gw", [NE, C], F32, kind="ExternalInput").ap()
    wg_d = nc.dram_tensor("wgate", [NE, DI, C], F32, kind="ExternalInput").ap()
    wu_d = nc.dram_tensor("wup", [NE, DI, C], F32, kind="ExternalInput").ap()
    wd_d = nc.dram_tensor("wdown", [NE, C, DI], F32, kind="ExternalInput").ap()
    idnb_d = nc.dram_tensor("idnb", [128, 128], BF16, kind="ExternalInput").ap()
    trius_d = nc.dram_tensor("trius", [128, 128], BF16, kind="ExternalInput").ap()
    tri168_d = nc.dram_tensor("tri168", [128, 128], BF16, kind="ExternalInput").ap()
    ecap_d = nc.dram_tensor("ecapc", [128, 1], F32, kind="ExternalInput").ap()
    idnf_d = nc.dram_tensor("idnf", [128, 128], F32, kind="ExternalInput").ap()
    o_d = nc.dram_tensor("o", [T, C], F32, kind="ExternalOutput").ap()

    scr_g = nc.dram_tensor("scr_g", [NE, DI, C], BF16, kind="Internal").ap()
    scr_u = nc.dram_tensor("scr_u", [NE, DI, C], BF16, kind="Internal").ap()
    scr_d = nc.dram_tensor("scr_d", [NE, C, DI], BF16, kind="Internal").ap()
    scr_gT = nc.dram_tensor("scr_gT", [NE, C, DI], BF16, kind="Internal").ap()
    scr_uT = nc.dram_tensor("scr_uT", [NE, C, DI], BF16, kind="Internal").ap()
    scr_dT = nc.dram_tensor("scr_dT", [NE, DI, C], BF16, kind="Internal").ap()
    scr_X = nc.dram_tensor("scr_X", [NE * CAP, C], BF16, kind="Internal").ap()
    scr_O = nc.dram_tensor("scr_O", [NE * CAP, C], BF16, kind="Internal").ap()

    ctx = ExitStack()
    with ctx:
        def sbuf(name, shape, dtype=F32):
            return ctx.enter_context(nc.sbuf_tensor(name, shape, dtype))

        def psum(name, shape, dtype=F32):
            return ctx.enter_context(nc.psum_tensor(name, shape, dtype))

        resid = [sbuf(f"res{c}", [128, T], F32) for c in range(CC)]
        xn = [sbuf(f"xn{c}", [128, T], BF16) for c in range(CC)]
        stage = [sbuf(f"stg{i}", [128, C], F32) for i in range(2)]
        idn_b = sbuf("idn_b", [128, 128], BF16)
        idn_f = sbuf("idn_f", [128, 128], F32)
        ones_c = sbuf("ones_c", [128, 1], BF16)
        ones_r = sbuf("ones_r", [1, 128], F32)
        e2g = sbuf("e2g", [128, 2], BF16)
        bout = sbuf("bout", [128, CC], F32)
        gwT = sbuf("gwT", [128, CC, NE], BF16)
        Lsb = sbuf("Lsb", [NE, 512], F32)
        Ltm = sbuf("Ltm", [128, TCH, NE], F32)
        Mtm = sbuf("Mtm", [128, TCH, NE], BF16)
        trius = sbuf("trius_sb", [128, 128], BF16)
        tri168 = sbuf("tri168_sb", [128, 128], BF16)
        ecap = sbuf("ecap_sb", [128, 1], F32)
        totr = sbuf("totr", [1, 128], BF16)
        totc = sbuf("totc", [128, 1], BF16)
        offc = sbuf("offc", [128, 1], F32)
        offrow = sbuf("offrow", [1, 128], F32)
        t8b = sbuf("t8b", [128, 128], F32)
        pos_f = sbuf("pos_f", [128, TCH], F32)
        pos_i = sbuf("pos_i", [128, TCH], mybir.dt.int32)
        topv = sbuf("topv", [128, TCH], F32)
        rwv = sbuf("rwv", [128, TCH], F32)
        rows = [sbuf(f"row{i}", [1, 512], F32) for i in range(3)]
        rows2 = [sbuf(f"rowb{i}", [1, 512], F32) for i in range(3)]
        tmpb = sbuf("tmpb", [128, 512], BF16)

        q4 = sbuf("q4", [128, 4], F32)
        t4m = sbuf("t4m", [4, 1], F32)
        z16 = sbuf("z16", [16, 16], F32)
        zrow = sbuf("zrow", [1, 16], F32)
        b2 = sbuf("b2", [1, 2], F32)
        epsr = sbuf("epsr", [1, 1], F32)
        biasg = sbuf("biasg", [128, 2], F32)

        S0 = psum("S0", [128, 1024], F32)
        S1 = psum("S1", [128, 1024], F32)
        Y0 = psum("Y0", [128, 512], F32)
        Y1 = psum("Y1", [128, 512], F32)
        P6 = psum("P6", [128, 512], F32)
        P7 = psum("P7", [128, 512], F32)
        S0h = [S0[:, 0:512], S0[:, 512:1024]]
        S1h = [S1[:, 0:512], S1[:, 512:1024]]
        GEN = [S0h[0], S0h[1], S1h[0], S1h[1], Y0[:], Y1[:], P6[:], P7[:]]

        class Ch:
            def __init__(self, name):
                self.sem = ctx.enter_context(nc.semaphore(name))
                self.n = 0

            def inc(self, instr, k=1):
                instr.then_inc(self.sem, k)
                self.n += k
                return self.n

        te = Ch("c_te")
        ve = Ch("c_ve")
        se = Ch("c_se")
        po = Ch("c_po")
        qs = Ch("c_qs")
        qg = Ch("c_qg")
        qa = Ch("c_qa")
        chans = {"te": te, "ve": ve, "se": se, "po": po, "qs": qs,
                 "qg": qg, "qa": qa}

        gwctx = ExitStack()
        gwst = gwctx.enter_context(nc.sbuf_tensor("gwst", [NE, C], BF16))
        block = ctx.enter_context(nc.Block())

        bar = {k: 0 for k in chans}
        waited = set()

        def new_phase():
            nonlocal bar, waited
            bar = {k: c.n for k, c in chans.items()}
            waited = set()

        def wbar(eng, me):
            if me in waited:
                return
            waited.add(me)
            for k, c in chans.items():
                if k != me and bar[k] > 0:
                    eng.wait_ge(c.sem, bar[k])

        def on_te(f):
            block.tensor(lambda e: (wbar(e, "te"), f(e)))

        def on_ve(f):
            block.vector(lambda e: (wbar(e, "ve"), f(e)))

        def on_se(f):
            block.scalar(lambda e: (wbar(e, "se"), f(e)))

        def on_po(f):
            block.gpsimd(lambda e: (wbar(e, "po"), f(e)))

        def on_qs(f):
            block.sync(lambda e: (wbar(e, "qs"), f(e)))

        def tap(name, tiles, free, dtype):
            if not debug:
                return
            part = tiles[0].shape[0]
            d = nc.dram_tensor(f"dbg_{name}", [len(tiles), part, free], dtype,
                               kind="ExternalOutput").ap()
            dbg[name] = d

            def t_(e, name=name, tiles=tiles, d=d):
                e.wait_ge(ve.sem, ve.n)
                e.wait_ge(te.sem, te.n)
                e.wait_ge(se.sem, se.n)
                e.wait_ge(po.sem, po.n)
                for c, t in enumerate(tiles):
                    e.dma_start(out=d[c], in_=t[:] if hasattr(t, "shape") is False else t).then_inc(qs.sem, 16)
            on_qs(t_)
            qs.n += 16 * len(tiles)

        def setup(e):
            e.dma_start(out=idn_b[:], in_=idnb_d).then_inc(qs.sem, 16)
            e.dma_start(out=idn_f[:], in_=idnf_d).then_inc(qs.sem, 16)
            e.dma_start(out=trius[:], in_=trius_d).then_inc(qs.sem, 16)
            e.dma_start(out=tri168[:], in_=tri168_d).then_inc(qs.sem, 16)
            e.dma_start(out=ecap[:], in_=ecap_d).then_inc(qs.sem, 16)
            e.dma_start(out=stage[1][0:NE, 0:128],
                        in_=bo_d.rearrange("(c p) -> c p", p=128)
                        ).then_inc(qs.sem, 16)
        on_qs(setup)
        qs.n += 16 * 6
        qs_setup = qs.n

        def setup_g(e):
            e.dma_start(out=gwst[:], in_=gw_d).then_inc(qg.sem, 16)
        on_po(setup_g)
        qg.n += 16
        qg_gw = qg.n

        def setup_v(e):
            e.memset(ones_c[:], 1.0)
            e.memset(ones_r[:], 1.0)
            ve.inc(e.memset(e2g[:], 0.0))
            e.wait_ge(ve.sem, ve.n)
            e.memset(e2g[0:64, 0:1], 1.0)
            e.memset(e2g[64:128, 1:2], 1.0)
            e.memset(z16[:], 0.0)
            ve.inc(e.memset(epsr[:], EPS))
        on_ve(setup_v)

        def setup_t(e):
            e.wait_ge(qs.sem, qs_setup)
            e.wait_ge(qg.sem, qg_gw)
            te.inc(e.transpose(P6[0:128, 0:NE], stage[1][0:NE, 0:128],
                               idn_f[0:NE, 0:NE]))
        on_te(setup_t)
        te_bout = te.n

        def setup_ev(e):
            e.wait_ge(te.sem, te_bout)
            ve.inc(e.tensor_copy(bout[:], P6[0:128, 0:NE]))
        on_ve(setup_ev)

        for fc in range(CC):
            def t_one(e, fc=fc):
                e.wait_ge(ve.sem, ve.n)
                te.inc(e.transpose(
                    P7[:, 0:128].bitcast(BF16)[:, 0:NE],
                    gwst[0:NE, fc * 128:(fc + 1) * 128],
                    idn_b[0:NE, 0:NE]))
            on_te(t_one)

            def v_one(e, fc=fc):
                e.wait_ge(te.sem, te.n)
                ve.inc(e.tensor_copy(gwT[:, fc, :],
                                     P7[:, 0:128].bitcast(BF16)[:, 0:NE]))
            on_ve(v_one)
        ve_gwT = ve.n
        gwctx.close()

        ve_x_evac = {}
        for tc in range(TCH):
            def ld(e, tc=tc):
                if tc == 1:
                    e.wait_ge(te.sem, te_bout)
                if tc >= 2:
                    e.wait_ge(ve.sem, ve_x_evac[tc - 2])
                e.dma_start(out=stage[tc % 2][:],
                            in_=x_d[tc * 128:(tc + 1) * 128, :]
                            ).then_inc(qs.sem, 16)
            on_qs(ld)
            qs.n += 16
            qn = qs.n

            def tr(e, tc=tc, qn=qn):
                e.wait_ge(qs.sem, qn)
                if tc == 0:
                    e.wait_ge(ve.sem, ve_gwT)
                else:
                    e.wait_ge(ve.sem, ve_x_evac[tc - 1])
                for c in range(CC):
                    i = e.transpose(GEN[c][:, 0:128],
                                    stage[tc % 2][:, c * 128:(c + 1) * 128],
                                    idn_f[:])
                te.inc(i)
            on_te(tr)
            tn = te.n

            def ev(e, tc=tc, tn=tn):
                e.wait_ge(te.sem, tn)
                for c in range(CC):
                    i = e.tensor_copy(resid[c][:, tc * 128:(tc + 1) * 128],
                                      GEN[c][:, 0:128])
                ve.inc(i)
            on_ve(ev)
            ve_x_evac[tc] = ve.n
        ve_x_done = ve.n
        tap("resid0", [r[:] for r in resid], T, F32)

        def emit_ln_mk(res_ready=None, p67_floor=None):
            te_xn_free = te.n
            ve_res = ve.n
            if res_ready is None:
                res_ready = lambda qt: ve_res
            row_ev = {}
            rowset = [(rows[0], rows[1], rows[2], rows[0]),
                      (rows2[0], rows2[1], rows2[2], rows2[0])]

            def emit_qt(qt):
                ql = qt * 512
                par = qt % 2
                Pst = P6 if par == 0 else P7
                Ba = S0h[0] if par == 0 else S1h[0]
                Bb = S0h[1] if par == 0 else S1h[1]
                r0, r1, r2, r3 = rowset[par]
                tmx = tmpb

                def sq(e, ql=ql, qt=qt):
                    e.wait_ge(ve.sem, res_ready(qt))
                    e.wait_ge(te.sem, te_xn_free)
                    for c in range(CC):
                        i = e.tensor_tensor(xn[c][:, ql:ql + 512],
                                            resid[c][:, ql:ql + 512],
                                            resid[c][:, ql:ql + 512],
                                            ALU.mult)
                    ve.inc(i)
                on_ve(sq)
                sq_n = ve.n

                def mm_ss(e, ql=ql, sq_n=sq_n, qt=qt, Pst=Pst):
                    e.wait_ge(ve.sem, sq_n)
                    if qt >= 2:
                        e.wait_ge(ve.sem, row_ev[qt - 2])
                    elif p67_floor == "res":
                        e.wait_ge(ve.sem, res_ready(qt))
                    else:
                        e.wait_ge(ve.sem,
                                  ve_res if p67_floor is None else p67_floor)
                    for c in range(CC):
                        i = e.matmul(Pst[0:1, 0:512], ones_c[:],
                                     xn[c][:, ql:ql + 512],
                                     start=(c == 0), stop=(c == CC - 1))
                    te.inc(i)
                on_te(mm_ss)
                t1n = te.n

                def ev_ss(e, t1n=t1n, Pst=Pst, r0=r0):
                    e.wait_ge(te.sem, t1n)
                    ve.inc(e.tensor_copy(r0[:], Pst[0:1, 0:512]))
                on_ve(ev_ss)

                def cast(e, ql=ql, t1n=t1n):
                    e.wait_ge(te.sem, t1n)
                    for c in range(CC):
                        i = e.tensor_copy(xn[c][:, ql:ql + 512],
                                          resid[c][:, ql:ql + 512])
                    ve.inc(i)
                on_ve(cast)
                cn = ve.n

                def mm_s(e, ql=ql, cn=cn, Pst=Pst):
                    e.wait_ge(ve.sem, cn)
                    for c in range(CC):
                        i = e.matmul(Pst[0:1, 0:512], ones_c[:],
                                     xn[c][:, ql:ql + 512],
                                     start=(c == 0), stop=(c == CC - 1))
                    te.inc(i)
                on_te(mm_s)
                t2n = te.n

                def rowmath(e, t2n=t2n, qt=qt, Pst=Pst, r0=r0, r1=r1,
                            r2=r2):
                    e.wait_ge(te.sem, t2n)
                    ve.inc(e.tensor_copy(r1[:], Pst[0:1, 0:512]))
                    row_ev[qt] = ve.n
                    e.wait_ge(ve.sem, ve.n)
                    ve.inc(e.tensor_tensor(r2[:], r1[:], r1[:], ALU.mult))
                    e.wait_ge(ve.sem, ve.n)
                    i = e.scalar_tensor_tensor(r0[:], r0[:], 1024.0,
                                               r2[:], ALU.mult,
                                               ALU.subtract)
                    ve.inc(i)
                on_ve(rowmath)
                rmn = ve.n

                def rowact(e, rmn=rmn, r0=r0, r2=r2):
                    e.wait_ge(ve.sem, rmn)
                    se.inc(e.activation(r2[:], r0[:], AF.Ln,
                                        bias=epsr[:],
                                        scale=1.0 / (1024.0 * 1024.0)))
                    e.wait_ge(se.sem, se.n)
                    i = e.activation(r2[:], r2[:], AF.Exp, scale=-0.5)
                    se.inc(i)
                on_se(rowact)
                ran = se.n

                def rowt3(e, ran=ran, r1=r1, r2=r2, r3=r3):
                    e.wait_ge(se.sem, ran)
                    ve.inc(e.tensor_tensor(r3[:], r1[:], r2[:], ALU.mult))
                on_ve(rowt3)
                t3n = ve.n

                def bcast(e, t3n=t3n, Ba=Ba, Bb=Bb, r2=r2, r3=r3):
                    e.wait_ge(ve.sem, t3n)
                    e.matmul(Ba, ones_r[:], r2[:], start=True, stop=True)
                    i = e.matmul(Bb, ones_r[:], r3[:], start=True, stop=True)
                    te.inc(i)
                on_te(bcast)
                bn = te.n

                def apply(e, ql=ql, bn=bn, Ba=Ba, Bb=Bb, tmx=tmx):
                    e.wait_ge(te.sem, bn)
                    for c in range(CC):
                        ve.inc(e.tensor_tensor(tmx[:],
                                               resid[c][:, ql:ql + 512],
                                               Ba, ALU.mult))
                        e.wait_ge(ve.sem, ve.n)
                        i = e.scalar_tensor_tensor(
                            xn[c][:, ql:ql + 512], Bb, -1.0 / 1024.0,
                            tmx[:], ALU.mult, ALU.add)
                        ve.inc(i)
                on_ve(apply)
                return ve.n
            return emit_qt

        def emit_ln(res_ready=None):
            eq = emit_ln_mk(res_ready)
            for qt in range(NQ):
                r = eq(qt)
            return r

        def make_trload(stgs):
            state = {"stg_free": {0: 0, 1: 0}, "trb": 0, "call": 0,
                     "bank_ev": {0: None, 1: None}}

            def trload(src_ap, nch, dst_slices, ld_wait_qg=None,
                       dst_wait_qs=None, on_g=False, floor_ve=None):
                sp = state["call"] % 2
                state["call"] += 1
                stg = stgs[sp]
                lq = qg if on_g else qs

                def ld(eng, src_ap=src_ap, nch=nch, sp=sp, stg=stg, lq=lq):
                    if ld_wait_qg is not None:
                        eng.wait_ge(qg.sem, ld_wait_qg)
                    eng.wait_ge(te.sem, state["stg_free"][sp])
                    eng.dma_start(
                        out=stg[:].rearrange("p c m -> p (c m)")
                        .rearrange("p (c m) -> p c m", c=nch),
                        in_=src_ap.rearrange("(c p) m -> p c m", p=128),
                    ).then_inc(lq.sem, 16)
                (on_po if on_g else on_qs)(ld)
                lq.n += 16
                ldn = lq.n
                inner = 4096 // nch
                flat = stg[:].rearrange("p c m -> p (c m)")
                floor = ve.n if floor_ve is None else floor_ve
                for dst_ap, blocks in dst_slices:
                    par = state["trb"] % 2
                    state["trb"] += 1
                    Pb = (P6 if par == 0 else P7)[:, 0:512].bitcast(BF16)

                    def trf(eng, blocks=blocks, Pb=Pb, ldn=ldn, inner=inner,
                            par=par, floor=floor, lq=lq):
                        eng.wait_ge(lq.sem, ldn)
                        bev = state["bank_ev"][par]
                        if bev:
                            eng.wait_ge(ve.sem, bev[1])
                        eng.wait_ge(ve.sem, floor)
                        for bi, (br, bc) in enumerate(blocks):
                            off = br * inner + bc * 128
                            i = eng.transpose(
                                Pb[:, 128 * bi:128 * (bi + 1)],
                                flat[:, off:off + 128], idn_b[:])
                        te.inc(i)
                    on_te(trf)
                    tn = te.n

                    def evf(eng, dst_ap=dst_ap, Pb=Pb, tn=tn,
                            nb=len(blocks), dw=dst_wait_qs, par=par):
                        eng.wait_ge(te.sem, tn)
                        if dw is not None:
                            eng.wait_ge(dw[0].sem, dw[1])
                        ve.inc(eng.tensor_copy(dst_ap[:, 0:128 * nb],
                                               Pb[:, 0:128 * nb]))
                        state["bank_ev"][par] = ("ve", ve.n)
                    on_ve(evf)
                state["stg_free"][sp] = te.n
            return trload

        def emit_routing(xn_ready):
            for qt in range(NQ):
                ql = qt * 512

                def mm_lg(e, ql=ql, xn_ready=xn_ready):
                    e.wait_ge(ve.sem, max(xn_ready, ve.n))
                    for c in range(CC):
                        i = e.matmul(S0h[0][0:NE, :], gwT[:, c, :],
                                     xn[c][:, ql:ql + 512],
                                     start=(c == 0), stop=(c == CC - 1))
                    te.inc(i)
                on_te(mm_lg)
                lgn = te.n

                def ev_lg(e, lgn=lgn):
                    e.wait_ge(te.sem, lgn)
                    ve.inc(e.tensor_copy(Lsb[:], S0h[0][0:NE, :]))
                on_ve(ev_lg)
                evn = ve.n

                def tr_l(e, qt=qt, evn=evn):
                    e.wait_ge(ve.sem, max(evn, ve.n))
                    for tci in range(4):
                        i = e.transpose(
                            S1h[0][:, 8 * tci:8 * tci + 8],
                            Lsb[0:NE, tci * 128:(tci + 1) * 128],
                            idn_f[0:NE, 0:NE])
                    te.inc(i)
                on_te(tr_l)
                trn = te.n

                def ev_l(e, qt=qt, trn=trn):
                    e.wait_ge(te.sem, trn)
                    ve.inc(e.tensor_copy(Ltm[:, 4 * qt:4 * qt + 4, :],
                                         S1h[0][:, 0:32]))
                on_ve(ev_l)
            lt_done = ve.n

            def stats(e, lt_done=lt_done):
                e.wait_ge(ve.sem, lt_done)
                ve.inc(e.reduce_max(topv[:], Ltm[:], axis=AX.X))
            on_ve(stats)
            tv_n = ve.n

            def sig(e, tv_n=tv_n):
                e.wait_ge(ve.sem, tv_n)
                se.inc(e.activation(rwv[:], topv[:], AF.Sigmoid))
            on_se(sig)
            sg_n = se.n

            def mask(e, sg_n=sg_n):
                e.wait_ge(se.sem, sg_n)
                for ei in range(NE):
                    i = e.tensor_tensor(Mtm[:, :, ei], Ltm[:, :, ei],
                                        topv[:], ALU.is_equal)
                ve.inc(i)
            on_ve(mask)
            mk_n = ve.n

            def mm_tot(e, mk_n=mk_n):
                e.wait_ge(ve.sem, max(mk_n, ve.n))
                for tc in range(TCH):
                    i = e.matmul(P7[0:1, 8 * tc:8 * tc + 8], ones_c[:],
                                 Mtm[:, tc, :], start=True, stop=True)
                te.inc(i)
            on_te(mm_tot)
            mtn = te.n

            def ev_tot(e, mtn=mtn):
                e.wait_ge(te.sem, mtn)
                ve.inc(e.tensor_copy(totr[:], P7[0:1, 0:128]))
            on_ve(ev_tot)
            etn = ve.n

            def tr_tot(e, etn=etn):
                e.wait_ge(ve.sem, etn)
                te.inc(e.transpose(P7[:, 128:256].bitcast(BF16)[:, 0:1],
                                   totr[:], idn_b[0:1, 0:1]))
            on_te(tr_tot)
            ttn = te.n

            def ev_totc(e, ttn=ttn):
                e.wait_ge(te.sem, ttn)
                ve.inc(e.tensor_copy(totc[:],
                                     P7[:, 128:256].bitcast(BF16)[:, 0:1]))
            on_ve(ev_totc)
            etc_n = ve.n

            def mm_off(e, etc_n=etc_n):
                e.wait_ge(ve.sem, etc_n)
                te.inc(e.matmul(P7[:, 256:257], tri168[:], totc[:],
                                start=True, stop=True))
            on_te(mm_off)
            mon = te.n

            def ev_off(e, mon=mon):
                e.wait_ge(te.sem, mon)
                ve.inc(e.tensor_tensor(offc[:], P7[:, 256:257], ecap[:],
                                       ALU.add))
            on_ve(ev_off)
            eon = ve.n

            def tr_off(e, eon=eon):
                e.wait_ge(ve.sem, eon)
                te.inc(e.transpose(P7[0:1, 384:512], offc[:], idn_f[:]))
            on_te(tr_off)
            ton = te.n

            def ev_offr(e, ton=ton):
                e.wait_ge(te.sem, ton)
                ve.inc(e.tensor_copy(offrow[:], P7[0:1, 384:512]))
            on_ve(ev_offr)
            orn = ve.n

            def mm_rank(e, orn=orn):
                e.wait_ge(ve.sem, max(orn, ve.n))
                for tc in range(TCH):
                    e.matmul(P6[:, 8 * tc:8 * tc + 8], trius[:],
                             Mtm[:, tc, :], start=True, stop=False)
                    i = e.matmul(P6[:, 8 * tc:8 * tc + 8], ones_r[:],
                                 offrow[:, 8 * tc:8 * tc + 8],
                                 start=False, stop=True)
                te.inc(i)
            on_te(mm_rank)
            rkn = te.n

            def ev_rank(e, rkn=rkn):
                e.wait_ge(te.sem, rkn)
                ve.inc(e.tensor_tensor(t8b[:], P6[:, 0:128],
                                       Mtm[:].rearrange("p a b -> p (a b)"),
                                       ALU.mult))
                e.wait_ge(ve.sem, ve.n)
                ve.inc(e.reduce_sum(
                    pos_f[:], t8b[:].rearrange("p (a b) -> p a b", b=8),
                    axis=AX.X))
                e.wait_ge(ve.sem, ve.n)
                ve.inc(e.tensor_copy(pos_i[:], pos_f[:]))
            on_ve(ev_rank)
            return ve.n

        def emit_ffn_sparse(wbuf, xgb, Gs, Us, Og, xst, otm, pos_ready,
                            w_ready, trload, after_expert=None,
                            after_tc=None, st_hook={}, xg_on_g=False):
            te_exp = {}
            ns = len(xst)
            ng = len(otm)
            SBH = [S0[:, 0:512], S0[:, 512:1024],
                   S1[:, 0:512], S1[:, 512:1024]]
            sc_done = {}
            sx_ev = {-1: ve.n}
            for tc in range(TCH):
                s = tc % ns
                xs = xst[s]

                Sb = SBH[(tc % ns) if ns == 4 else 2 * (tc % 2)]

                def tr_x(e, tc=tc, Sb=Sb):
                    e.wait_ge(ve.sem, sx_ev.get(tc - ns, sx_ev.get(-1, 0)))
                    if tc >= ns:
                        e.wait_ge(qg.sem, sc_done[tc - ns])
                    for c in range(CC):
                        i = e.transpose(
                            Sb.bitcast(BF16)[:, 128 * c:128 * (c + 1)],
                            xn[c][:, tc * 128:(tc + 1) * 128], idn_b[:])
                    te.inc(i)
                on_te(tr_x)
                txn = te.n

                def ev_x(e, tc=tc, txn=txn, xs=xs, Sb=Sb):
                    e.wait_ge(te.sem, txn)
                    i = e.tensor_scalar(
                        xs[:], Sb.bitcast(BF16)[:, 0:1024],
                        rwv[:, tc:tc + 1], None, ALU.mult)
                    ve.inc(i)
                on_ve(ev_x)
                exn = ve.n
                sx_ev[tc] = exn

                def sc_x(e, tc=tc, exn=exn, xs=xs, pos_ready=pos_ready):
                    e.wait_ge(ve.sem, max(exn, pos_ready))
                    e.indirect_dma_start(
                        out=scr_X,
                        out_offset=IndirectOffsetOnAxis(pos_i[:, tc:tc + 1], 0),
                        in_=xs[:], in_offset=None,
                    ).then_inc(qg.sem, 16)
                on_po(sc_x)
                qg.n += 16
                sc_done[tc] = qg.n
            scat_all = qg.n

            og_store = {}
            gu_free = 0
            for e_i in range(NE):
                s = e_i % 2
                wgT, wuT, wdT = wbuf[s]
                xg = xgb[s]
                og = Og[s]
                xgf = xg[:].rearrange("p c m -> p (c m)")

                trload(scr_X[e_i * CAP:(e_i + 1) * CAP, :], 4,
                       [(xgf[:, fc * 512:(fc + 2) * 512],
                         [(sc, f) for f in (fc, fc + 1) for sc in range(4)])
                        for fc in range(0, CC, 2)],
                       ld_wait_qg=scat_all,
                       dst_wait_qs=((qa, og_store[e_i - 2])
                                    if e_i >= 2 else None),
                       on_g=xg_on_g, floor_ve=pos_ready)
                xg_n = ve.n
                xg_se = se.n
                if after_expert is not None and e_i >= 1:
                    after_expert(e_i - 1, te_exp)

                wr = w_ready[e_i]
                if len(wr) == 4:
                    ch_w, n_w = wr[0], wr[1]
                    ch_w2, n_w2 = wr[2], wr[3]
                else:
                    ch_w, n_w = wr
                    ch_w2, n_w2 = None, 0
                SB4 = [S0h[0], S0h[1], S1h[0], S1h[1]]

                def mm_g(eng, n_w=n_w, wgT=wgT, ch_w=ch_w, xg=xg,
                         xg_n=xg_n, gf=gu_free, xg_se=xg_se,
                         ch_w2=ch_w2, n_w2=n_w2):
                    eng.wait_ge(ch_w.sem, n_w)
                    if ch_w2 is not None:
                        eng.wait_ge(ch_w2.sem, n_w2)
                    eng.wait_ge(se.sem, xg_se)
                    eng.wait_ge(ve.sem, max(xg_n, gf))
                    for ic in range(4):
                        for c in range(CC):
                            i = eng.matmul(
                                SB4[ic], wgT[:, c, ic * 128:(ic + 1) * 128],
                                xg[:, c, :],
                                start=(c == 0), stop=(c == CC - 1))
                    te.inc(i)
                on_te(mm_g)
                gn = te.n

                def ev_g(eng, gn=gn, Gs=Gs):
                    eng.wait_ge(te.sem, gn)
                    eng.tensor_copy(Gs.rearrange("p a b -> p (a b)")
                                    [:, 0:1024], S0[:])
                    i = eng.tensor_copy(Gs.rearrange("p a b -> p (a b)")
                                        [:, 1024:2048], S1[:])
                    ve.inc(i, 2)
                on_ve(ev_g)
                evg_n = ve.n

                def silu(eng, evg_n=evg_n, Gs=Gs):
                    eng.wait_ge(ve.sem, evg_n)
                    se.inc(eng.activation(
                        Gs.rearrange("p a b -> p (a b)"),
                        Gs.rearrange("p a b -> p (a b)"), AF.Silu))
                on_se(silu)
                sl_n = se.n

                def mm_u(eng, n_w=n_w, wuT=wuT, ch_w=ch_w, xg=xg,
                         evg_n=evg_n):
                    eng.wait_ge(ve.sem, evg_n)
                    for ic in range(4):
                        for c in range(CC):
                            i = eng.matmul(
                                SB4[ic], wuT[:, c, ic * 128:(ic + 1) * 128],
                                xg[:, c, :],
                                start=(c == 0), stop=(c == CC - 1))
                    te.inc(i)
                on_te(mm_u)
                un = te.n

                def ev_u(eng, un=un, Us=Us):
                    eng.wait_ge(te.sem, un)
                    eng.tensor_copy(Us.rearrange("p a b -> p (a b)")
                                    [:, 0:1024], S0[:])
                    i = eng.tensor_copy(Us.rearrange("p a b -> p (a b)")
                                        [:, 1024:2048], S1[:])
                    ve.inc(i, 2)
                on_ve(ev_u)
                gu_done = ve.n

                def hmul(eng, sl_n=sl_n, gu_done=gu_done, Gs=Gs, Us=Us):
                    eng.wait_ge(se.sem, sl_n)
                    eng.wait_ge(ve.sem, gu_done)
                    ve.inc(eng.tensor_tensor(
                        Us.rearrange("p a b -> p (a b)"),
                        Us.rearrange("p a b -> p (a b)"),
                        Gs.rearrange("p a b -> p (a b)"), ALU.mult))
                on_ve(hmul)
                h_n = ve.n

                DB = [Y0[:], Y1[:], S0h[0], S0h[1], S1h[0], S1h[1]]
                karr = [(sc4, hh) for sc4 in range(4) for hh in range(2)]

                def mm_d1(eng, h_n=h_n, wdT=wdT, Us=Us):
                    eng.wait_ge(ve.sem, h_n)
                    for k in range(6):
                        sc4, hh = karr[k]
                        for ic in range(4):
                            i = eng.matmul(
                                DB[k], Us[:, ic, sc4 * 128:(sc4 + 1) * 128],
                                wdT[:, ic, hh * 512:(hh + 1) * 512],
                                start=(ic == 0), stop=(ic == 3))
                    te.inc(i)
                on_te(mm_d1)
                d1n = te.n

                def ev_d1(eng, d1n=d1n, og=og, e_i=e_i):
                    eng.wait_ge(te.sem, d1n)
                    if e_i >= 1:
                        eng.wait_ge(qa.sem, og_store[e_i - 1])
                    for k in range(2):
                        sc4, hh = karr[k]
                        i = eng.tensor_copy(
                            og[:, sc4, hh * 512:(hh + 1) * 512], DB[k])
                    ve.inc(i)
                    for k in range(2, 6):
                        sc4, hh = karr[k]
                        i = eng.tensor_copy(
                            og[:, sc4, hh * 512:(hh + 1) * 512], DB[k])
                    ve.inc(i)
                on_ve(ev_d1)
                ed1y = ve.n - 1
                ed1 = ve.n

                def mm_d2(eng, ed1=ed1y, wdT=wdT, Us=Us):
                    eng.wait_ge(ve.sem, ed1)
                    for k in range(6, 8):
                        sc4, hh = karr[k]
                        for ic in range(4):
                            i = eng.matmul(
                                DB[k - 6],
                                Us[:, ic, sc4 * 128:(sc4 + 1) * 128],
                                wdT[:, ic, hh * 512:(hh + 1) * 512],
                                start=(ic == 0), stop=(ic == 3))
                    te.inc(i)
                on_te(mm_d2)
                d2n = te.n

                def ev_d2(eng, d2n=d2n, og=og):
                    eng.wait_ge(te.sem, d2n)
                    for k in range(6, 8):
                        sc4, hh = karr[k]
                        i = eng.tensor_copy(
                            og[:, sc4, hh * 512:(hh + 1) * 512], DB[k - 6])
                    ve.inc(i)
                on_ve(ev_d2)
                og_done = ve.n
                gu_free = og_done

                def st_og(eng, e_i=e_i, og=og, og_done=og_done):
                    eng.wait_ge(ve.sem, og_done)
                    eng.dma_start(
                        out=scr_O[e_i * CAP:(e_i + 1) * CAP, :]
                        .rearrange("(c p) m -> p c m", p=128),
                        in_=og[:]).then_inc(qa.sem, 16)
                on_se(st_og)
                qa.n += 16
                og_store[e_i] = qa.n

                te_exp[e_i] = te.n
            if after_expert is not None:
                after_expert(NE - 1, te_exp)
            all_og = qa.n

            ga_done = {}
            ot_used = {}
            ao_ev = {-1: ve.n}
            for tc in range(TCH):
                s = tc % ng
                ot = otm[s]

                def ga(eng, tc=tc, ot=ot, s=s):
                    eng.wait_ge(qa.sem, all_og)
                    if tc >= ng:
                        eng.wait_ge(te.sem, ot_used[tc - ng])
                    if after_tc is not None and (tc - 2) in st_hook:
                        eng.wait_ge(qs.sem, st_hook[tc - 2])
                    eng.indirect_dma_start(
                        out=ot[:], out_offset=None,
                        in_=scr_O,
                        in_offset=IndirectOffsetOnAxis(pos_i[:, tc:tc + 1],
                                                       0),
                    ).then_inc(qg.sem, 16)
                on_po(ga)
                qg.n += 16
                gn_ = qg.n

                Sb = SBH[(tc % ng) if ng == 4 else 2 * (tc % 2)]

                def tr_o(eng, tc=tc, gn_=gn_, ot=ot, Sb=Sb):
                    eng.wait_ge(qg.sem, gn_)
                    eng.wait_ge(ve.sem, ao_ev.get(tc - ng, ao_ev.get(-1, 0)))
                    for c in range(CC):
                        i = eng.transpose(
                            Sb.bitcast(BF16)
                            [:, 128 * c:128 * (c + 1)],
                            ot[:, c * 128:(c + 1) * 128], idn_b[:])
                    te.inc(i)
                on_te(tr_o)
                ot_used[tc] = te.n
                trn = te.n

                def add_o(eng, tc=tc, trn=trn, Sb=Sb):
                    eng.wait_ge(te.sem, trn)
                    for c in range(CC):
                        i = eng.scalar_tensor_tensor(
                            resid[c][:, tc * 128:(tc + 1) * 128],
                            Sb.bitcast(BF16)
                            [:, 128 * c:128 * (c + 1)],
                            0.5, resid[c][:, tc * 128:(tc + 1) * 128],
                            ALU.mult, ALU.add)
                    ve.inc(i)
                on_ve(add_o)
                ao_ev[tc] = ve.n
                if after_tc is not None:
                    st_hook[tc] = after_tc(tc, ve.n)
            return ve.n, te.n

        new_phase()
        with ExitStack() as pctx:
            def psb(name, shape, dtype=BF16):
                return pctx.enter_context(nc.sbuf_tensor(name, shape, dtype))

            wbuf = []
            for s in range(2):
                wbuf.append((psb(f"wgT{s}", [128, CC, DI]),
                             psb(f"wuT{s}", [128, CC, DI]),
                             psb(f"wdT{s}", [128, 4, C])))
            Gs = xn[2][:].rearrange("p (a b) -> p a b", a=4)
            Us = xn[3][:].rearrange("p (a b) -> p a b", a=4)
            xgb = [psb("xgb0", [128, CC, 512]), psb("xgb1", [128, CC, 512])]
            Og = [x[:].rearrange("p c m -> p (c m)")
                  .rearrange("p (c m) -> p c m", c=4) for x in xgb]
            stg = [psb("stg_p0", [128, 4, 1024]), psb("stg_p1", [128, 4, 1024])]
            trload = make_trload(stg)
            xse = [psb("xse0", [128, 1024]), psb("xse1", [128, 1024])]
            xst = ([stage[i][:].bitcast(BF16)[:, 0:1024] for i in range(2)]
                   + [x[:] for x in xse])
            otm = xst[0:2]

            w_ready1 = {}
            w4_done = {}
            w2_done = {}

            def emit_w12(e_i):
                def w12(eng, e_i=e_i):
                    eng.dma_start(out=scr_g[e_i], in_=wg_d[e_i]
                                  ).then_inc(qg.sem, 16)
                    eng.dma_start(out=scr_u[e_i], in_=wu_d[e_i]
                                  ).then_inc(qg.sem, 16)
                    eng.dma_start(out=scr_d[e_i], in_=wd_d[e_i]
                                  ).then_inc(qg.sem, 16)
                on_po(w12)
                qg.n += 48
                w2_done[e_i] = qg.n


            prep_floor = [None]

            def emit_prep(e_i, te_exp):
                s = e_i % 2
                wgT, wuT, wdT = wbuf[s]
                dw = (qa, w4_done[e_i - 2]) if e_i >= 2 else None
                for mat, dst in ((scr_g[e_i], wgT), (scr_u[e_i], wuT)):
                    dflat = dst[:].rearrange("p c m -> p (c m)")
                    trload(
                        mat, 4,
                        [(dflat[:, fc * 512:(fc + 2) * 512],
                          [(ic, f) for f in (fc, fc + 1) for ic in range(4)])
                         for fc in range(0, CC, 2)],
                        ld_wait_qg=w2_done[e_i], dst_wait_qs=dw,
                        floor_ve=prep_floor[0])
                trload(
                    scr_d[e_i], 8,
                    [(wdT[:, ic, :], [(hc, ic) for hc in range(CC)])
                     for ic in range(4)],
                    ld_wait_qg=w2_done[e_i], dst_wait_qs=dw,
                    floor_ve=prep_floor[0])
                ev_now = ve.n

                def st_T(eng, e_i=e_i, wgT=wgT, wuT=wuT, wdT=wdT,
                         ev_now=ev_now):
                    eng.wait_ge(ve.sem, ev_now)
                    eng.dma_start(
                        out=scr_gT[e_i].rearrange("(c p) m -> p c m", p=128),
                        in_=wgT[:]).then_inc(qa.sem, 16)
                    eng.dma_start(
                        out=scr_uT[e_i].rearrange("(c p) m -> p c m", p=128),
                        in_=wuT[:]).then_inc(qa.sem, 16)
                    eng.dma_start(
                        out=scr_dT[e_i].rearrange("(c p) m -> p c m", p=128),
                        in_=wdT[:]).then_inc(qa.sem, 16)
                on_se(st_T)
                qa.n += 48
                w4_done[e_i] = qa.n
                w_ready1[e_i] = (ve, ve.n, se, se.n)

            for e_i in range(NE):
                emit_w12(e_i)

            xn1 = emit_ln(res_ready=lambda qt: ve_x_evac[4 * qt + 3])
            tap("xn1", [t[:] for t in xn], T, BF16)
            pr1 = emit_routing(xn1)
            prep_floor[0] = pr1
            emit_prep(0, {})
            emit_prep(1, {})
            tap("mtm1", [Mtm[:]], TCH * NE, BF16)
            tap("rwv1", [rwv[:]], TCH, F32)
            tap("pos1", [pos_f[:]], TCH, F32)

            def prep_next(e_i, te_exp):
                if e_i + 2 < NE:
                    emit_prep(e_i + 2, te_exp)

            emit_ffn_sparse(wbuf, xgb, Gs, Us, Og, xst, otm, pr1,
                            w_ready1, trload, after_expert=prep_next)
            tap("resid1", [r[:] for r in resid], T, F32)
            tap("wgt0", [wbuf[0][0][:].rearrange("p c m -> p (c m)")], CC * DI, BF16)

        w4_all = [qa.n]
        new_phase()
        with ExitStack() as actx:
            y = [actx.enter_context(nc.sbuf_tensor(f"y{c}", [128, T], BF16))
                 for c in range(CC)]
            rza = actx.enter_context(nc.sbuf_tensor("rza", [1, 512], F32))
            rzb_r = actx.enter_context(nc.sbuf_tensor("rzb_r", [1, 512], F32))
            ysb = actx.enter_context(nc.sbuf_tensor("ysb", [128, 512], BF16))
            t4 = actx.enter_context(nc.sbuf_tensor("t4", [4, 128], F32))

            xn2 = emit_ln()

            with ExitStack() as a1ctx:
                def a1(name, shape, dtype=BF16):
                    return a1ctx.enter_context(
                        nc.sbuf_tensor(name, shape, dtype))

                vt = a1("vt", [128, TCH, 8, 65])
                tap("xn2", [t[:] for t in xn], T, BF16)
                wvsb = a1("wvsb", [128, CC, 512])
                wqk = [(a1(f"wqs{s}", [128, CC, 128]),
                        a1(f"wks{s}", [128, CC, 128])) for s in range(2)]
                Qg = a1("Qg", [128, T])
                Kg = a1("Kg", [128, T])
                pts = [a1(f"pts{i}", [128, 1024]) for i in range(2)]

                def vones(eng):
                    ve.inc(eng.memset(vt[:, :, :, 64:65], 1.0))
                on_ve(vones)

                def ld_qk0(eng):
                    eng.dma_start(
                        out=wqk[0][0][:],
                        in_=wq_d[:, 0:128].rearrange("(c p) m -> p c m",
                                                     p=128),
                    ).then_inc(qg.sem, 16)
                    eng.dma_start(
                        out=wqk[0][1][:],
                        in_=wk_d[:, 0:128].rearrange("(c p) m -> p c m",
                                                     p=128),
                    ).then_inc(qg.sem, 16)
                on_po(ld_qk0)
                qg.n += 32
                qk_ld = {0: qg.n}

                def emit_vhalf(half, xn_ready):
                    def ldv(eng, half=half):
                        eng.dma_start(
                            out=wvsb[:],
                            in_=wv_d[:, half * 512:(half + 1) * 512]
                            .rearrange("(c p) m -> p c m", p=128),
                        ).then_inc(qg.sem, 16)
                    on_po(ldv)
                    qg.n += 16
                    vl = qg.n
                    vfloor = ve.n
                    vbank_ev = {}
                    for tc in range(TCH):
                        def mmv(eng, tc=tc, vl=vl, xn_ready=xn_ready,
                                bev=vbank_ev.get(tc % 4, vfloor)):
                            eng.wait_ge(qg.sem, vl)
                            eng.wait_ge(ve.sem, max(xn_ready, bev))
                            eng.wait_ge(se.sem, se.n)
                            bank = GEN[tc % 4]
                            for c in range(CC):
                                i = eng.matmul(
                                    bank, xn[c][:, tc * 128:(tc + 1) * 128],
                                    wvsb[:, c, :],
                                    start=(c == 0), stop=(c == CC - 1))
                            te.inc(i)
                        on_te(mmv)
                        vn = te.n

                        def evv(eng, tc=tc, vn=vn):
                            eng.wait_ge(te.sem, vn)
                            bank = GEN[tc % 4]
                            ve.inc(eng.tensor_copy(
                                vt[:, tc, :, 0:64],
                                bank[:, 0:512].rearrange("p (h c) -> p h c",
                                                         h=8)))
                        on_ve(evv)
                        vbank_ev[tc % 4] = ve.n
                    return ve.n

                emit_vhalf(0, xn2)

                yfm_cnt = {0: (0, 0), 1: (0, 0)}
                s_exp_cnt = {0: 0, 1: 0}
                rzb_rd = {0: 0, 1: 0}

                def emit_group(g):
                    sset = g % 2
                    wqs, wks = wqk[sset]
                    if g + 1 < 8:
                        def ldn(eng, g=g):
                            eng.wait_ge(te.sem, te.n)
                            c0 = (g + 1) * 128
                            eng.dma_start(
                                out=wqk[(g + 1) % 2][0][:],
                                in_=wq_d[:, c0:c0 + 128]
                                .rearrange("(c p) m -> p c m", p=128),
                            ).then_inc(qg.sem, 16)
                            eng.dma_start(
                                out=wqk[(g + 1) % 2][1][:],
                                in_=wk_d[:, c0:c0 + 128]
                                .rearrange("(c p) m -> p c m", p=128),
                            ).then_inc(qg.sem, 16)
                        on_po(ldn)
                        qg.n += 32
                        qk_ld[g + 1] = qg.n

                    pbe = {0: ve.n, 1: ve.n}
                    for which, wsl, dst in ((0, wqs, Qg), (1, wks, Kg)):
                        for qt in range(NQ):
                            ql = qt * 512
                            bit = (which * NQ + qt) % 2
                            bankP = P6 if bit == 0 else P7

                            def mmp(eng, wsl=wsl, ql=ql, bankP=bankP, g=g,
                                    bev=pbe[bit]):
                                eng.wait_ge(qg.sem, qk_ld[g])
                                eng.wait_ge(ve.sem, bev)
                                for c in range(CC):
                                    i = eng.matmul(bankP[:], wsl[:, c, :],
                                                   xn[c][:, ql:ql + 512],
                                                   start=(c == 0),
                                                   stop=(c == CC - 1))
                                te.inc(i)
                            on_te(mmp)
                            pn = te.n

                            def evp(eng, dst=dst, ql=ql, bankP=bankP, pn=pn):
                                eng.wait_ge(te.sem, pn)
                                ve.inc(eng.tensor_copy(dst[:, ql:ql + 512],
                                                       bankP[:]))
                            on_ve(evp)
                            pbe[bit] = ve.n
                    qk_done = ve.n

                    def sq_q(eng, qk_done=qk_done):
                        eng.wait_ge(ve.sem, qk_done)
                        eng.wait_ge(se.sem, se.n)
                        eng.wait_ge(te.sem, te.n)
                        eng.tensor_tensor(pts[0][:], Qg[:, 0:1024],
                                          Qg[:, 0:1024], ALU.mult)
                        i = eng.tensor_tensor(pts[1][:], Qg[:, 1024:2048],
                                              Qg[:, 1024:2048], ALU.mult)
                        ve.inc(i)
                    on_ve(sq_q)
                    sqq_n = ve.n

                    def mm_qss(eng, sqq_n=sqq_n):
                        eng.wait_ge(ve.sem, sqq_n)
                        for tc in range(TCH):
                            src = pts[0] if tc < 8 else pts[1]
                            i = eng.matmul(
                                P7[:, 2 * tc:2 * tc + 2],
                                src[:, (tc % 8) * 128:(tc % 8 + 1) * 128],
                                e2g[:], start=True, stop=True)
                        te.inc(i)
                    on_te(mm_qss)
                    qssn = te.n

                    def sq_k(eng, qssn=qssn):
                        eng.wait_ge(te.sem, qssn)
                        eng.tensor_tensor(pts[0][:], Kg[:, 0:1024],
                                          Kg[:, 0:1024], ALU.mult)
                        i = eng.tensor_tensor(pts[1][:], Kg[:, 1024:2048],
                                              Kg[:, 1024:2048], ALU.mult)
                        ve.inc(i)
                    on_ve(sq_k)
                    sqk_n = ve.n

                    def mm_kss(eng, sqk_n=sqk_n):
                        eng.wait_ge(ve.sem, sqk_n)
                        for tc in range(TCH):
                            src = pts[0] if tc < 8 else pts[1]
                            i = eng.matmul(
                                P7[:, 32 + 2 * tc:32 + 2 * tc + 2],
                                src[:, (tc % 8) * 128:(tc % 8 + 1) * 128],
                                e2g[:], start=True, stop=True)
                        te.inc(i)
                    on_te(mm_kss)
                    kssn = te.n

                    def ev_qss(eng, kssn=kssn):
                        eng.wait_ge(te.sem, kssn)
                        eng.reduce_max(
                            q4[:, 0:2],
                            P7[:, 0:32].rearrange("p (t j) -> p j t", j=2),
                            axis=AX.X)
                        i = eng.reduce_max(
                            q4[:, 2:4],
                            P7[:, 32:64].rearrange("p (t j) -> p j t", j=2),
                            axis=AX.X)
                        ve.inc(i)
                    on_ve(ev_qss)
                    q4n = ve.n

                    def tr_q4(eng, q4n=q4n):
                        eng.wait_ge(ve.sem, q4n)
                        te.inc(eng.transpose(P7[0:4, 64:192], q4[:],
                                             idn_f[:]))
                    on_te(tr_q4)
                    t4n = te.n

                    def red_q4(eng, t4n=t4n):
                        eng.wait_ge(te.sem, t4n)
                        e = eng
                        ve.inc(e.tensor_copy(t4[:], P7[0:4, 64:192]))
                        e.wait_ge(ve.sem, ve.n)
                        ve.inc(e.reduce_max(t4m[:], t4[:], axis=AX.X))
                        e.wait_ge(ve.sem, ve.n)
                        ve.inc(e.tensor_copy(z16[0:4, 0:1], t4m[:]))
                    on_ve(red_q4)
                    z16n = ve.n

                    def tr_z(eng, z16n=z16n):
                        eng.wait_ge(ve.sem, z16n)
                        te.inc(eng.transpose(P7[0:16, 192:208], z16[:],
                                             idn_f[0:16, 0:16]))
                    on_te(tr_z)
                    tzn = te.n

                    def mk_b2(eng, tzn=tzn):
                        eng.wait_ge(te.sem, tzn)
                        ve.inc(eng.tensor_copy(zrow[:], P7[0:1, 192:208]))
                        eng.wait_ge(ve.sem, ve.n)
                        ve.inc(eng.tensor_tensor(b2[:], zrow[:, 0:2],
                                                 zrow[:, 2:4], ALU.add))
                    on_ve(mk_b2)
                    b2n = ve.n

                    def bc_b2(eng, b2n=b2n):
                        eng.wait_ge(ve.sem, b2n)
                        te.inc(eng.matmul(P7[:, 208:210], ones_r[:], b2[:],
                                          start=True, stop=True))
                    on_te(bc_b2)
                    bcn = te.n

                    def ev_b2(eng, bcn=bcn):
                        eng.wait_ge(te.sem, bcn)
                        eng.wait_ge(se.sem, max(s_exp_cnt[0],
                                                s_exp_cnt[1]))
                        ve.inc(eng.tensor_scalar(
                            biasg[:], P7[:, 208:210],
                            -0.5 * SCALE * 1.01, None, ALU.mult))
                    on_ve(ev_b2)
                    bias_n = ve.n

                    for qt in range(NQ - 1, -1, -1):
                        nk = 4 * qt + 4
                        nb = nk // 2
                        ql = qt * 512
                        T0 = te.n
                        se0 = se.n
                        po0 = po.n
                        pred_te = lambda b, j, T0=T0: T0 + 2 * b + j + 1
                        pred_se = lambda b, j, se0=se0: se0 + 2 * b + j + 1

                        def pred_po(b, j, qt=qt, po0=po0):
                            return po0 + (b - 2 * qt) * 2 + j + 1

                        for b in range(nb):
                            for j in range(2):
                                hb = 64 * j
                                blk = (2 * g + j) % 8
                                Stile = S0 if j == 0 else S1
                                Yv = Y0 if j == 0 else Y1
                                pt = pts[j]
                                kcs = (2 * b, 2 * b + 1)

                                def mm_sc(eng, b=b, j=j, kcs=kcs,
                                          Stile=Stile, hb=hb, ql=ql,
                                          bias_n=bias_n, blk=blk, Yv=Yv,
                                          pt=pt, qt=qt):
                                    if b == 0:
                                        eng.wait_ge(se.sem, s_exp_cnt[j])
                                        eng.wait_ge(ve.sem, bias_n)
                                        rv, rs = yfm_cnt[j]
                                        if rv:
                                            eng.wait_ge(ve.sem, rv)
                                        if rs:
                                            eng.wait_ge(se.sem, rs)
                                    else:
                                        eng.wait_ge(se.sem,
                                                    pred_se(b - 1, j))
                                        if 2 * b - 1 >= 4 * qt:
                                            eng.wait_ge(po.sem,
                                                        pred_po(b - 1, j))
                                        for i_k, kc in enumerate(
                                                (2 * b - 2, 2 * b - 1)):
                                            eng.matmul(
                                                Yv[0:65, :],
                                                vt[:, kc, blk, :],
                                                pt[:, i_k * 512:
                                                   (i_k + 1) * 512],
                                                start=(kc == 0), stop=False)
                                    for i_k, kc in enumerate(kcs):
                                        i = eng.matmul(
                                            Stile[:, i_k * 512:
                                                  (i_k + 1) * 512],
                                            Kg[hb:hb + 64,
                                               kc * 128:(kc + 1) * 128],
                                            Qg[hb:hb + 64, ql:ql + 512],
                                            start=True, stop=True)
                                    te.inc(i)
                                on_te(mm_sc)
                                assert te.n == pred_te(b, j)

                        def expall(eng, nb=nb):
                            eng.wait_ge(ve.sem, bias_n)
                            for b in range(nb):
                                for j in range(2):
                                    eng.wait_ge(te.sem, pred_te(b, j))
                                    i = eng.activation(
                                        pts[j][:],
                                        S0[:] if j == 0 else S1[:],
                                        AF.Exp, bias=biasg[:, j:j + 1],
                                        scale=SCALE)
                                    se.inc(i)
                        on_se(expall)
                        assert se.n == pred_se(nb - 1, 1)
                        for j in range(2):
                            s_exp_cnt[j] = pred_se(nb - 1, j)

                        for b in (2 * qt, 2 * qt + 1):
                            for j in range(2):
                                pt = pts[j]
                                kcs = (2 * b, 2 * b + 1)

                                def dfix(eng, b=b, j=j, kcs=kcs, qt=qt,
                                         pt=pt):
                                    eng.wait_ge(se.sem, pred_se(b, j))
                                    for i_k, kc in enumerate(kcs):
                                        if kc < 4 * qt:
                                            continue
                                        off = i_k * 512
                                        vs = 128 * (kc - 4 * qt)
                                        if vs > 0:
                                            eng.memset(
                                                pt[:, off:off + vs], 0.0)
                                        i = eng.affine_select(
                                            out=pt[:, off + vs:
                                                   off + vs + 128],
                                            in_=pt[:, off + vs:
                                                   off + vs + 128],
                                            pattern=[[1, 128]],
                                            compare_op=ALU.is_ge,
                                            fill=0.0, base=0,
                                            channel_multiplier=-1)
                                    po.inc(i)
                                on_po(dfix)
                                assert po.n == pred_po(b, j), (po.n, b, j)

                        avn = {}
                        for j in range(2):
                            blk = (2 * g + j) % 8
                            Yv = Y0 if j == 0 else Y1
                            pt = pts[j]

                            def mm_av_last(eng, j=j, nb=nb, nk=nk, blk=blk,
                                           Yv=Yv, pt=pt):
                                eng.wait_ge(se.sem, pred_se(nb - 1, j))
                                eng.wait_ge(po.sem, pred_po(nb - 1, j))
                                for i_k, kc in enumerate((nk - 2, nk - 1)):
                                    i = eng.matmul(
                                        Yv[0:65, :], vt[:, kc, blk, :],
                                        pt[:, i_k * 512:(i_k + 1) * 512],
                                        start=(kc == 0), stop=(kc == nk - 1))
                                te.inc(i)
                            on_te(mm_av_last)
                            avn[j] = te.n

                        def nrm_ln(eng, avn=avn):
                            eng.wait_ge(te.sem, max(avn[0], avn[1]))
                            eng.activation(rza[:], Y0[64:65, :], AF.Ln)
                            se.inc(eng.activation(rzb_r[:], Y1[64:65, :],
                                                  AF.Ln))
                            eng.wait_ge(se.sem, se.n)
                            eng.activation(rza[:], rza[:], AF.Exp,
                                           scale=-1.0)
                            i = eng.activation(rzb_r[:], rzb_r[:], AF.Exp,
                                               scale=-1.0)
                            se.inc(i)
                        on_se(nrm_ln)
                        rz_se = se.n

                        for j in range(2):
                            hb = 64 * j
                            Yv = Y0 if j == 0 else Y1
                            Pz = P6 if j == 0 else P7
                            rzj = rza if j == 0 else rzb_r

                            def nrm_bc(eng, rz_se=rz_se, Pz=Pz, rzj=rzj):
                                eng.wait_ge(se.sem, rz_se)
                                te.inc(eng.matmul(Pz[0:64, :],
                                                  ones_r[:, 0:64], rzj[:],
                                                  start=True, stop=True))
                            on_te(nrm_bc)
                            nbn = te.n

                            def nrm_ysb(eng, j=j, Yv=Yv, avn=avn, hb=hb):
                                eng.wait_ge(te.sem, avn[j])
                                ve.inc(eng.tensor_copy(ysb[hb:hb + 64, :],
                                                       Yv[0:64, :]))
                            on_ve(nrm_ysb)
                            ysn = ve.n

                            def nrm_yfm(eng, g=g, hb=hb, ql=ql, nbn=nbn,
                                        ysn=ysn, Pz=Pz):
                                eng.wait_ge(te.sem, nbn)
                                eng.wait_ge(ve.sem, ysn)
                                i = eng.tensor_tensor(
                                    y[g][hb:hb + 64, ql:ql + 512],
                                    ysb[hb:hb + 64, :], Pz[0:64, :],
                                    ALU.mult)
                                ve.inc(i)
                            on_ve(nrm_yfm)
                            yfm_cnt[j] = (ve.n, 0)

                for g in range(4):
                    emit_group(g)
                emit_vhalf(1, xn2)
                for g in range(4, 8):
                    emit_group(g)
                tap("qg7", [Qg[:], Kg[:]], T, BF16)
                tap("biasg7", [biasg[:]], 2, F32)
                tap("y", [t[:] for t in y], T, BF16)
                tap("vt15", [vt[:, 15, :, :].rearrange("p a b -> p (a b)")], 8 * 65, BF16)

            with ExitStack() as a2ctx:
                wout = a2ctx.enter_context(
                    nc.sbuf_tensor("wout", [128, CC, C], BF16))

                def ld_wo(eng):
                    eng.wait_ge(te.sem, te.n)
                    eng.dma_start(
                        out=wout[:],
                        in_=wo_d.rearrange("(c p) m -> p c m", p=128),
                    ).then_inc(qg.sem, 16)
                on_po(ld_wo)
                qg.n += 16
                wo_n = qg.n

                obe = {0: ve.n, 1: ve.n}
                ose = se.n
                oq_ev = {}
                eq3 = emit_ln_mk(res_ready=lambda qt: oq_ev[qt],
                                 p67_floor=ve.n)
                for qt in range(NQ):
                    for fo in range(CC):
                        ql = qt * 512
                        obit = (qt * CC + fo) % 2
                        bankP = Y0 if obit == 0 else Y1

                        def mm_o(eng, fo=fo, ql=ql, bankP=bankP, wo_n=wo_n,
                                 bev=obe[obit], ose=ose):
                            eng.wait_ge(qg.sem, wo_n)
                            eng.wait_ge(ve.sem, bev)
                            eng.wait_ge(se.sem, ose)
                            for c in range(CC):
                                i = eng.matmul(
                                    bankP[:],
                                    wout[:, c, fo * 128:(fo + 1) * 128],
                                    y[c][:, ql:ql + 512],
                                    start=(c == 0), stop=(c == CC - 1))
                            te.inc(i)
                        on_te(mm_o)
                        on_ = te.n

                        def ev_o(eng, fo=fo, ql=ql, bankP=bankP, on_=on_):
                            eng.wait_ge(te.sem, on_)
                            i = eng.scalar_tensor_tensor(
                                resid[fo][:, ql:ql + 512], bankP[:],
                                bout[:, fo:fo + 1],
                                resid[fo][:, ql:ql + 512], ALU.add, ALU.add)
                            ve.inc(i)
                        on_ve(ev_o)
                        obe[obit] = ve.n
                    oq_ev[qt] = ve.n
                    xn3 = eq3(qt)

        tap("resid2", [r[:] for r in resid], T, F32)
        new_phase()
        with ExitStack() as f2ctx:
            def fsb(name, shape, dtype=BF16):
                return f2ctx.enter_context(nc.sbuf_tensor(name, shape, dtype))

            wbuf2 = []
            for s in range(2):
                wbuf2.append((fsb(f"w2gT{s}", [128, CC, DI]),
                              fsb(f"w2uT{s}", [128, CC, DI]),
                              fsb(f"w2dT{s}", [128, 4, C])))
            Gs2 = xn[2][:].rearrange("p (a b) -> p a b", a=4)
            Us2 = xn[3][:].rearrange("p (a b) -> p a b", a=4)
            xgb2 = [fsb("xgb20", [128, CC, 512]),
                    fsb("xgb21", [128, CC, 512])]
            Og2 = [x[:].rearrange("p c m -> p (c m)")
                   .rearrange("p (c m) -> p c m", c=4) for x in xgb2]
            stg2 = [fsb("stg_f20", [128, 4, 1024]),
                    fsb("stg_f21", [128, 4, 1024])]
            trload2 = make_trload(stg2)
            xse2 = [fsb("xse20", [128, 1024]), fsb("xse21", [128, 1024])]
            xst2 = ([stage[i][:].bitcast(BF16)[:, 0:1024] for i in range(2)]
                    + [x[:] for x in xse2])
            otm2 = xst2[0:2]

            pr2 = emit_routing(xn3)

            w_ready2 = {}

            def emit_load2(e_i, te_exp):
                s = e_i % 2

                def ld(eng, e_i=e_i, s=s, te_exp=te_exp):
                    eng.wait_ge(qa.sem, w4_all[0])
                    if e_i >= 2:
                        eng.wait_ge(te.sem, te_exp[e_i - 2])
                    eng.dma_start(
                        out=wbuf2[s][0][:],
                        in_=scr_gT[e_i].rearrange("(c p) m -> p c m", p=128),
                    ).then_inc(qs.sem, 16)
                    eng.dma_start(
                        out=wbuf2[s][1][:],
                        in_=scr_uT[e_i].rearrange("(c p) m -> p c m", p=128),
                    ).then_inc(qs.sem, 16)
                    eng.dma_start(
                        out=wbuf2[s][2][:],
                        in_=scr_dT[e_i].rearrange("(c p) m -> p c m", p=128),
                    ).then_inc(qs.sem, 16)
                on_qs(ld)
                qs.n += 48
                w_ready2[e_i] = (qs, qs.n)

            emit_load2(0, {})
            emit_load2(1, {})

            def load_next(e_i, te_exp):
                if e_i + 2 < NE:
                    emit_load2(e_i + 2, te_exp)

            st_n = {}

            NB4 = [Y0[:], Y1[:], P6[:], P7[:]]

            def out_tc(tc, resid_ve):
                for w in range(2):
                    def tro(e, tc=tc, resid_ve=resid_ve, w=w):
                        e.wait_ge(ve.sem, resid_ve if w == 0 else ve.n)
                        if w == 0 and tc >= 2:
                            e.wait_ge(qs.sem, st_n[tc - 2])
                        for ci in range(4):
                            c = 4 * w + ci
                            i = e.transpose(
                                NB4[ci][:, 0:128],
                                resid[c][:, tc * 128:(tc + 1) * 128],
                                idn_f[:])
                        te.inc(i)
                    on_te(tro)
                    tn = te.n

                    def evo(e, tn=tn, tc=tc, w=w):
                        e.wait_ge(te.sem, tn)
                        for ci in range(4):
                            c = 4 * w + ci
                            i = e.tensor_copy(
                                stage[tc % 2][:, c * 128:(c + 1) * 128],
                                NB4[ci][:, 0:128])
                        ve.inc(i)
                    on_ve(evo)
                en = ve.n

                def sto(e, tc=tc, en=en):
                    e.wait_ge(ve.sem, en)
                    e.dma_start(out=o_d[tc * 128:(tc + 1) * 128, :],
                                in_=stage[tc % 2][:]).then_inc(qs.sem, 16)
                on_qs(sto)
                qs.n += 16
                st_n[tc] = qs.n
                return qs.n

            emit_ffn_sparse(wbuf2, xgb2, Gs2, Us2, Og2, xst2, otm2, pr2,
                            w_ready2, trload2, after_expert=load_next,
                            after_tc=out_tc)

        def fin(e):
            e.wait_ge(qs.sem, qs.n)
        on_qs(fin)

    return nc


_CACHE = {}


def _tri168():
    t = np.zeros((128, 128), np.float32)
    for p in range(128):
        for q in range(128):
            if (p // 8 < q // 8) and (p % 8 == q % 8):
                t[p, q] = 1.0
    return t.astype(ml_dtypes.bfloat16)


def kernel(L0_attn_x, L0_attn_W_query, L0_attn_W_key, L0_attn_W_value,
           L0_attn_W_out, L0_attn_b_out, L0_ffn_gate_w, L0_ffn_w_gate,
           L0_ffn_w_up, L0_ffn_w_down):
    if "nc" not in _CACHE:
        _CACHE["nc"] = build()
    nc = _CACHE["nc"]

    x = np.ascontiguousarray(np.asarray(L0_attn_x, dtype=np.float32))
    com = dict(
        wq=np.ascontiguousarray(np.asarray(L0_attn_W_query, np.float32)),
        wk=np.ascontiguousarray(np.asarray(L0_attn_W_key, np.float32)),
        wv=np.ascontiguousarray(np.asarray(L0_attn_W_value, np.float32)),
        wo=np.ascontiguousarray(np.asarray(L0_attn_W_out, np.float32)),
        bo=np.ascontiguousarray(np.asarray(L0_attn_b_out, np.float32)),
        gw=np.ascontiguousarray(np.asarray(L0_ffn_gate_w, np.float32)),
        wgate=np.ascontiguousarray(np.asarray(L0_ffn_w_gate, np.float32)),
        wup=np.ascontiguousarray(np.asarray(L0_ffn_w_up, np.float32)),
        wdown=np.ascontiguousarray(np.asarray(L0_ffn_w_down, np.float32)),
        idnb=np.eye(128, dtype=ml_dtypes.bfloat16),
        idnf=np.eye(128, dtype=np.float32),
        trius=np.triu(np.ones((128, 128)), 1).astype(ml_dtypes.bfloat16),
        tri168=_tri168(),
        ecapc=(np.arange(128) % 8 * CAP).astype(np.float32).reshape(128, 1),
    )
    in_maps = [dict(com, x=np.ascontiguousarray(x[b])) for b in range(8)]
    res = run_bass_kernel_spmd(nc, in_maps, core_ids=list(range(8)))
    return np.stack([np.asarray(r["o"], dtype=np.float32)
                     for r in res.results], axis=0)


# version 40
# speedup vs baseline: 1.0246x; 1.0008x over previous
import numpy as np
import ml_dtypes
from contextlib import ExitStack

import concourse.bass as bass
import concourse.mybir as mybir
from concourse.bass_utils import run_bass_kernel_spmd
from concourse.bass_core_rust import IndirectOffsetOnAxis

F32 = mybir.dt.float32
BF16 = mybir.dt.bfloat16
AF = mybir.ActivationFunctionType
ALU = mybir.AluOpType
AX = mybir.AxisListType

T = 2048
C = 1024
NE = 8
DI = 512
CC = C // 128
TCH = T // 128
NQ = T // 512
SCALE = 0.125
EPS = 1e-5
CAP = 512


def build(debug=False):
    nc = bass.Bass("TRN2", target_bir_lowering=False)
    dbg = {}

    x_d = nc.dram_tensor("x", [T, C], F32, kind="ExternalInput").ap()
    wq_d = nc.dram_tensor("wq", [C, C], F32, kind="ExternalInput").ap()
    wk_d = nc.dram_tensor("wk", [C, C], F32, kind="ExternalInput").ap()
    wv_d = nc.dram_tensor("wv", [C, C], F32, kind="ExternalInput").ap()
    wo_d = nc.dram_tensor("wo", [C, C], F32, kind="ExternalInput").ap()
    bo_d = nc.dram_tensor("bo", [C], F32, kind="ExternalInput").ap()
    gw_d = nc.dram_tensor("gw", [NE, C], F32, kind="ExternalInput").ap()
    wg_d = nc.dram_tensor("wgate", [NE, DI, C], F32, kind="ExternalInput").ap()
    wu_d = nc.dram_tensor("wup", [NE, DI, C], F32, kind="ExternalInput").ap()
    wd_d = nc.dram_tensor("wdown", [NE, C, DI], F32, kind="ExternalInput").ap()
    idnb_d = nc.dram_tensor("idnb", [128, 128], BF16, kind="ExternalInput").ap()
    trius_d = nc.dram_tensor("trius", [128, 128], BF16, kind="ExternalInput").ap()
    tri168_d = nc.dram_tensor("tri168", [128, 128], BF16, kind="ExternalInput").ap()
    ecap_d = nc.dram_tensor("ecapc", [128, 1], F32, kind="ExternalInput").ap()
    idnf_d = nc.dram_tensor("idnf", [128, 128], F32, kind="ExternalInput").ap()
    o_d = nc.dram_tensor("o", [T, C], F32, kind="ExternalOutput").ap()

    scr_g = nc.dram_tensor("scr_g", [NE, DI, C], BF16, kind="Internal").ap()
    scr_u = nc.dram_tensor("scr_u", [NE, DI, C], BF16, kind="Internal").ap()
    scr_d = nc.dram_tensor("scr_d", [NE, C, DI], BF16, kind="Internal").ap()
    scr_gT = nc.dram_tensor("scr_gT", [NE, C, DI], BF16, kind="Internal").ap()
    scr_uT = nc.dram_tensor("scr_uT", [NE, C, DI], BF16, kind="Internal").ap()
    scr_dT = nc.dram_tensor("scr_dT", [NE, DI, C], BF16, kind="Internal").ap()
    scr_X = nc.dram_tensor("scr_X", [NE * CAP, C], BF16, kind="Internal").ap()
    scr_O = nc.dram_tensor("scr_O", [NE * CAP, C], BF16, kind="Internal").ap()

    ctx = ExitStack()
    with ctx:
        def sbuf(name, shape, dtype=F32):
            return ctx.enter_context(nc.sbuf_tensor(name, shape, dtype))

        def psum(name, shape, dtype=F32):
            return ctx.enter_context(nc.psum_tensor(name, shape, dtype))

        resid = [sbuf(f"res{c}", [128, T], F32) for c in range(CC)]
        xn = [sbuf(f"xn{c}", [128, T], BF16) for c in range(CC)]
        stage = [sbuf(f"stg{i}", [128, C], F32) for i in range(2)]
        idn_b = sbuf("idn_b", [128, 128], BF16)
        idn_f = sbuf("idn_f", [128, 128], F32)
        ones_c = sbuf("ones_c", [128, 1], BF16)
        ones_r = sbuf("ones_r", [1, 128], F32)
        e2g = sbuf("e2g", [128, 2], BF16)
        bout = sbuf("bout", [128, CC], F32)
        gwT = sbuf("gwT", [128, CC, NE], BF16)
        Lsb = sbuf("Lsb", [NE, 512], F32)
        Ltm = sbuf("Ltm", [128, TCH, NE], F32)
        Mtm = sbuf("Mtm", [128, TCH, NE], BF16)
        trius = sbuf("trius_sb", [128, 128], BF16)
        tri168 = sbuf("tri168_sb", [128, 128], BF16)
        ecap = sbuf("ecap_sb", [128, 1], F32)
        totr = sbuf("totr", [1, 128], BF16)
        totc = sbuf("totc", [128, 1], BF16)
        offc = sbuf("offc", [128, 1], F32)
        offrow = sbuf("offrow", [1, 128], F32)
        t8b = sbuf("t8b", [128, 128], F32)
        pos_f = sbuf("pos_f", [128, TCH], F32)
        pos_i = sbuf("pos_i", [128, TCH], mybir.dt.int32)
        topv = sbuf("topv", [128, TCH], F32)
        rwv = sbuf("rwv", [128, TCH], F32)
        rows = [sbuf(f"row{i}", [1, 512], F32) for i in range(3)]
        rows2 = [sbuf(f"rowb{i}", [1, 512], F32) for i in range(3)]
        tmpb = sbuf("tmpb", [128, 512], BF16)

        q4 = sbuf("q4", [128, 4], F32)
        t4m = sbuf("t4m", [4, 1], F32)
        z16 = sbuf("z16", [16, 16], F32)
        zrow = sbuf("zrow", [1, 16], F32)
        b2 = sbuf("b2", [1, 2], F32)
        epsr = sbuf("epsr", [1, 1], F32)
        biasg = sbuf("biasg", [128, 2], F32)

        S0 = psum("S0", [128, 1024], F32)
        S1 = psum("S1", [128, 1024], F32)
        Y0 = psum("Y0", [128, 512], F32)
        Y1 = psum("Y1", [128, 512], F32)
        P6 = psum("P6", [128, 512], F32)
        P7 = psum("P7", [128, 512], F32)
        S0h = [S0[:, 0:512], S0[:, 512:1024]]
        S1h = [S1[:, 0:512], S1[:, 512:1024]]
        GEN = [S0h[0], S0h[1], S1h[0], S1h[1], Y0[:], Y1[:], P6[:], P7[:]]

        class Ch:
            def __init__(self, name):
                self.sem = ctx.enter_context(nc.semaphore(name))
                self.n = 0

            def inc(self, instr, k=1):
                instr.then_inc(self.sem, k)
                self.n += k
                return self.n

        te = Ch("c_te")
        ve = Ch("c_ve")
        se = Ch("c_se")
        po = Ch("c_po")
        qs = Ch("c_qs")
        qg = Ch("c_qg")
        qa = Ch("c_qa")
        chans = {"te": te, "ve": ve, "se": se, "po": po, "qs": qs,
                 "qg": qg, "qa": qa}

        gwctx = ExitStack()
        gwst = gwctx.enter_context(nc.sbuf_tensor("gwst", [NE, C], BF16))
        block = ctx.enter_context(nc.Block())

        bar = {k: 0 for k in chans}
        waited = set()

        def new_phase():
            nonlocal bar, waited
            bar = {k: c.n for k, c in chans.items()}
            waited = set()

        def wbar(eng, me):
            if me in waited:
                return
            waited.add(me)
            for k, c in chans.items():
                if k != me and bar[k] > 0:
                    eng.wait_ge(c.sem, bar[k])

        def on_te(f):
            block.tensor(lambda e: (wbar(e, "te"), f(e)))

        def on_ve(f):
            block.vector(lambda e: (wbar(e, "ve"), f(e)))

        def on_se(f):
            block.scalar(lambda e: (wbar(e, "se"), f(e)))

        def on_po(f):
            block.gpsimd(lambda e: (wbar(e, "po"), f(e)))

        def on_qs(f):
            block.sync(lambda e: (wbar(e, "qs"), f(e)))

        def tap(name, tiles, free, dtype):
            if not debug:
                return
            part = tiles[0].shape[0]
            d = nc.dram_tensor(f"dbg_{name}", [len(tiles), part, free], dtype,
                               kind="ExternalOutput").ap()
            dbg[name] = d

            def t_(e, name=name, tiles=tiles, d=d):
                e.wait_ge(ve.sem, ve.n)
                e.wait_ge(te.sem, te.n)
                e.wait_ge(se.sem, se.n)
                e.wait_ge(po.sem, po.n)
                for c, t in enumerate(tiles):
                    e.dma_start(out=d[c], in_=t[:] if hasattr(t, "shape") is False else t).then_inc(qs.sem, 16)
            on_qs(t_)
            qs.n += 16 * len(tiles)

        def setup(e):
            e.dma_start(out=idn_b[:], in_=idnb_d).then_inc(qs.sem, 16)
            e.dma_start(out=idn_f[:], in_=idnf_d).then_inc(qs.sem, 16)
            e.dma_start(out=trius[:], in_=trius_d).then_inc(qs.sem, 16)
            e.dma_start(out=tri168[:], in_=tri168_d).then_inc(qs.sem, 16)
            e.dma_start(out=ecap[:], in_=ecap_d).then_inc(qs.sem, 16)
            e.dma_start(out=stage[1][0:NE, 0:128],
                        in_=bo_d.rearrange("(c p) -> c p", p=128)
                        ).then_inc(qs.sem, 16)
        on_qs(setup)
        qs.n += 16 * 6
        qs_setup = qs.n

        def setup_g(e):
            e.dma_start(out=gwst[:], in_=gw_d).then_inc(qg.sem, 16)
        on_po(setup_g)
        qg.n += 16
        qg_gw = qg.n

        def setup_v(e):
            e.memset(ones_c[:], 1.0)
            e.memset(ones_r[:], 1.0)
            ve.inc(e.memset(e2g[:], 0.0))
            e.wait_ge(ve.sem, ve.n)
            e.memset(e2g[0:64, 0:1], 1.0)
            e.memset(e2g[64:128, 1:2], 1.0)
            e.memset(z16[:], 0.0)
            ve.inc(e.memset(epsr[:], EPS))
        on_ve(setup_v)

        def setup_t(e):
            e.wait_ge(qs.sem, qs_setup)
            e.wait_ge(qg.sem, qg_gw)
            te.inc(e.transpose(P6[0:128, 0:NE], stage[1][0:NE, 0:128],
                               idn_f[0:NE, 0:NE]))
        on_te(setup_t)
        te_bout = te.n

        def setup_ev(e):
            e.wait_ge(te.sem, te_bout)
            ve.inc(e.tensor_copy(bout[:], P6[0:128, 0:NE]))
        on_ve(setup_ev)

        for fc in range(CC):
            def t_one(e, fc=fc):
                e.wait_ge(ve.sem, ve.n)
                te.inc(e.transpose(
                    P7[:, 0:128].bitcast(BF16)[:, 0:NE],
                    gwst[0:NE, fc * 128:(fc + 1) * 128],
                    idn_b[0:NE, 0:NE]))
            on_te(t_one)

            def v_one(e, fc=fc):
                e.wait_ge(te.sem, te.n)
                ve.inc(e.tensor_copy(gwT[:, fc, :],
                                     P7[:, 0:128].bitcast(BF16)[:, 0:NE]))
            on_ve(v_one)
        ve_gwT = ve.n
        gwctx.close()

        ve_x_evac = {}
        for tc in range(TCH):
            def ld(e, tc=tc):
                if tc == 1:
                    e.wait_ge(te.sem, te_bout)
                if tc >= 2:
                    e.wait_ge(ve.sem, ve_x_evac[tc - 2])
                e.dma_start(out=stage[tc % 2][:],
                            in_=x_d[tc * 128:(tc + 1) * 128, :]
                            ).then_inc(qs.sem, 16)
            on_qs(ld)
            qs.n += 16
            qn = qs.n

            def tr(e, tc=tc, qn=qn):
                e.wait_ge(qs.sem, qn)
                if tc == 0:
                    e.wait_ge(ve.sem, ve_gwT)
                else:
                    e.wait_ge(ve.sem, ve_x_evac[tc - 1])
                for c in range(CC):
                    i = e.transpose(GEN[c][:, 0:128],
                                    stage[tc % 2][:, c * 128:(c + 1) * 128],
                                    idn_f[:])
                te.inc(i)
            on_te(tr)
            tn = te.n

            def ev(e, tc=tc, tn=tn):
                e.wait_ge(te.sem, tn)
                for c in range(CC):
                    i = e.tensor_copy(resid[c][:, tc * 128:(tc + 1) * 128],
                                      GEN[c][:, 0:128])
                ve.inc(i)
            on_ve(ev)
            ve_x_evac[tc] = ve.n
        ve_x_done = ve.n
        tap("resid0", [r[:] for r in resid], T, F32)

        def emit_ln_mk(res_ready=None, p67_floor=None):
            te_xn_free = te.n
            ve_res = ve.n
            if res_ready is None:
                res_ready = lambda qt: ve_res
            row_ev = {}
            rowset = [(rows[0], rows[1], rows[2], rows[0]),
                      (rows2[0], rows2[1], rows2[2], rows2[0])]

            def emit_qt(qt):
                ql = qt * 512
                par = qt % 2
                Pst = P6 if par == 0 else P7
                Ba = S0h[0] if par == 0 else S1h[0]
                Bb = S0h[1] if par == 0 else S1h[1]
                r0, r1, r2, r3 = rowset[par]
                tmx = tmpb

                def sq(e, ql=ql, qt=qt):
                    e.wait_ge(ve.sem, res_ready(qt))
                    e.wait_ge(te.sem, te_xn_free)
                    for c in range(CC):
                        i = e.tensor_tensor(xn[c][:, ql:ql + 512],
                                            resid[c][:, ql:ql + 512],
                                            resid[c][:, ql:ql + 512],
                                            ALU.mult)
                    ve.inc(i)
                on_ve(sq)
                sq_n = ve.n

                def mm_ss(e, ql=ql, sq_n=sq_n, qt=qt, Pst=Pst):
                    e.wait_ge(ve.sem, sq_n)
                    if qt >= 2:
                        e.wait_ge(ve.sem, row_ev[qt - 2])
                    elif p67_floor == "res":
                        e.wait_ge(ve.sem, res_ready(qt))
                    else:
                        e.wait_ge(ve.sem,
                                  ve_res if p67_floor is None else p67_floor)
                    for c in range(CC):
                        i = e.matmul(Pst[0:1, 0:512], ones_c[:],
                                     xn[c][:, ql:ql + 512],
                                     start=(c == 0), stop=(c == CC - 1))
                    te.inc(i)
                on_te(mm_ss)
                t1n = te.n

                def ev_ss(e, t1n=t1n, Pst=Pst, r0=r0):
                    e.wait_ge(te.sem, t1n)
                    ve.inc(e.tensor_copy(r0[:], Pst[0:1, 0:512]))
                on_ve(ev_ss)

                def cast(e, ql=ql, t1n=t1n):
                    e.wait_ge(te.sem, t1n)
                    for c in range(CC):
                        i = e.tensor_copy(xn[c][:, ql:ql + 512],
                                          resid[c][:, ql:ql + 512])
                    ve.inc(i)
                on_ve(cast)
                cn = ve.n

                def mm_s(e, ql=ql, cn=cn, Pst=Pst):
                    e.wait_ge(ve.sem, cn)
                    for c in range(CC):
                        i = e.matmul(Pst[0:1, 0:512], ones_c[:],
                                     xn[c][:, ql:ql + 512],
                                     start=(c == 0), stop=(c == CC - 1))
                    te.inc(i)
                on_te(mm_s)
                t2n = te.n

                def rowmath(e, t2n=t2n, qt=qt, Pst=Pst, r0=r0, r1=r1,
                            r2=r2):
                    e.wait_ge(te.sem, t2n)
                    ve.inc(e.tensor_copy(r1[:], Pst[0:1, 0:512]))
                    row_ev[qt] = ve.n
                    e.wait_ge(ve.sem, ve.n)
                    ve.inc(e.tensor_tensor(r2[:], r1[:], r1[:], ALU.mult))
                    e.wait_ge(ve.sem, ve.n)
                    i = e.scalar_tensor_tensor(r0[:], r0[:], 1024.0,
                                               r2[:], ALU.mult,
                                               ALU.subtract)
                    ve.inc(i)
                on_ve(rowmath)
                rmn = ve.n

                def rowact(e, rmn=rmn, r0=r0, r2=r2):
                    e.wait_ge(ve.sem, rmn)
                    se.inc(e.activation(r2[:], r0[:], AF.Ln,
                                        bias=epsr[:],
                                        scale=1.0 / (1024.0 * 1024.0)))
                    e.wait_ge(se.sem, se.n)
                    i = e.activation(r2[:], r2[:], AF.Exp, scale=-0.5)
                    se.inc(i)
                on_se(rowact)
                ran = se.n

                def rowt3(e, ran=ran, r1=r1, r2=r2, r3=r3):
                    e.wait_ge(se.sem, ran)
                    ve.inc(e.tensor_tensor(r3[:], r1[:], r2[:], ALU.mult))
                on_ve(rowt3)
                t3n = ve.n

                def bcast(e, t3n=t3n, Ba=Ba, Bb=Bb, r2=r2, r3=r3):
                    e.wait_ge(ve.sem, t3n)
                    e.matmul(Ba, ones_r[:], r2[:], start=True, stop=True)
                    i = e.matmul(Bb, ones_r[:], r3[:], start=True, stop=True)
                    te.inc(i)
                on_te(bcast)
                bn = te.n

                def apply(e, ql=ql, bn=bn, Ba=Ba, Bb=Bb, tmx=tmx):
                    e.wait_ge(te.sem, bn)
                    for c in range(CC):
                        ve.inc(e.tensor_tensor(tmx[:],
                                               resid[c][:, ql:ql + 512],
                                               Ba, ALU.mult))
                        e.wait_ge(ve.sem, ve.n)
                        i = e.scalar_tensor_tensor(
                            xn[c][:, ql:ql + 512], Bb, -1.0 / 1024.0,
                            tmx[:], ALU.mult, ALU.add)
                        ve.inc(i)
                on_ve(apply)
                return ve.n
            return emit_qt

        def emit_ln(res_ready=None):
            eq = emit_ln_mk(res_ready)
            for qt in range(NQ):
                r = eq(qt)
            return r

        def make_trload(stgs):
            state = {"stg_free": {0: 0, 1: 0}, "trb": 0, "call": 0,
                     "bank_ev": {0: None, 1: None}}

            def trload(src_ap, nch, dst_slices, ld_wait_qg=None,
                       dst_wait_qs=None, on_g=False, floor_ve=None):
                sp = state["call"] % 2
                state["call"] += 1
                stg = stgs[sp]
                lq = qg if on_g else qs

                def ld(eng, src_ap=src_ap, nch=nch, sp=sp, stg=stg, lq=lq):
                    if ld_wait_qg is not None:
                        eng.wait_ge(qg.sem, ld_wait_qg)
                    eng.wait_ge(te.sem, state["stg_free"][sp])
                    eng.dma_start(
                        out=stg[:].rearrange("p c m -> p (c m)")
                        .rearrange("p (c m) -> p c m", c=nch),
                        in_=src_ap.rearrange("(c p) m -> p c m", p=128),
                    ).then_inc(lq.sem, 16)
                (on_po if on_g else on_qs)(ld)
                lq.n += 16
                ldn = lq.n
                inner = 4096 // nch
                flat = stg[:].rearrange("p c m -> p (c m)")
                floor = ve.n if floor_ve is None else floor_ve
                for dst_ap, blocks in dst_slices:
                    par = state["trb"] % 2
                    state["trb"] += 1
                    Pb = (P6 if par == 0 else P7)[:, 0:512].bitcast(BF16)

                    def trf(eng, blocks=blocks, Pb=Pb, ldn=ldn, inner=inner,
                            par=par, floor=floor, lq=lq):
                        eng.wait_ge(lq.sem, ldn)
                        bev = state["bank_ev"][par]
                        if bev:
                            eng.wait_ge(ve.sem, bev[1])
                        eng.wait_ge(ve.sem, floor)
                        for bi, (br, bc) in enumerate(blocks):
                            off = br * inner + bc * 128
                            i = eng.transpose(
                                Pb[:, 128 * bi:128 * (bi + 1)],
                                flat[:, off:off + 128], idn_b[:])
                        te.inc(i)
                    on_te(trf)
                    tn = te.n

                    def evf(eng, dst_ap=dst_ap, Pb=Pb, tn=tn,
                            nb=len(blocks), dw=dst_wait_qs, par=par):
                        eng.wait_ge(te.sem, tn)
                        if dw is not None:
                            eng.wait_ge(dw[0].sem, dw[1])
                        ve.inc(eng.tensor_copy(dst_ap[:, 0:128 * nb],
                                               Pb[:, 0:128 * nb]))
                        state["bank_ev"][par] = ("ve", ve.n)
                    on_ve(evf)
                state["stg_free"][sp] = te.n
            return trload

        def emit_routing(xn_ready):
            for qt in range(NQ):
                ql = qt * 512

                def mm_lg(e, ql=ql, xn_ready=xn_ready):
                    e.wait_ge(ve.sem, max(xn_ready, ve.n))
                    for c in range(CC):
                        i = e.matmul(S0h[0][0:NE, :], gwT[:, c, :],
                                     xn[c][:, ql:ql + 512],
                                     start=(c == 0), stop=(c == CC - 1))
                    te.inc(i)
                on_te(mm_lg)
                lgn = te.n

                def ev_lg(e, lgn=lgn):
                    e.wait_ge(te.sem, lgn)
                    ve.inc(e.tensor_copy(Lsb[:], S0h[0][0:NE, :]))
                on_ve(ev_lg)
                evn = ve.n

                def tr_l(e, qt=qt, evn=evn):
                    e.wait_ge(ve.sem, max(evn, ve.n))
                    for tci in range(4):
                        i = e.transpose(
                            S1h[0][:, 8 * tci:8 * tci + 8],
                            Lsb[0:NE, tci * 128:(tci + 1) * 128],
                            idn_f[0:NE, 0:NE])
                    te.inc(i)
                on_te(tr_l)
                trn = te.n

                def ev_l(e, qt=qt, trn=trn):
                    e.wait_ge(te.sem, trn)
                    ve.inc(e.tensor_copy(Ltm[:, 4 * qt:4 * qt + 4, :],
                                         S1h[0][:, 0:32]))
                on_ve(ev_l)
            lt_done = ve.n

            def stats(e, lt_done=lt_done):
                e.wait_ge(ve.sem, lt_done)
                ve.inc(e.reduce_max(topv[:], Ltm[:], axis=AX.X))
            on_ve(stats)
            tv_n = ve.n

            def sig(e, tv_n=tv_n):
                e.wait_ge(ve.sem, tv_n)
                se.inc(e.activation(rwv[:], topv[:], AF.Sigmoid))
            on_se(sig)
            sg_n = se.n

            def mask(e, sg_n=sg_n):
                e.wait_ge(se.sem, sg_n)
                for ei in range(NE):
                    i = e.tensor_tensor(Mtm[:, :, ei], Ltm[:, :, ei],
                                        topv[:], ALU.is_equal)
                ve.inc(i)
            on_ve(mask)
            mk_n = ve.n

            def mm_tot(e, mk_n=mk_n):
                e.wait_ge(ve.sem, max(mk_n, ve.n))
                for tc in range(TCH):
                    i = e.matmul(P7[0:1, 8 * tc:8 * tc + 8], ones_c[:],
                                 Mtm[:, tc, :], start=True, stop=True)
                te.inc(i)
            on_te(mm_tot)
            mtn = te.n

            def ev_tot(e, mtn=mtn):
                e.wait_ge(te.sem, mtn)
                ve.inc(e.tensor_copy(totr[:], P7[0:1, 0:128]))
            on_ve(ev_tot)
            etn = ve.n

            def tr_tot(e, etn=etn):
                e.wait_ge(ve.sem, etn)
                te.inc(e.transpose(P7[:, 128:256].bitcast(BF16)[:, 0:1],
                                   totr[:], idn_b[0:1, 0:1]))
            on_te(tr_tot)
            ttn = te.n

            def ev_totc(e, ttn=ttn):
                e.wait_ge(te.sem, ttn)
                ve.inc(e.tensor_copy(totc[:],
                                     P7[:, 128:256].bitcast(BF16)[:, 0:1]))
            on_ve(ev_totc)
            etc_n = ve.n

            def mm_off(e, etc_n=etc_n):
                e.wait_ge(ve.sem, etc_n)
                te.inc(e.matmul(P7[:, 256:257], tri168[:], totc[:],
                                start=True, stop=True))
            on_te(mm_off)
            mon = te.n

            def ev_off(e, mon=mon):
                e.wait_ge(te.sem, mon)
                ve.inc(e.tensor_tensor(offc[:], P7[:, 256:257], ecap[:],
                                       ALU.add))
            on_ve(ev_off)
            eon = ve.n

            def tr_off(e, eon=eon):
                e.wait_ge(ve.sem, eon)
                te.inc(e.transpose(P7[0:1, 384:512], offc[:], idn_f[:]))
            on_te(tr_off)
            ton = te.n

            def ev_offr(e, ton=ton):
                e.wait_ge(te.sem, ton)
                ve.inc(e.tensor_copy(offrow[:], P7[0:1, 384:512]))
            on_ve(ev_offr)
            orn = ve.n

            def mm_rank(e, orn=orn):
                e.wait_ge(ve.sem, max(orn, ve.n))
                for tc in range(TCH):
                    e.matmul(P6[:, 8 * tc:8 * tc + 8], trius[:],
                             Mtm[:, tc, :], start=True, stop=False)
                    i = e.matmul(P6[:, 8 * tc:8 * tc + 8], ones_r[:],
                                 offrow[:, 8 * tc:8 * tc + 8],
                                 start=False, stop=True)
                te.inc(i)
            on_te(mm_rank)
            rkn = te.n

            def ev_rank(e, rkn=rkn):
                e.wait_ge(te.sem, rkn)
                ve.inc(e.tensor_tensor(t8b[:], P6[:, 0:128],
                                       Mtm[:].rearrange("p a b -> p (a b)"),
                                       ALU.mult))
                e.wait_ge(ve.sem, ve.n)
                ve.inc(e.reduce_sum(
                    pos_f[:], t8b[:].rearrange("p (a b) -> p a b", b=8),
                    axis=AX.X))
                e.wait_ge(ve.sem, ve.n)
                ve.inc(e.tensor_copy(pos_i[:], pos_f[:]))
            on_ve(ev_rank)
            return ve.n

        def emit_ffn_sparse(wbuf, xgb, Gs, Us, Og, xst, otm, pos_ready,
                            w_ready, trload, after_expert=None,
                            after_tc=None, st_hook={}, xg_on_g=False):
            te_exp = {}
            ns = len(xst)
            ng = len(otm)
            SBH = [S0[:, 0:512], S0[:, 512:1024],
                   S1[:, 0:512], S1[:, 512:1024]]
            sc_done = {}
            sx_ev = {-1: ve.n}
            for tc in range(TCH):
                s = tc % ns
                xs = xst[s]

                Sb = SBH[(tc % ns) if ns == 4 else 2 * (tc % 2)]

                def tr_x(e, tc=tc, Sb=Sb):
                    e.wait_ge(ve.sem, sx_ev.get(tc - ns, sx_ev.get(-1, 0)))
                    if tc >= ns:
                        e.wait_ge(qg.sem, sc_done[tc - ns])
                    for c in range(CC):
                        i = e.transpose(
                            Sb.bitcast(BF16)[:, 128 * c:128 * (c + 1)],
                            xn[c][:, tc * 128:(tc + 1) * 128], idn_b[:])
                    te.inc(i)
                on_te(tr_x)
                txn = te.n

                def ev_x(e, tc=tc, txn=txn, xs=xs, Sb=Sb):
                    e.wait_ge(te.sem, txn)
                    i = e.tensor_scalar(
                        xs[:], Sb.bitcast(BF16)[:, 0:1024],
                        rwv[:, tc:tc + 1], None, ALU.mult)
                    ve.inc(i)
                on_ve(ev_x)
                exn = ve.n
                sx_ev[tc] = exn

                def sc_x(e, tc=tc, exn=exn, xs=xs, pos_ready=pos_ready):
                    e.wait_ge(ve.sem, max(exn, pos_ready))
                    e.indirect_dma_start(
                        out=scr_X,
                        out_offset=IndirectOffsetOnAxis(pos_i[:, tc:tc + 1], 0),
                        in_=xs[:], in_offset=None,
                    ).then_inc(qg.sem, 16)
                on_po(sc_x)
                qg.n += 16
                sc_done[tc] = qg.n
            scat_all = qg.n

            og_store = {}
            gu_free = 0
            for e_i in range(NE):
                s = e_i % 2
                wgT, wuT, wdT = wbuf[s]
                xg = xgb[s]
                og = Og[s]
                xgf = xg[:].rearrange("p c m -> p (c m)")

                trload(scr_X[e_i * CAP:(e_i + 1) * CAP, :], 4,
                       [(xgf[:, fc * 512:(fc + 2) * 512],
                         [(sc, f) for f in (fc, fc + 1) for sc in range(4)])
                        for fc in range(0, CC, 2)],
                       ld_wait_qg=scat_all,
                       dst_wait_qs=((qa, og_store[e_i - 2])
                                    if e_i >= 2 else None),
                       on_g=xg_on_g, floor_ve=pos_ready)
                xg_n = ve.n
                xg_se = se.n
                if after_expert is not None and e_i >= 1:
                    after_expert(e_i - 1, te_exp)

                wr = w_ready[e_i]
                if len(wr) == 4:
                    ch_w, n_w = wr[0], wr[1]
                    ch_w2, n_w2 = wr[2], wr[3]
                else:
                    ch_w, n_w = wr
                    ch_w2, n_w2 = None, 0
                SB4 = [S0h[0], S0h[1], S1h[0], S1h[1]]

                def mm_g(eng, n_w=n_w, wgT=wgT, ch_w=ch_w, xg=xg,
                         xg_n=xg_n, gf=gu_free, xg_se=xg_se,
                         ch_w2=ch_w2, n_w2=n_w2):
                    eng.wait_ge(ch_w.sem, n_w)
                    if ch_w2 is not None:
                        eng.wait_ge(ch_w2.sem, n_w2)
                    eng.wait_ge(se.sem, xg_se)
                    eng.wait_ge(ve.sem, max(xg_n, gf))
                    for ic in range(4):
                        for c in range(CC):
                            i = eng.matmul(
                                SB4[ic], wgT[:, c, ic * 128:(ic + 1) * 128],
                                xg[:, c, :],
                                start=(c == 0), stop=(c == CC - 1))
                    te.inc(i)
                on_te(mm_g)
                gn = te.n

                def ev_g(eng, gn=gn, Gs=Gs):
                    eng.wait_ge(te.sem, gn)
                    i = eng.tensor_copy(Gs.rearrange("p a b -> p (a b)")
                                        [:, 0:1024], S0[:])
                    ve.inc(i)
                    i = eng.tensor_copy(Gs.rearrange("p a b -> p (a b)")
                                        [:, 1024:2048], S1[:])
                    ve.inc(i)
                on_ve(ev_g)
                evg_a = ve.n - 1
                evg_n = ve.n

                def silu(eng, evg_n=evg_n, Gs=Gs):
                    eng.wait_ge(ve.sem, evg_n)
                    se.inc(eng.activation(
                        Gs.rearrange("p a b -> p (a b)"),
                        Gs.rearrange("p a b -> p (a b)"), AF.Silu))
                on_se(silu)
                sl_n = se.n

                def mm_u(eng, n_w=n_w, wuT=wuT, ch_w=ch_w, xg=xg,
                         evg_a=evg_a, evg_n=evg_n):
                    eng.wait_ge(ve.sem, evg_a)
                    for ic in range(4):
                        if ic == 2:
                            eng.wait_ge(ve.sem, evg_n)
                        for c in range(CC):
                            i = eng.matmul(
                                SB4[ic], wuT[:, c, ic * 128:(ic + 1) * 128],
                                xg[:, c, :],
                                start=(c == 0), stop=(c == CC - 1))
                    te.inc(i)
                on_te(mm_u)
                un = te.n

                def ev_u(eng, un=un, Us=Us):
                    eng.wait_ge(te.sem, un)
                    eng.tensor_copy(Us.rearrange("p a b -> p (a b)")
                                    [:, 0:1024], S0[:])
                    i = eng.tensor_copy(Us.rearrange("p a b -> p (a b)")
                                        [:, 1024:2048], S1[:])
                    ve.inc(i, 2)
                on_ve(ev_u)
                gu_done = ve.n

                def hmul(eng, sl_n=sl_n, gu_done=gu_done, Gs=Gs, Us=Us):
                    eng.wait_ge(se.sem, sl_n)
                    eng.wait_ge(ve.sem, gu_done)
                    ve.inc(eng.tensor_tensor(
                        Us.rearrange("p a b -> p (a b)"),
                        Us.rearrange("p a b -> p (a b)"),
                        Gs.rearrange("p a b -> p (a b)"), ALU.mult))
                on_ve(hmul)
                h_n = ve.n

                DB = [Y0[:], Y1[:], S0h[0], S0h[1], S1h[0], S1h[1]]
                karr = [(sc4, hh) for sc4 in range(4) for hh in range(2)]

                def mm_d1(eng, h_n=h_n, wdT=wdT, Us=Us):
                    eng.wait_ge(ve.sem, h_n)
                    for k in range(6):
                        sc4, hh = karr[k]
                        for ic in range(4):
                            i = eng.matmul(
                                DB[k], Us[:, ic, sc4 * 128:(sc4 + 1) * 128],
                                wdT[:, ic, hh * 512:(hh + 1) * 512],
                                start=(ic == 0), stop=(ic == 3))
                    te.inc(i)
                on_te(mm_d1)
                d1n = te.n

                def ev_d1(eng, d1n=d1n, og=og, e_i=e_i):
                    eng.wait_ge(te.sem, d1n)
                    if e_i >= 1:
                        eng.wait_ge(qa.sem, og_store[e_i - 1])
                    for k in range(2):
                        sc4, hh = karr[k]
                        i = eng.tensor_copy(
                            og[:, sc4, hh * 512:(hh + 1) * 512], DB[k])
                    ve.inc(i)
                    for k in range(2, 6):
                        sc4, hh = karr[k]
                        i = eng.tensor_copy(
                            og[:, sc4, hh * 512:(hh + 1) * 512], DB[k])
                    ve.inc(i)
                on_ve(ev_d1)
                ed1y = ve.n - 1
                ed1 = ve.n

                def mm_d2(eng, ed1=ed1y, wdT=wdT, Us=Us):
                    eng.wait_ge(ve.sem, ed1)
                    for k in range(6, 8):
                        sc4, hh = karr[k]
                        for ic in range(4):
                            i = eng.matmul(
                                DB[k - 6],
                                Us[:, ic, sc4 * 128:(sc4 + 1) * 128],
                                wdT[:, ic, hh * 512:(hh + 1) * 512],
                                start=(ic == 0), stop=(ic == 3))
                    te.inc(i)
                on_te(mm_d2)
                d2n = te.n

                def ev_d2(eng, d2n=d2n, og=og):
                    eng.wait_ge(te.sem, d2n)
                    for k in range(6, 8):
                        sc4, hh = karr[k]
                        i = eng.tensor_copy(
                            og[:, sc4, hh * 512:(hh + 1) * 512], DB[k - 6])
                    ve.inc(i)
                on_ve(ev_d2)
                og_done = ve.n
                gu_free = og_done

                def st_og(eng, e_i=e_i, og=og, og_done=og_done):
                    eng.wait_ge(ve.sem, og_done)
                    eng.dma_start(
                        out=scr_O[e_i * CAP:(e_i + 1) * CAP, :]
                        .rearrange("(c p) m -> p c m", p=128),
                        in_=og[:]).then_inc(qa.sem, 16)
                on_se(st_og)
                qa.n += 16
                og_store[e_i] = qa.n

                te_exp[e_i] = te.n
            if after_expert is not None:
                after_expert(NE - 1, te_exp)
            all_og = qa.n

            ga_done = {}
            ot_used = {}
            ao_ev = {-1: ve.n}
            for tc in range(TCH):
                s = tc % ng
                ot = otm[s]

                def ga(eng, tc=tc, ot=ot, s=s):
                    eng.wait_ge(qa.sem, all_og)
                    if tc >= ng:
                        eng.wait_ge(te.sem, ot_used[tc - ng])
                    if after_tc is not None and (tc - 2) in st_hook:
                        eng.wait_ge(qs.sem, st_hook[tc - 2])
                    eng.indirect_dma_start(
                        out=ot[:], out_offset=None,
                        in_=scr_O,
                        in_offset=IndirectOffsetOnAxis(pos_i[:, tc:tc + 1],
                                                       0),
                    ).then_inc(qg.sem, 16)
                on_po(ga)
                qg.n += 16
                gn_ = qg.n

                Sb = SBH[(tc % ng) if ng == 4 else 2 * (tc % 2)]

                def tr_o(eng, tc=tc, gn_=gn_, ot=ot, Sb=Sb):
                    eng.wait_ge(qg.sem, gn_)
                    eng.wait_ge(ve.sem, ao_ev.get(tc - ng, ao_ev.get(-1, 0)))
                    for c in range(CC):
                        i = eng.transpose(
                            Sb.bitcast(BF16)
                            [:, 128 * c:128 * (c + 1)],
                            ot[:, c * 128:(c + 1) * 128], idn_b[:])
                    te.inc(i)
                on_te(tr_o)
                ot_used[tc] = te.n
                trn = te.n

                def add_o(eng, tc=tc, trn=trn, Sb=Sb):
                    eng.wait_ge(te.sem, trn)
                    for c in range(CC):
                        i = eng.scalar_tensor_tensor(
                            resid[c][:, tc * 128:(tc + 1) * 128],
                            Sb.bitcast(BF16)
                            [:, 128 * c:128 * (c + 1)],
                            0.5, resid[c][:, tc * 128:(tc + 1) * 128],
                            ALU.mult, ALU.add)
                    ve.inc(i)
                on_ve(add_o)
                ao_ev[tc] = ve.n
                if after_tc is not None:
                    st_hook[tc] = after_tc(tc, ve.n)
            return ve.n, te.n

        new_phase()
        with ExitStack() as pctx:
            def psb(name, shape, dtype=BF16):
                return pctx.enter_context(nc.sbuf_tensor(name, shape, dtype))

            wbuf = []
            for s in range(2):
                wbuf.append((psb(f"wgT{s}", [128, CC, DI]),
                             psb(f"wuT{s}", [128, CC, DI]),
                             psb(f"wdT{s}", [128, 4, C])))
            Gs = xn[2][:].rearrange("p (a b) -> p a b", a=4)
            Us = xn[3][:].rearrange("p (a b) -> p a b", a=4)
            xgb = [psb("xgb0", [128, CC, 512]), psb("xgb1", [128, CC, 512])]
            Og = [x[:].rearrange("p c m -> p (c m)")
                  .rearrange("p (c m) -> p c m", c=4) for x in xgb]
            stg = [psb("stg_p0", [128, 4, 1024]), psb("stg_p1", [128, 4, 1024])]
            trload = make_trload(stg)
            xse = [psb("xse0", [128, 1024]), psb("xse1", [128, 1024])]
            xst = ([stage[i][:].bitcast(BF16)[:, 0:1024] for i in range(2)]
                   + [x[:] for x in xse])
            otm = xst[0:2]

            w_ready1 = {}
            w4_done = {}
            w2_done = {}

            def emit_w12(e_i):
                def w12(eng, e_i=e_i):
                    eng.dma_start(out=scr_g[e_i], in_=wg_d[e_i]
                                  ).then_inc(qg.sem, 16)
                    eng.dma_start(out=scr_u[e_i], in_=wu_d[e_i]
                                  ).then_inc(qg.sem, 16)
                    eng.dma_start(out=scr_d[e_i], in_=wd_d[e_i]
                                  ).then_inc(qg.sem, 16)
                on_po(w12)
                qg.n += 48
                w2_done[e_i] = qg.n


            prep_floor = [None]

            def emit_prep(e_i, te_exp):
                s = e_i % 2
                wgT, wuT, wdT = wbuf[s]
                dw = (qa, w4_done[e_i - 2]) if e_i >= 2 else None
                for mat, dst in ((scr_g[e_i], wgT), (scr_u[e_i], wuT)):
                    dflat = dst[:].rearrange("p c m -> p (c m)")
                    trload(
                        mat, 4,
                        [(dflat[:, fc * 512:(fc + 2) * 512],
                          [(ic, f) for f in (fc, fc + 1) for ic in range(4)])
                         for fc in range(0, CC, 2)],
                        ld_wait_qg=w2_done[e_i], dst_wait_qs=dw,
                        floor_ve=prep_floor[0])
                trload(
                    scr_d[e_i], 8,
                    [(wdT[:, ic, :], [(hc, ic) for hc in range(CC)])
                     for ic in range(4)],
                    ld_wait_qg=w2_done[e_i], dst_wait_qs=dw,
                    floor_ve=prep_floor[0])
                ev_now = ve.n

                def st_T(eng, e_i=e_i, wgT=wgT, wuT=wuT, wdT=wdT,
                         ev_now=ev_now):
                    eng.wait_ge(ve.sem, ev_now)
                    eng.dma_start(
                        out=scr_gT[e_i].rearrange("(c p) m -> p c m", p=128),
                        in_=wgT[:]).then_inc(qa.sem, 16)
                    eng.dma_start(
                        out=scr_uT[e_i].rearrange("(c p) m -> p c m", p=128),
                        in_=wuT[:]).then_inc(qa.sem, 16)
                    eng.dma_start(
                        out=scr_dT[e_i].rearrange("(c p) m -> p c m", p=128),
                        in_=wdT[:]).then_inc(qa.sem, 16)
                on_se(st_T)
                qa.n += 48
                w4_done[e_i] = qa.n
                w_ready1[e_i] = (ve, ve.n, se, se.n)

            for e_i in range(NE):
                emit_w12(e_i)

            xn1 = emit_ln(res_ready=lambda qt: ve_x_evac[4 * qt + 3])
            tap("xn1", [t[:] for t in xn], T, BF16)
            pr1 = emit_routing(xn1)
            prep_floor[0] = pr1
            emit_prep(0, {})
            emit_prep(1, {})
            tap("mtm1", [Mtm[:]], TCH * NE, BF16)
            tap("rwv1", [rwv[:]], TCH, F32)
            tap("pos1", [pos_f[:]], TCH, F32)

            def prep_next(e_i, te_exp):
                if e_i + 2 < NE:
                    emit_prep(e_i + 2, te_exp)

            emit_ffn_sparse(wbuf, xgb, Gs, Us, Og, xst, otm, pr1,
                            w_ready1, trload, after_expert=prep_next)
            tap("resid1", [r[:] for r in resid], T, F32)
            tap("wgt0", [wbuf[0][0][:].rearrange("p c m -> p (c m)")], CC * DI, BF16)

        w4_all = [qa.n]
        new_phase()
        with ExitStack() as actx:
            y = [actx.enter_context(nc.sbuf_tensor(f"y{c}", [128, T], BF16))
                 for c in range(CC)]
            rza = actx.enter_context(nc.sbuf_tensor("rza", [1, 512], F32))
            rzb_r = actx.enter_context(nc.sbuf_tensor("rzb_r", [1, 512], F32))
            ysb = actx.enter_context(nc.sbuf_tensor("ysb", [128, 512], BF16))
            t4 = actx.enter_context(nc.sbuf_tensor("t4", [4, 128], F32))

            xn2 = emit_ln()

            with ExitStack() as a1ctx:
                def a1(name, shape, dtype=BF16):
                    return a1ctx.enter_context(
                        nc.sbuf_tensor(name, shape, dtype))

                vt = a1("vt", [128, TCH, 8, 65])
                tap("xn2", [t[:] for t in xn], T, BF16)
                wvsb = a1("wvsb", [128, CC, 512])
                wqk = [(a1(f"wqs{s}", [128, CC, 128]),
                        a1(f"wks{s}", [128, CC, 128])) for s in range(2)]
                Qg = a1("Qg", [128, T])
                Kg = a1("Kg", [128, T])
                pts = [a1(f"pts{i}", [128, 1024]) for i in range(2)]

                def vones(eng):
                    ve.inc(eng.memset(vt[:, :, :, 64:65], 1.0))
                on_ve(vones)

                def ld_qk0(eng):
                    eng.dma_start(
                        out=wqk[0][0][:],
                        in_=wq_d[:, 0:128].rearrange("(c p) m -> p c m",
                                                     p=128),
                    ).then_inc(qg.sem, 16)
                    eng.dma_start(
                        out=wqk[0][1][:],
                        in_=wk_d[:, 0:128].rearrange("(c p) m -> p c m",
                                                     p=128),
                    ).then_inc(qg.sem, 16)
                on_po(ld_qk0)
                qg.n += 32
                qk_ld = {0: qg.n}

                def emit_vhalf(half, xn_ready):
                    def ldv(eng, half=half):
                        eng.dma_start(
                            out=wvsb[:],
                            in_=wv_d[:, half * 512:(half + 1) * 512]
                            .rearrange("(c p) m -> p c m", p=128),
                        ).then_inc(qg.sem, 16)
                    on_po(ldv)
                    qg.n += 16
                    vl = qg.n
                    vfloor = ve.n
                    vbank_ev = {}
                    for tc in range(TCH):
                        def mmv(eng, tc=tc, vl=vl, xn_ready=xn_ready,
                                bev=vbank_ev.get(tc % 4, vfloor)):
                            eng.wait_ge(qg.sem, vl)
                            eng.wait_ge(ve.sem, max(xn_ready, bev))
                            eng.wait_ge(se.sem, se.n)
                            bank = GEN[tc % 4]
                            for c in range(CC):
                                i = eng.matmul(
                                    bank, xn[c][:, tc * 128:(tc + 1) * 128],
                                    wvsb[:, c, :],
                                    start=(c == 0), stop=(c == CC - 1))
                            te.inc(i)
                        on_te(mmv)
                        vn = te.n

                        def evv(eng, tc=tc, vn=vn):
                            eng.wait_ge(te.sem, vn)
                            bank = GEN[tc % 4]
                            ve.inc(eng.tensor_copy(
                                vt[:, tc, :, 0:64],
                                bank[:, 0:512].rearrange("p (h c) -> p h c",
                                                         h=8)))
                        on_ve(evv)
                        vbank_ev[tc % 4] = ve.n
                    return ve.n

                emit_vhalf(0, xn2)

                yfm_cnt = {0: (0, 0), 1: (0, 0)}
                s_exp_cnt = {0: 0, 1: 0}
                rzb_rd = {0: 0, 1: 0}

                def emit_group(g):
                    sset = g % 2
                    wqs, wks = wqk[sset]
                    if g + 1 < 8:
                        def ldn(eng, g=g):
                            eng.wait_ge(te.sem, te.n)
                            c0 = (g + 1) * 128
                            eng.dma_start(
                                out=wqk[(g + 1) % 2][0][:],
                                in_=wq_d[:, c0:c0 + 128]
                                .rearrange("(c p) m -> p c m", p=128),
                            ).then_inc(qg.sem, 16)
                            eng.dma_start(
                                out=wqk[(g + 1) % 2][1][:],
                                in_=wk_d[:, c0:c0 + 128]
                                .rearrange("(c p) m -> p c m", p=128),
                            ).then_inc(qg.sem, 16)
                        on_po(ldn)
                        qg.n += 32
                        qk_ld[g + 1] = qg.n

                    pbe = {0: ve.n, 1: ve.n}
                    for which, wsl, dst in ((0, wqs, Qg), (1, wks, Kg)):
                        for qt in range(NQ):
                            ql = qt * 512
                            bit = (which * NQ + qt) % 2
                            bankP = P6 if bit == 0 else P7

                            def mmp(eng, wsl=wsl, ql=ql, bankP=bankP, g=g,
                                    bev=pbe[bit]):
                                eng.wait_ge(qg.sem, qk_ld[g])
                                eng.wait_ge(ve.sem, bev)
                                for c in range(CC):
                                    i = eng.matmul(bankP[:], wsl[:, c, :],
                                                   xn[c][:, ql:ql + 512],
                                                   start=(c == 0),
                                                   stop=(c == CC - 1))
                                te.inc(i)
                            on_te(mmp)
                            pn = te.n

                            def evp(eng, dst=dst, ql=ql, bankP=bankP, pn=pn):
                                eng.wait_ge(te.sem, pn)
                                ve.inc(eng.tensor_copy(dst[:, ql:ql + 512],
                                                       bankP[:]))
                            on_ve(evp)
                            pbe[bit] = ve.n
                    qk_done = ve.n

                    def sq_q(eng, qk_done=qk_done):
                        eng.wait_ge(ve.sem, qk_done)
                        eng.wait_ge(se.sem, se.n)
                        eng.wait_ge(te.sem, te.n)
                        eng.tensor_tensor(pts[0][:], Qg[:, 0:1024],
                                          Qg[:, 0:1024], ALU.mult)
                        i = eng.tensor_tensor(pts[1][:], Qg[:, 1024:2048],
                                              Qg[:, 1024:2048], ALU.mult)
                        ve.inc(i)
                    on_ve(sq_q)
                    sqq_n = ve.n

                    def mm_qss(eng, sqq_n=sqq_n):
                        eng.wait_ge(ve.sem, sqq_n)
                        for tc in range(TCH):
                            src = pts[0] if tc < 8 else pts[1]
                            i = eng.matmul(
                                P7[:, 2 * tc:2 * tc + 2],
                                src[:, (tc % 8) * 128:(tc % 8 + 1) * 128],
                                e2g[:], start=True, stop=True)
                        te.inc(i)
                    on_te(mm_qss)
                    qssn = te.n

                    def sq_k(eng, qssn=qssn):
                        eng.wait_ge(te.sem, qssn)
                        eng.tensor_tensor(pts[0][:], Kg[:, 0:1024],
                                          Kg[:, 0:1024], ALU.mult)
                        i = eng.tensor_tensor(pts[1][:], Kg[:, 1024:2048],
                                              Kg[:, 1024:2048], ALU.mult)
                        ve.inc(i)
                    on_ve(sq_k)
                    sqk_n = ve.n

                    def mm_kss(eng, sqk_n=sqk_n):
                        eng.wait_ge(ve.sem, sqk_n)
                        for tc in range(TCH):
                            src = pts[0] if tc < 8 else pts[1]
                            i = eng.matmul(
                                P7[:, 32 + 2 * tc:32 + 2 * tc + 2],
                                src[:, (tc % 8) * 128:(tc % 8 + 1) * 128],
                                e2g[:], start=True, stop=True)
                        te.inc(i)
                    on_te(mm_kss)
                    kssn = te.n

                    def ev_qss(eng, kssn=kssn):
                        eng.wait_ge(te.sem, kssn)
                        eng.reduce_max(
                            q4[:, 0:2],
                            P7[:, 0:32].rearrange("p (t j) -> p j t", j=2),
                            axis=AX.X)
                        i = eng.reduce_max(
                            q4[:, 2:4],
                            P7[:, 32:64].rearrange("p (t j) -> p j t", j=2),
                            axis=AX.X)
                        ve.inc(i)
                    on_ve(ev_qss)
                    q4n = ve.n

                    def tr_q4(eng, q4n=q4n):
                        eng.wait_ge(ve.sem, q4n)
                        te.inc(eng.transpose(P7[0:4, 64:192], q4[:],
                                             idn_f[:]))
                    on_te(tr_q4)
                    t4n = te.n

                    def red_q4(eng, t4n=t4n):
                        eng.wait_ge(te.sem, t4n)
                        e = eng
                        ve.inc(e.tensor_copy(t4[:], P7[0:4, 64:192]))
                        e.wait_ge(ve.sem, ve.n)
                        ve.inc(e.reduce_max(t4m[:], t4[:], axis=AX.X))
                        e.wait_ge(ve.sem, ve.n)
                        ve.inc(e.tensor_copy(z16[0:4, 0:1], t4m[:]))
                    on_ve(red_q4)
                    z16n = ve.n

                    def tr_z(eng, z16n=z16n):
                        eng.wait_ge(ve.sem, z16n)
                        te.inc(eng.transpose(P7[0:16, 192:208], z16[:],
                                             idn_f[0:16, 0:16]))
                    on_te(tr_z)
                    tzn = te.n

                    def mk_b2(eng, tzn=tzn):
                        eng.wait_ge(te.sem, tzn)
                        ve.inc(eng.tensor_copy(zrow[:], P7[0:1, 192:208]))
                        eng.wait_ge(ve.sem, ve.n)
                        ve.inc(eng.tensor_tensor(b2[:], zrow[:, 0:2],
                                                 zrow[:, 2:4], ALU.add))
                    on_ve(mk_b2)
                    b2n = ve.n

                    def bc_b2(eng, b2n=b2n):
                        eng.wait_ge(ve.sem, b2n)
                        te.inc(eng.matmul(P7[:, 208:210], ones_r[:], b2[:],
                                          start=True, stop=True))
                    on_te(bc_b2)
                    bcn = te.n

                    def ev_b2(eng, bcn=bcn):
                        eng.wait_ge(te.sem, bcn)
                        eng.wait_ge(se.sem, max(s_exp_cnt[0],
                                                s_exp_cnt[1]))
                        ve.inc(eng.tensor_scalar(
                            biasg[:], P7[:, 208:210],
                            -0.5 * SCALE * 1.01, None, ALU.mult))
                    on_ve(ev_b2)
                    bias_n = ve.n

                    for qt in range(NQ - 1, -1, -1):
                        nk = 4 * qt + 4
                        nb = nk // 2
                        ql = qt * 512
                        T0 = te.n
                        se0 = se.n
                        po0 = po.n
                        pred_te = lambda b, j, T0=T0: T0 + 2 * b + j + 1
                        pred_se = lambda b, j, se0=se0: se0 + 2 * b + j + 1

                        def pred_po(b, j, qt=qt, po0=po0):
                            return po0 + (b - 2 * qt) * 2 + j + 1

                        for b in range(nb):
                            for j in range(2):
                                hb = 64 * j
                                blk = (2 * g + j) % 8
                                Stile = S0 if j == 0 else S1
                                Yv = Y0 if j == 0 else Y1
                                pt = pts[j]
                                kcs = (2 * b, 2 * b + 1)

                                def mm_sc(eng, b=b, j=j, kcs=kcs,
                                          Stile=Stile, hb=hb, ql=ql,
                                          bias_n=bias_n, blk=blk, Yv=Yv,
                                          pt=pt, qt=qt):
                                    if b == 0:
                                        eng.wait_ge(se.sem, s_exp_cnt[j])
                                        eng.wait_ge(ve.sem, bias_n)
                                        rv, rs = yfm_cnt[j]
                                        if rv:
                                            eng.wait_ge(ve.sem, rv)
                                        if rs:
                                            eng.wait_ge(se.sem, rs)
                                    else:
                                        eng.wait_ge(se.sem,
                                                    pred_se(b - 1, j))
                                        if 2 * b - 1 >= 4 * qt:
                                            eng.wait_ge(po.sem,
                                                        pred_po(b - 1, j))
                                        for i_k, kc in enumerate(
                                                (2 * b - 2, 2 * b - 1)):
                                            eng.matmul(
                                                Yv[0:65, :],
                                                vt[:, kc, blk, :],
                                                pt[:, i_k * 512:
                                                   (i_k + 1) * 512],
                                                start=(kc == 0), stop=False)
                                    for i_k, kc in enumerate(kcs):
                                        i = eng.matmul(
                                            Stile[:, i_k * 512:
                                                  (i_k + 1) * 512],
                                            Kg[hb:hb + 64,
                                               kc * 128:(kc + 1) * 128],
                                            Qg[hb:hb + 64, ql:ql + 512],
                                            start=True, stop=True)
                                    te.inc(i)
                                on_te(mm_sc)
                                assert te.n == pred_te(b, j)

                        def expall(eng, nb=nb):
                            eng.wait_ge(ve.sem, bias_n)
                            for b in range(nb):
                                for j in range(2):
                                    eng.wait_ge(te.sem, pred_te(b, j))
                                    i = eng.activation(
                                        pts[j][:],
                                        S0[:] if j == 0 else S1[:],
                                        AF.Exp, bias=biasg[:, j:j + 1],
                                        scale=SCALE)
                                    se.inc(i)
                        on_se(expall)
                        assert se.n == pred_se(nb - 1, 1)
                        for j in range(2):
                            s_exp_cnt[j] = pred_se(nb - 1, j)

                        for b in (2 * qt, 2 * qt + 1):
                            for j in range(2):
                                pt = pts[j]
                                kcs = (2 * b, 2 * b + 1)

                                def dfix(eng, b=b, j=j, kcs=kcs, qt=qt,
                                         pt=pt):
                                    eng.wait_ge(se.sem, pred_se(b, j))
                                    for i_k, kc in enumerate(kcs):
                                        if kc < 4 * qt:
                                            continue
                                        off = i_k * 512
                                        vs = 128 * (kc - 4 * qt)
                                        if vs > 0:
                                            eng.memset(
                                                pt[:, off:off + vs], 0.0)
                                        i = eng.affine_select(
                                            out=pt[:, off + vs:
                                                   off + vs + 128],
                                            in_=pt[:, off + vs:
                                                   off + vs + 128],
                                            pattern=[[1, 128]],
                                            compare_op=ALU.is_ge,
                                            fill=0.0, base=0,
                                            channel_multiplier=-1)
                                    po.inc(i)
                                on_po(dfix)
                                assert po.n == pred_po(b, j), (po.n, b, j)

                        avn = {}
                        for j in range(2):
                            blk = (2 * g + j) % 8
                            Yv = Y0 if j == 0 else Y1
                            pt = pts[j]

                            def mm_av_last(eng, j=j, nb=nb, nk=nk, blk=blk,
                                           Yv=Yv, pt=pt):
                                eng.wait_ge(se.sem, pred_se(nb - 1, j))
                                eng.wait_ge(po.sem, pred_po(nb - 1, j))
                                for i_k, kc in enumerate((nk - 2, nk - 1)):
                                    i = eng.matmul(
                                        Yv[0:65, :], vt[:, kc, blk, :],
                                        pt[:, i_k * 512:(i_k + 1) * 512],
                                        start=(kc == 0), stop=(kc == nk - 1))
                                te.inc(i)
                            on_te(mm_av_last)
                            avn[j] = te.n

                        def nrm_ln(eng, avn=avn):
                            eng.wait_ge(te.sem, max(avn[0], avn[1]))
                            eng.activation(rza[:], Y0[64:65, :], AF.Ln)
                            se.inc(eng.activation(rzb_r[:], Y1[64:65, :],
                                                  AF.Ln))
                            eng.wait_ge(se.sem, se.n)
                            eng.activation(rza[:], rza[:], AF.Exp,
                                           scale=-1.0)
                            i = eng.activation(rzb_r[:], rzb_r[:], AF.Exp,
                                               scale=-1.0)
                            se.inc(i)
                        on_se(nrm_ln)
                        rz_se = se.n

                        for j in range(2):
                            hb = 64 * j
                            Yv = Y0 if j == 0 else Y1
                            Pz = P6 if j == 0 else P7
                            rzj = rza if j == 0 else rzb_r

                            def nrm_bc(eng, rz_se=rz_se, Pz=Pz, rzj=rzj):
                                eng.wait_ge(se.sem, rz_se)
                                te.inc(eng.matmul(Pz[0:64, :],
                                                  ones_r[:, 0:64], rzj[:],
                                                  start=True, stop=True))
                            on_te(nrm_bc)
                            nbn = te.n

                            def nrm_ysb(eng, j=j, Yv=Yv, avn=avn, hb=hb):
                                eng.wait_ge(te.sem, avn[j])
                                ve.inc(eng.tensor_copy(ysb[hb:hb + 64, :],
                                                       Yv[0:64, :]))
                            on_ve(nrm_ysb)
                            ysn = ve.n

                            def nrm_yfm(eng, g=g, hb=hb, ql=ql, nbn=nbn,
                                        ysn=ysn, Pz=Pz):
                                eng.wait_ge(te.sem, nbn)
                                eng.wait_ge(ve.sem, ysn)
                                i = eng.tensor_tensor(
                                    y[g][hb:hb + 64, ql:ql + 512],
                                    ysb[hb:hb + 64, :], Pz[0:64, :],
                                    ALU.mult)
                                ve.inc(i)
                            on_ve(nrm_yfm)
                            yfm_cnt[j] = (ve.n, 0)

                for g in range(4):
                    emit_group(g)
                emit_vhalf(1, xn2)
                for g in range(4, 8):
                    emit_group(g)
                tap("qg7", [Qg[:], Kg[:]], T, BF16)
                tap("biasg7", [biasg[:]], 2, F32)
                tap("y", [t[:] for t in y], T, BF16)
                tap("vt15", [vt[:, 15, :, :].rearrange("p a b -> p (a b)")], 8 * 65, BF16)

            with ExitStack() as a2ctx:
                wout = a2ctx.enter_context(
                    nc.sbuf_tensor("wout", [128, CC, C], BF16))

                def ld_wo(eng):
                    eng.wait_ge(te.sem, te.n)
                    eng.dma_start(
                        out=wout[:],
                        in_=wo_d.rearrange("(c p) m -> p c m", p=128),
                    ).then_inc(qg.sem, 16)
                on_po(ld_wo)
                qg.n += 16
                wo_n = qg.n

                obe = {0: ve.n, 1: ve.n}
                ose = se.n
                oq_ev = {}
                eq3 = emit_ln_mk(res_ready=lambda qt: oq_ev[qt],
                                 p67_floor=ve.n)
                for qt in range(NQ):
                    for fo in range(CC):
                        ql = qt * 512
                        obit = (qt * CC + fo) % 2
                        bankP = Y0 if obit == 0 else Y1

                        def mm_o(eng, fo=fo, ql=ql, bankP=bankP, wo_n=wo_n,
                                 bev=obe[obit], ose=ose):
                            eng.wait_ge(qg.sem, wo_n)
                            eng.wait_ge(ve.sem, bev)
                            eng.wait_ge(se.sem, ose)
                            for c in range(CC):
                                i = eng.matmul(
                                    bankP[:],
                                    wout[:, c, fo * 128:(fo + 1) * 128],
                                    y[c][:, ql:ql + 512],
                                    start=(c == 0), stop=(c == CC - 1))
                            te.inc(i)
                        on_te(mm_o)
                        on_ = te.n

                        def ev_o(eng, fo=fo, ql=ql, bankP=bankP, on_=on_):
                            eng.wait_ge(te.sem, on_)
                            i = eng.scalar_tensor_tensor(
                                resid[fo][:, ql:ql + 512], bankP[:],
                                bout[:, fo:fo + 1],
                                resid[fo][:, ql:ql + 512], ALU.add, ALU.add)
                            ve.inc(i)
                        on_ve(ev_o)
                        obe[obit] = ve.n
                    oq_ev[qt] = ve.n
                    xn3 = eq3(qt)

        tap("resid2", [r[:] for r in resid], T, F32)
        new_phase()
        with ExitStack() as f2ctx:
            def fsb(name, shape, dtype=BF16):
                return f2ctx.enter_context(nc.sbuf_tensor(name, shape, dtype))

            wbuf2 = []
            for s in range(2):
                wbuf2.append((fsb(f"w2gT{s}", [128, CC, DI]),
                              fsb(f"w2uT{s}", [128, CC, DI]),
                              fsb(f"w2dT{s}", [128, 4, C])))
            Gs2 = xn[2][:].rearrange("p (a b) -> p a b", a=4)
            Us2 = xn[3][:].rearrange("p (a b) -> p a b", a=4)
            xgb2 = [fsb("xgb20", [128, CC, 512]),
                    fsb("xgb21", [128, CC, 512])]
            Og2 = [x[:].rearrange("p c m -> p (c m)")
                   .rearrange("p (c m) -> p c m", c=4) for x in xgb2]
            stg2 = [fsb("stg_f20", [128, 4, 1024]),
                    fsb("stg_f21", [128, 4, 1024])]
            trload2 = make_trload(stg2)
            xse2 = [fsb("xse20", [128, 1024]), fsb("xse21", [128, 1024])]
            xst2 = ([stage[i][:].bitcast(BF16)[:, 0:1024] for i in range(2)]
                    + [x[:] for x in xse2])
            otm2 = xst2[0:2]

            pr2 = emit_routing(xn3)

            w_ready2 = {}

            def emit_load2(e_i, te_exp):
                s = e_i % 2

                def ld(eng, e_i=e_i, s=s, te_exp=te_exp):
                    eng.wait_ge(qa.sem, w4_all[0])
                    if e_i >= 2:
                        eng.wait_ge(te.sem, te_exp[e_i - 2])
                    eng.dma_start(
                        out=wbuf2[s][0][:],
                        in_=scr_gT[e_i].rearrange("(c p) m -> p c m", p=128),
                    ).then_inc(qs.sem, 16)
                    eng.dma_start(
                        out=wbuf2[s][1][:],
                        in_=scr_uT[e_i].rearrange("(c p) m -> p c m", p=128),
                    ).then_inc(qs.sem, 16)
                    eng.dma_start(
                        out=wbuf2[s][2][:],
                        in_=scr_dT[e_i].rearrange("(c p) m -> p c m", p=128),
                    ).then_inc(qs.sem, 16)
                on_qs(ld)
                qs.n += 48
                w_ready2[e_i] = (qs, qs.n)

            emit_load2(0, {})
            emit_load2(1, {})

            def load_next(e_i, te_exp):
                if e_i + 2 < NE:
                    emit_load2(e_i + 2, te_exp)

            st_n = {}

            NB4 = [Y0[:], Y1[:], P6[:], P7[:]]

            def out_tc(tc, resid_ve):
                for w in range(2):
                    def tro(e, tc=tc, resid_ve=resid_ve, w=w):
                        e.wait_ge(ve.sem, resid_ve if w == 0 else ve.n)
                        if w == 0 and tc >= 2:
                            e.wait_ge(qs.sem, st_n[tc - 2])
                        for ci in range(4):
                            c = 4 * w + ci
                            i = e.transpose(
                                NB4[ci][:, 0:128],
                                resid[c][:, tc * 128:(tc + 1) * 128],
                                idn_f[:])
                        te.inc(i)
                    on_te(tro)
                    tn = te.n

                    def evo(e, tn=tn, tc=tc, w=w):
                        e.wait_ge(te.sem, tn)
                        for ci in range(4):
                            c = 4 * w + ci
                            i = e.tensor_copy(
                                stage[tc % 2][:, c * 128:(c + 1) * 128],
                                NB4[ci][:, 0:128])
                        ve.inc(i)
                    on_ve(evo)
                en = ve.n

                def sto(e, tc=tc, en=en):
                    e.wait_ge(ve.sem, en)
                    e.dma_start(out=o_d[tc * 128:(tc + 1) * 128, :],
                                in_=stage[tc % 2][:]).then_inc(qs.sem, 16)
                on_qs(sto)
                qs.n += 16
                st_n[tc] = qs.n
                return qs.n

            emit_ffn_sparse(wbuf2, xgb2, Gs2, Us2, Og2, xst2, otm2, pr2,
                            w_ready2, trload2, after_expert=load_next,
                            after_tc=out_tc)

        def fin(e):
            e.wait_ge(qs.sem, qs.n)
        on_qs(fin)

    return nc


_CACHE = {}


def _tri168():
    t = np.zeros((128, 128), np.float32)
    for p in range(128):
        for q in range(128):
            if (p // 8 < q // 8) and (p % 8 == q % 8):
                t[p, q] = 1.0
    return t.astype(ml_dtypes.bfloat16)


def kernel(L0_attn_x, L0_attn_W_query, L0_attn_W_key, L0_attn_W_value,
           L0_attn_W_out, L0_attn_b_out, L0_ffn_gate_w, L0_ffn_w_gate,
           L0_ffn_w_up, L0_ffn_w_down):
    if "nc" not in _CACHE:
        _CACHE["nc"] = build()
    nc = _CACHE["nc"]

    x = np.ascontiguousarray(np.asarray(L0_attn_x, dtype=np.float32))
    com = dict(
        wq=np.ascontiguousarray(np.asarray(L0_attn_W_query, np.float32)),
        wk=np.ascontiguousarray(np.asarray(L0_attn_W_key, np.float32)),
        wv=np.ascontiguousarray(np.asarray(L0_attn_W_value, np.float32)),
        wo=np.ascontiguousarray(np.asarray(L0_attn_W_out, np.float32)),
        bo=np.ascontiguousarray(np.asarray(L0_attn_b_out, np.float32)),
        gw=np.ascontiguousarray(np.asarray(L0_ffn_gate_w, np.float32)),
        wgate=np.ascontiguousarray(np.asarray(L0_ffn_w_gate, np.float32)),
        wup=np.ascontiguousarray(np.asarray(L0_ffn_w_up, np.float32)),
        wdown=np.ascontiguousarray(np.asarray(L0_ffn_w_down, np.float32)),
        idnb=np.eye(128, dtype=ml_dtypes.bfloat16),
        idnf=np.eye(128, dtype=np.float32),
        trius=np.triu(np.ones((128, 128)), 1).astype(ml_dtypes.bfloat16),
        tri168=_tri168(),
        ecapc=(np.arange(128) % 8 * CAP).astype(np.float32).reshape(128, 1),
    )
    in_maps = [dict(com, x=np.ascontiguousarray(x[b])) for b in range(8)]
    res = run_bass_kernel_spmd(nc, in_maps, core_ids=list(range(8)))
    return np.stack([np.asarray(r["o"], dtype=np.float32)
                     for r in res.results], axis=0)
